# version 02
# baseline (speedup 1.0000x reference)
import math
import numpy as np
import concourse.bass as bass
import concourse.mybir as mybir
from concourse.bass_utils import run_bass_kernel_spmd

F32 = mybir.dt.float32
F16 = mybir.dt.float16
I32 = mybir.dt.int32
U8 = mybir.dt.uint8
OP = mybir.AluOpType
AF = mybir.ActivationFunctionType
AX = mybir.AxisListType

NCORE = 8
B = 2
T = 8192
C = 1024
CH = 8
TOK = 2048
TL = 1024
NT = 4
TS = 512
IFFN = 2816
ICH = 22
CKDIM = 4096
CKCH = 32
EMOE = 8
IEXP = 512
LN_EPS = 1e-5
QMAX = 127.0
RCH = 8
RB = 8

ALL_STAGES = ["s1", "s2", "s3", "s4", "s5", "s6"]
S2_TRUNC = None
SPLIT = [ALL_STAGES]
DEBUG = False


class Sem:
    def __init__(self, handle):
        self.h = handle
        self.val = 0

    def inc(self, instr, n=1):
        instr.then_inc(self.h, n)
        self.val += n
        return instr


class Ctx:
    def __init__(self, nc, block):
        self.nc = nc
        self.block = block
        self.sems = {}
        self.sb = {}
        self.ps = {}
        self.dram = {}
        self._cms = {}
        self.ident = None

    def sem(self, name):
        if name not in self.sems:
            cm = self.nc.semaphore(name)
            self.sems[name] = Sem(cm.__enter__())
            self._cms["sem_" + name] = cm
        return self.sems[name]

    def sbuf(self, name, shape, dtype=F32):
        if name not in self.sb:
            cm = self.nc.sbuf_tensor(name, list(shape), dtype)
            t = cm.__enter__()
            self._cms["sb_" + name] = cm
            self.sb[name] = t.ap() if hasattr(t, "ap") else t
        return self.sb[name]

    def free_sbuf(self, *names):
        for name in names:
            if name in self.sb:
                self._cms.pop("sb_" + name).__exit__(None, None, None)
                del self.sb[name]

    def psum(self, name, shape, dtype=F32):
        if name not in self.ps:
            cm = self.nc.psum_tensor(name, list(shape), dtype)
            t = cm.__enter__()
            self._cms["ps_" + name] = cm
            self.ps[name] = t.ap() if hasattr(t, "ap") else t
        return self.ps[name]

    def dramt(self, name, shape, dtype=F32, kind="Internal"):
        if name not in self.dram:
            self.dram[name] = self.nc.dram_tensor(name, list(shape), dtype, kind=kind).ap()
        return self.dram[name]

    def tap(self, name, src_ap, shape=None, dtype=F32):
        if not DEBUG:
            return
        blk = self.block
        shape = list(shape if shape is not None else src_ap.shape)
        d = self.dramt("tap_" + name, shape, dtype, kind="ExternalOutput")
        s = self.sem("st")
        self.barrier()

        @blk.gpsimd
        def _(gp):
            s.inc(gp.dma_start(out=d[tuple(slice(0, n) for n in shape)], in_=src_ap), 16)
            gp.wait_ge(s.h, s.val)

        self.barrier()

    def barrier(self):
        blk = self.block
        f = self.sem("flow")
        engines = ["vector", "scalar", "tensor", "gpsimd", "sync"]
        base = f.val
        for e in engines:
            @getattr(blk, e)
            def _(eng, e=e):
                if e in ("vector", "scalar", "gpsimd"):
                    eng.drain()
                eng.nop().then_inc(f.h, 1)
        tgt = base + len(engines)
        f.val = tgt
        for e in engines:
            @getattr(blk, e)
            def _(eng):
                eng.wait_ge(f.h, tgt)


def XTiles(ctx):
    return [ctx.sbuf(f"X{c}", (128, TOK)) for c in range(CH)]


def ATiles(ctx):
    return [ctx.sbuf(f"A{c}", (128, TOK)) for c in range(CH)]


def emit_ln_stats(ctx, src):
    nc, blk = ctx.nc, ctx.block
    nsl = TOK // 128
    ones = ctx.sbuf("ones_col", (128, 1))
    sq = ctx.sbuf("SCR2", (128, TOK))
    stat = ctx.sbuf("ln_stat", (128, 64))
    ps_stat = ctx.psum("ps_stat", (128, 64))
    psT = ctx.psum("psT", (128, 128))
    statT = ctx.sbuf("ln_statT", (64, 128))
    rstd_b = ctx.sbuf("SCR0", (128, TOK))
    mrstd_b = ctx.sbuf("SCR1", (128, TOK))
    row_dram = ctx.dramt("ln_rows", (64 * 128,))
    s_dve = ctx.sem("dve")
    s_mm = ctx.sem("mm")
    s_act = ctx.sem("act")
    s_io = ctx.sem("io")

    ctx.barrier()
    dve_base, mm_base = s_dve.val, s_mm.val

    for c in range(CH):
        @blk.vector
        def _(vector, c=c):
            if c > 0:
                vector.wait_ge(s_mm.h, mm_base + c)
            s_dve.inc(vector.tensor_tensor(sq[:, :], src[c][:, :], src[c][:, :], OP.mult), 1)

        @blk.tensor
        def _(tensor, c=c):
            tensor.wait_ge(s_dve.h, dve_base + c + 1)
            for j in range(nsl):
                tensor.matmul(ps_stat[:, j:j + 1], src[c][:, 128 * j:128 * (j + 1)],
                              ones[:, 0:1], start=(c == 0 and j == 0), stop=False)
            for j in range(nsl):
                i = tensor.matmul(ps_stat[:, 32 + j:33 + j], sq[:, 128 * j:128 * (j + 1)],
                                  ones[:, 0:1], start=False,
                                  stop=(c == CH - 1 and j == nsl - 1))
            s_mm.inc(i, 1)

    @blk.vector
    def _(vector):
        vector.wait_ge(s_mm.h, mm_base + CH)
        vector.tensor_scalar(stat[:, 0:nsl], ps_stat[:, 0:nsl], 1.0 / C, None, OP.mult)
        vector.tensor_scalar(stat[:, 32:32 + nsl], ps_stat[:, 32:32 + nsl], 1.0 / C, None, OP.mult)
        vector.tensor_tensor(stat[:, 48:48 + nsl], stat[:, 0:nsl], stat[:, 0:nsl], OP.mult)
        vector.drain()
        vector.tensor_tensor(stat[:, 32:32 + nsl], stat[:, 32:32 + nsl], stat[:, 48:48 + nsl], OP.subtract)
        vector.drain()
        s_dve.inc(vector.tensor_scalar(stat[:, 32:32 + nsl], stat[:, 32:32 + nsl], 1.0,
                                       LN_EPS, OP.mult, OP.add), 1)

    @blk.scalar
    def _(scalar):
        scalar.wait_ge(s_dve.h, s_dve.val)
        scalar.activation(stat[:, 32:32 + nsl], stat[:, 32:32 + nsl], AF.Sqrt)
        scalar.drain()
        s_act.inc(scalar.copy(stat[:, 32:33], stat[:, 32:33]), 1)

    @blk.vector
    def _(vector):
        vector.wait_ge(s_act.h, s_act.val)
        vector.reciprocal(stat[:, 16:16 + nsl], stat[:, 32:32 + nsl])
        vector.drain()
        s_dve.inc(vector.tensor_tensor(stat[:, 0:nsl], stat[:, 0:nsl], stat[:, 16:16 + nsl], OP.mult), 1)

    @blk.tensor
    def _(tensor):
        tensor.wait_ge(s_dve.h, s_dve.val)
        s_mm.inc(tensor.transpose(psT[0:32, :], stat[:, 0:32], ctx.ident[:, :]), 1)

    @blk.vector
    def _(vector):
        vector.wait_ge(s_mm.h, s_mm.val)
        s_dve.inc(vector.tensor_copy(statT[0:32, :], psT[0:32, :]), 1)

    @blk.gpsimd
    def _(gp):
        gp.wait_ge(s_dve.h, s_dve.val)
        s_io.inc(gp.dma_start(out=row_dram[0:32 * 128], in_=statT[0:32, :]), 16)
        gp.wait_ge(s_io.h, s_io.val)
        s_io.inc(gp.dma_start(out=mrstd_b[:, :], in_=row_dram[0:TOK].unsqueeze(0).broadcast_to([128, TOK])), 16)
        s_io.inc(gp.dma_start(out=rstd_b[:, :], in_=row_dram[16 * 128:16 * 128 + TOK].unsqueeze(0).broadcast_to([128, TOK])), 16)

    @blk.vector
    def _(vector):
        vector.wait_ge(s_io.h, s_io.val)
        vector.nop()

    ctx.barrier()


def emit_ln_apply(ctx, vector, writes, src, gamma=None, beta=None):
    rstd_b = ctx.sb["SCR0"]
    mrstd_b = ctx.sb["SCR1"]
    for c in range(CH):
        for d_ap, s_sl in writes[c]:
            vector.tensor_tensor(d_ap, src[c][:, s_sl], rstd_b[:, s_sl], OP.mult)
    for c in range(CH):
        for d_ap, s_sl in writes[c]:
            i = vector.tensor_tensor(d_ap, d_ap, mrstd_b[:, s_sl], OP.subtract)
    if gamma is not None:
        for c in range(CH):
            for d_ap, s_sl in writes[c]:
                i = vector.tensor_scalar(d_ap, d_ap, gamma[:, c:c + 1],
                                         beta[:, c:c + 1], OP.mult, OP.add)
    return i


def emit_ln(ctx, src, dst, gamma=None, beta=None):
    emit_ln_stats(ctx, src)
    blk = ctx.block
    s_dve = ctx.sem("dve")

    @blk.vector
    def _(vector):
        writes = [[(dst[c][:, :], slice(0, TOK))] for c in range(CH)]
        s_dve.inc(emit_ln_apply(ctx, vector, writes, src, gamma, beta), 1)

    ctx.barrier()


def emit_absmax_allreduce(ctx, tiles, cc_tag):
    nc, blk = ctx.nc, ctx.block
    red = ctx.sbuf("red32", (128, 48))
    redT = ctx.sbuf("redT", (1, 128))
    psT = ctx.psum("psT", (128, 128))
    cc_in = ctx.dramt(f"cc_{cc_tag}_in", (8,))
    cc_out = ctx.dramt(f"cc_{cc_tag}_out", (8,))
    s_dve = ctx.sem("dve")
    s_mm = ctx.sem("mm")
    s_io = ctx.sem("io")
    s_cc = ctx.sem("cc")

    ctx.barrier()

    @blk.vector
    def _(vector):
        n = len(tiles)
        for i, t in enumerate(tiles):
            vector.tensor_reduce(red[:, i:i + 1], t, AX.X, OP.max, apply_absolute_value=True)
        step = 1
        while step < n:
            vector.drain()
            for i in range(0, n - step, 2 * step):
                vector.tensor_tensor(red[:, i:i + 1], red[:, i:i + 1],
                                     red[:, i + step:i + step + 1], OP.max)
            step *= 2
        vector.drain()
        s_dve.inc(vector.nop(), 1)

    @blk.tensor
    def _(tensor):
        tensor.wait_ge(s_dve.h, s_dve.val)
        s_mm.inc(tensor.transpose(psT[0:1, :], red[:, 0:1], ctx.ident[:, :]), 1)

    @blk.vector
    def _(vector):
        vector.wait_ge(s_mm.h, s_mm.val)
        vector.tensor_copy(redT[0:1, :], psT[0:1, :])
        vector.drain()
        s_dve.inc(vector.tensor_reduce(redT[0:1, 0:1], redT[0:1, :], AX.X, OP.max), 1)

    @blk.gpsimd
    def _(gp):
        gp.wait_ge(s_dve.h, s_dve.val)
        s_io.inc(gp.dma_start(out=cc_in[0:1], in_=redT[0:1, 0:1]), 16)
        gp.wait_ge(s_io.h, s_io.val)
        s_cc.inc(gp.collective_compute(
            "AllReduce", OP.max, replica_groups=[list(range(NCORE))],
            ins=[cc_in[0:1]], outs=[cc_out[0:1]]), 1)
        gp.wait_ge(s_cc.h, s_cc.val)

    ctx.barrier()


def emit_quant_scales(ctx, cc_tag, clip_tile, prefix):
    nc, blk = ctx.nc, ctx.block
    sc = ctx.sbuf(f"{prefix}_tmp", (1, 4))
    inv_b = ctx.sbuf(f"{prefix}_inv", (128, 1))
    s_b = ctx.sbuf(f"{prefix}_s", (128, 1))
    sc_dram = ctx.dramt(f"{prefix}_dram", (4,))
    s_io = ctx.sem("io")
    s_dve = ctx.sem("dve")
    s_act = ctx.sem("act")

    @blk.gpsimd
    def _(gp):
        s_io.inc(gp.dma_start(out=sc[0:1, 0:1], in_=ctx.dram[f"cc_{cc_tag}_out"][0:1]), 16)

    @blk.scalar
    def _(scalar):
        scalar.wait_ge(s_io.h, s_io.val)
        scalar.activation(sc[0:1, 1:2], clip_tile, AF.Sigmoid)
        scalar.drain()
        s_act.inc(scalar.copy(sc[0:1, 1:2], sc[0:1, 1:2]), 1)

    @blk.vector
    def _(vector):
        vector.wait_ge(s_act.h, s_act.val)
        vector.tensor_tensor(sc[0:1, 0:1], sc[0:1, 0:1], sc[0:1, 1:2], OP.mult)
        vector.drain()
        vector.tensor_scalar(sc[0:1, 0:1], sc[0:1, 0:1], 1.0 / QMAX, None, OP.mult)
        vector.drain()
        s_dve.inc(vector.reciprocal(sc[0:1, 1:2], sc[0:1, 0:1]), 1)

    @blk.gpsimd
    def _(gp):
        gp.wait_ge(s_dve.h, s_dve.val)
        s_io.inc(gp.dma_start(out=sc_dram[0:2], in_=sc[0:1, 0:2]), 16)
        gp.wait_ge(s_io.h, s_io.val)
        s_io.inc(gp.dma_start(out=s_b[:, 0:1], in_=sc_dram[0:1].unsqueeze(0).broadcast_to([128, 1])), 16)
        s_io.inc(gp.dma_start(out=inv_b[:, 0:1], in_=sc_dram[1:2].unsqueeze(0).broadcast_to([128, 1])), 16)
        gp.wait_ge(s_io.h, s_io.val)

    ctx.barrier()


def round_ops(vector, dst, src, inv_ap, qi, qf, qg, drains=True):
    w = src.shape[-1]
    D = (lambda: vector.drain()) if drains else (lambda: None)
    vector.tensor_scalar(qf[:, 0:w], src, inv_ap, None, OP.mult)
    D()
    vector.tensor_copy(qi[:, 0:w], qf[:, 0:w])
    D()
    vector.tensor_tensor(qg[:, 0:w], qf[:, 0:w], qi[:, 0:w], OP.subtract)
    D()
    vector.tensor_scalar(qf[:, 0:w], qg[:, 0:w], 0.5, None, OP.is_gt)
    vector.tensor_scalar(qg[:, 0:w], qg[:, 0:w], -0.5, None, OP.is_lt)
    D()
    vector.tensor_tensor(qf[:, 0:w], qf[:, 0:w], qi[:, 0:w], OP.add)
    D()
    vector.tensor_tensor(qf[:, 0:w], qf[:, 0:w], qg[:, 0:w], OP.subtract)
    D()
    return vector.tensor_scalar(dst, qf[:, 0:w], -128.0, 127.0, OP.max, OP.min)


def quant_ops(vector, dst, src, inv_ap, qi, qf, qg):
    return round_ops(vector, dst, src, inv_ap, qi, qf, qg, drains=True)


def quant2_ops(vector, dst_a, src_a, dst_b, src_b, inv_ap, qia, qfa, qga, qib, qfb, qgb):
    wa = src_a.shape[-1]
    wb = src_b.shape[-1]
    vector.tensor_scalar(qfa[:, 0:wa], src_a, inv_ap, None, OP.mult)
    vector.tensor_scalar(qfb[:, 0:wb], src_b, inv_ap, None, OP.mult)
    vector.tensor_copy(qia[:, 0:wa], qfa[:, 0:wa])
    vector.tensor_copy(qib[:, 0:wb], qfb[:, 0:wb])
    vector.tensor_tensor(qga[:, 0:wa], qfa[:, 0:wa], qia[:, 0:wa], OP.subtract)
    vector.tensor_tensor(qgb[:, 0:wb], qfb[:, 0:wb], qib[:, 0:wb], OP.subtract)
    vector.tensor_scalar(qfa[:, 0:wa], qga[:, 0:wa], 0.5, None, OP.is_gt)
    vector.tensor_scalar(qfb[:, 0:wb], qgb[:, 0:wb], 0.5, None, OP.is_gt)
    vector.tensor_scalar(qga[:, 0:wa], qga[:, 0:wa], -0.5, None, OP.is_lt)
    vector.tensor_scalar(qgb[:, 0:wb], qgb[:, 0:wb], -0.5, None, OP.is_lt)
    vector.tensor_tensor(qfa[:, 0:wa], qfa[:, 0:wa], qia[:, 0:wa], OP.add)
    vector.tensor_tensor(qfb[:, 0:wb], qfb[:, 0:wb], qib[:, 0:wb], OP.add)
    vector.tensor_tensor(qfa[:, 0:wa], qfa[:, 0:wa], qga[:, 0:wa], OP.subtract)
    vector.tensor_tensor(qfb[:, 0:wb], qfb[:, 0:wb], qgb[:, 0:wb], OP.subtract)
    vector.tensor_scalar(dst_a, qfa[:, 0:wa], -128.0, 127.0, OP.max, OP.min)
    return vector.tensor_scalar(dst_b, qfb[:, 0:wb], -128.0, 127.0, OP.max, OP.min)


def emit_ffn0_weight_quant(ctx):
    nc, blk = ctx.nc, ctx.block
    s_io = ctx.sem("io")
    s_dve = ctx.sem("dve")
    s_st = ctx.sem("st")
    AB = ctx.sbuf("ABUF", (128, 16384))
    wa = AB[:, 0:2816]
    wb = AB[:, 2816:5632]
    inv_t = AB[:, 5632:8448]
    qi = AB.bitcast(I32)[:, 8448:11264]
    wg_ = AB[:, 11264:14080]
    wb16 = AB.bitcast(F16)[:, 28160:30976]

    ctx.barrier()
    specs = (
        (IFFN, C, "wg_a", "wg_b", "wq_gate", "wsc_g"),
        (IFFN, C, "wu_a", "wu_b", "wq_up", "wsc_u"),
        (C, IFFN, "wd_a", "wd_b", "wq_down", "wsc_d"),
    )
    for rows, cols, a_in, b_in, q_out, sct in specs:
        a = ctx.dram[a_in]
        b = ctx.dram[b_in]
        q = ctx.dramt(q_out, (cols, rows), F16)
        scd = ctx.dramt(f"{q_out}_sc", (rows,))
        sc_tile = ctx.sbuf(sct, (128, rows // 128))
        np_r = rows // 128

        for g in range(np_r):
            @blk.sync
            def _(sync, g=g, a=a, cols=cols):
                sync.wait_ge(s_dve.h, s_dve.val)
                s_io.inc(sync.dma_start(out=wa[:, 0:cols], in_=a[128 * g:128 * (g + 1), 0:cols]), 16)

            @blk.vector
            def _(vector, g=g, cols=cols, sc_tile=sc_tile):
                vector.wait_ge(s_io.h, s_io.val)
                vector.tensor_reduce(sc_tile[:, g:g + 1], wa[:, 0:cols], AX.X, OP.max,
                                     apply_absolute_value=True)
                vector.drain()
                s_dve.inc(vector.tensor_scalar(sc_tile[:, g:g + 1], sc_tile[:, g:g + 1],
                                               1.0 / QMAX, None, OP.mult), 1)

        @blk.gpsimd
        def _(gp, scd=scd, np_r=np_r, sc_tile=sc_tile):
            gp.wait_ge(s_dve.h, s_dve.val)
            for g in range(np_r):
                s_st.inc(gp.dma_start(out=scd[128 * g:128 * (g + 1)], in_=sc_tile[:, g:g + 1]), 16)
            gp.wait_ge(s_st.h, s_st.val)

        ctx.barrier()

        @blk.gpsimd
        def _(gp, scd=scd, rows=rows):
            s_io.inc(gp.dma_start(out=inv_t[:, 0:rows],
                                  in_=scd[0:rows].unsqueeze(0).broadcast_to([128, rows])), 16)

        @blk.vector
        def _(vector, rows=rows):
            vector.wait_ge(s_io.h, s_io.val)
            s_dve.inc(vector.reciprocal(inv_t[:, 0:rows], inv_t[:, 0:rows]), 1)
            vector.drain()

        for ct in range(cols // 128):
            @blk.sync
            def _(sync, ct=ct, b=b, rows=rows):
                sync.wait_ge(s_dve.h, s_dve.val)
                s_io.inc(sync.dma_start(out=wb[:, 0:rows], in_=b[128 * ct:128 * (ct + 1), 0:rows]), 16)

            @blk.vector
            def _(vector, rows=rows):
                vector.wait_ge(s_io.h, s_io.val)
                vector.wait_ge(s_st.h, s_st.val)
                vector.tensor_tensor(wb[:, 0:rows], wb[:, 0:rows], inv_t[:, 0:rows], OP.mult)
                vector.drain()
                vector.tensor_copy(qi[:, 0:rows], wb[:, 0:rows])
                vector.drain()
                vector.tensor_tensor(wg_[:, 0:rows], wb[:, 0:rows], qi[:, 0:rows], OP.subtract)
                vector.drain()
                vector.tensor_scalar(wb[:, 0:rows], wg_[:, 0:rows], 0.5, None, OP.is_gt)
                vector.tensor_scalar(wg_[:, 0:rows], wg_[:, 0:rows], -0.5, None, OP.is_lt)
                vector.drain()
                vector.tensor_tensor(wb[:, 0:rows], wb[:, 0:rows], qi[:, 0:rows], OP.add)
                vector.drain()
                vector.tensor_tensor(wb[:, 0:rows], wb[:, 0:rows], wg_[:, 0:rows], OP.subtract)
                vector.drain()
                s_dve.inc(vector.tensor_scalar(wb16[:, 0:rows], wb[:, 0:rows], -128.0, 127.0,
                                               OP.max, OP.min), 1)

            @blk.gpsimd
            def _(gp, ct=ct, q=q, rows=rows):
                gp.wait_ge(s_dve.h, s_dve.val)
                s_st.inc(gp.dma_start(out=q[128 * ct:128 * (ct + 1), 0:rows], in_=wb16[:, 0:rows]), 16)

    @blk.gpsimd
    def _(gp):
        gp.wait_ge(s_st.h, s_st.val)

    ctx.barrier()


def emit_ffn0(ctx, stage):
    nc, blk = ctx.nc, ctx.block
    X = XTiles(ctx)
    AB = ctx.sbuf("ABUF", (128, 16384))
    Y = [AB[:, 2048 * c:2048 * (c + 1)] for c in range(CH)]
    emit_ln(ctx, X, Y)
    ctx.tap(f"{stage}_ln0", Y[0])
    ctx.tap(f"{stage}_rstdb", ctx.sb["SCR0"][:, :])
    ctx.tap(f"{stage}_mrstdb", ctx.sb["SCR1"][:, :])

    emit_absmax_allreduce(ctx, [y for y in Y], f"{stage}x")
    clip_x = ctx.sbuf("clip_x_t", (1, 2))
    emit_quant_scales(ctx, f"{stage}x", clip_x[0:1, 0:1], f"{stage}qx")

    s_dve = ctx.sem("dve")
    s_io = ctx.sem("io")
    s_mm = ctx.sem("mm")
    s_act = ctx.sem("act")
    s_st = ctx.sem("st")
    SCR0a = ctx.sbuf("SCR0", (128, TOK))
    SCR1a = ctx.sbuf("SCR1", (128, TOK))
    qi = SCR0a.bitcast(I32)[:, 0:TS]
    qf = SCR0a[:, TS:2 * TS]
    qg = SCR0a[:, 2 * TS:3 * TS]
    qib = SCR0a.bitcast(I32)[:, 3 * TS:4 * TS]
    qfb = SCR1a[:, 0:TS]
    qgb = SCR1a[:, TS:2 * TS]
    inv_x = ctx.sb[f"{stage}qx_inv"]
    sx = ctx.sb[f"{stage}qx_s"]
    XQ = [ctx.sbuf(f"XQ{c}", (128, TOK), F16) for c in range(CH)]

    @blk.vector
    def _(vector):
        subs = [(c, nt) for c in range(CH) for nt in range(NT)]
        for j in range(0, len(subs), 2):
            (ca, na), (cb, nb) = subs[j], subs[j + 1]
            sa = slice(TS * na, TS * (na + 1))
            sb = slice(TS * nb, TS * (nb + 1))
            i = quant2_ops(vector, XQ[ca][:, sa], Y[ca][:, sa],
                           XQ[cb][:, sb], Y[cb][:, sb], inv_x[:, 0:1],
                           qi, qf, qg, qib, qfb, qgb)
        s_dve.inc(i, 1)

    wsxg = ctx.sbuf("wsxg", (128, ICH))
    wsxu = ctx.sbuf("wsxu", (128, ICH))
    wsxd = ctx.sbuf("wsxd", (128, CH))

    @blk.vector
    def _(vector):
        vector.tensor_scalar(wsxg[:, :], ctx.sb["wsc_g"][:, :], sx[:, 0:1], None, OP.mult)
        s_dve.inc(vector.tensor_scalar(wsxu[:, :], ctx.sb["wsc_u"][:, :], sx[:, 0:1], None, OP.mult), 1)

    ctx.barrier()
    ctx.tap(f"{stage}_sc", ctx.sb[f"{stage}qx_tmp"][0:1, 0:2], (1, 2))
    ctx.tap(f"{stage}_xq0", XQ[0][:, :], (128, TOK), F16)
    ctx.tap(f"{stage}_wsxg", wsxg[:, :])

    wq_g = ctx.dram["wq_gate"]
    wq_u = ctx.dram["wq_up"]
    h_spill = ctx.dramt("h_spill", (IFFN, TOK))
    wt3s = ctx.sbuf("wt3", (128, CH * 128))
    wtg = wt3s.bitcast(F16)[:, 0:CH * 128]
    wtu = wt3s.bitcast(F16)[:, CH * 128:2 * CH * 128]
    go = ctx.sbuf("SCR0", (128, TOK))
    uo = ctx.sbuf("SCR1", (128, TOK))
    ho = ctx.sbuf("SCR2", (128, TOK))
    hmax = ctx.sbuf("ffn_hmax", (128, 68))
    psN = [ctx.psum(f"ps{i}", (128, TS)) for i in range(4)]

    for o in range(ICH):
        @blk.sync
        def _(sync, o=o):
            if o > 0:
                sync.wait_ge(s_mm.h, s_mm.val)
            for k in range(CH):
                s_io.inc(sync.dma_start(out=wtg[:, 128 * k:128 * (k + 1)],
                                        in_=wq_g[128 * k:128 * (k + 1), 128 * o:128 * (o + 1)]), 16)
                s_io.inc(sync.dma_start(out=wtu[:, 128 * k:128 * (k + 1)],
                                        in_=wq_u[128 * k:128 * (k + 1), 128 * o:128 * (o + 1)]), 16)

        @blk.tensor
        def _(tensor, o=o):
            tensor.wait_ge(s_io.h, s_io.val)
            tensor.wait_ge(s_dve.h, s_dve.val)
            for nt in range(NT):
                for k in range(CH):
                    i = tensor.matmul(psN[nt][:, :], wtg[:, 128 * k:128 * (k + 1)],
                                      XQ[k][:, TS * nt:TS * (nt + 1)],
                                      start=(k == 0), stop=(k == CH - 1))
            s_mm.inc(i, 1)

        @blk.vector
        def _(vector, o=o):
            vector.wait_ge(s_mm.h, s_mm.val)
            vector.wait_ge(s_st.h, s_st.val)
            for nt in range(NT):
                i = vector.tensor_scalar(go[:, TS * nt:TS * (nt + 1)], psN[nt][:, :],
                                         wsxg[:, o:o + 1], None, OP.mult)
            s_dve.inc(i, 1)

        @blk.tensor
        def _(tensor, o=o):
            tensor.wait_ge(s_dve.h, s_dve.val)
            for nt in range(NT):
                for k in range(CH):
                    i = tensor.matmul(psN[nt][:, :], wtu[:, 128 * k:128 * (k + 1)],
                                      XQ[k][:, TS * nt:TS * (nt + 1)],
                                      start=(k == 0), stop=(k == CH - 1))
            s_mm.inc(i, 1)

        @blk.scalar
        def _(scalar, o=o):
            scalar.wait_ge(s_dve.h, s_dve.val)
            scalar.activation(go[:, :], go[:, :], AF.Silu)
            scalar.drain()
            s_act.inc(scalar.copy(go[:, 0:1], go[:, 0:1]), 1)

        @blk.vector
        def _(vector, o=o):
            vector.wait_ge(s_mm.h, s_mm.val)
            for nt in range(NT):
                vector.tensor_scalar(uo[:, TS * nt:TS * (nt + 1)], psN[nt][:, :],
                                     wsxu[:, o:o + 1], None, OP.mult)
            vector.wait_ge(s_act.h, s_act.val)
            vector.wait_ge(s_st.h, s_st.val)
            for nt in range(NT):
                sl = slice(TS * nt, TS * (nt + 1))
                vector.tensor_tensor(ho[:, sl], go[:, sl], uo[:, sl], OP.mult)
            vector.tensor_reduce(hmax[:, 2 * o:2 * o + 1], ho[:, 0:TOK // 2], AX.X,
                                 OP.max, apply_absolute_value=True)
            i = vector.tensor_reduce(hmax[:, 2 * o + 1:2 * o + 2], ho[:, TOK // 2:TOK], AX.X,
                                     OP.max, apply_absolute_value=True)
            s_dve.inc(i, 1)

        @blk.gpsimd
        def _(gp, o=o):
            gp.wait_ge(s_dve.h, s_dve.val)
            s_st.inc(gp.dma_start(out=h_spill[128 * o:128 * (o + 1), :], in_=ho[:, :]), 16)

    ctx.barrier()
    ctx.tap(f"{stage}_h21", ho[:, :])
    ctx.tap(f"{stage}_hmax", hmax[:, 0:2 * ICH], (128, 2 * ICH))

    emit_absmax_allreduce(ctx, [hmax[:, i:i + 1] for i in range(2 * ICH)], f"{stage}h")
    clip_d = ctx.sbuf("clip_d_t", (1, 2))
    emit_quant_scales(ctx, f"{stage}h", clip_d[0:1, 0:1], f"{stage}qh")
    inv_h = ctx.sb[f"{stage}qh_inv"]
    sh = ctx.sb[f"{stage}qh_s"]

    @blk.vector
    def _(vector):
        vector.tensor_scalar(wsxd[:, :], ctx.sb["wsc_d"][:, :], sh[:, 0:1], None, OP.mult)
        vector.drain()
        s_dve.inc(vector.tensor_scalar(wsxd[:, :], wsxd[:, :], 0.5, None, OP.mult), 1)

    ctx.barrier()

    wq_d = ctx.dram["wq_down"]
    HQ = AB.bitcast(F16)[:, 0:ICH * TS]
    wtd = AB.bitcast(F16)[:, 11264:11264 + ICH * 128]
    hbuf = AB[:, 7040:7552]
    dq = AB[:, 7552:8064]

    for nt in range(NT):
        sl = slice(TS * nt, TS * (nt + 1))

        for kk in range(ICH):
            @blk.sync
            def _(sync, kk=kk, nt=nt, sl=sl):
                sync.wait_ge(s_dve.h, s_dve.val)
                s_io.inc(sync.dma_start(out=hbuf, in_=h_spill[128 * kk:128 * (kk + 1), sl]), 16)

            @blk.vector
            def _(vector, kk=kk):
                vector.wait_ge(s_io.h, s_io.val)
                vector.wait_ge(s_mm.h, s_mm.val)
                s_dve.inc(quant_ops(vector, HQ[:, TS * kk:TS * (kk + 1)], hbuf,
                                    inv_h[:, 0:1], qi, qf, qg), 1)
                vector.drain()

        for oo in range(CH):
            @blk.sync
            def _(sync, oo=oo):
                sync.wait_ge(s_mm.h, s_mm.val)
                for kk in range(ICH):
                    s_io.inc(sync.dma_start(out=wtd[:, 128 * kk:128 * (kk + 1)],
                                            in_=wq_d[128 * kk:128 * (kk + 1), 128 * oo:128 * (oo + 1)]), 16)

            @blk.tensor
            def _(tensor, oo=oo):
                tensor.wait_ge(s_io.h, s_io.val)
                tensor.wait_ge(s_dve.h, s_dve.val)
                for kk in range(ICH):
                    i = tensor.matmul(psN[0][:, :], wtd[:, 128 * kk:128 * (kk + 1)],
                                      HQ[:, TS * kk:TS * (kk + 1)],
                                      start=(kk == 0), stop=(kk == ICH - 1))
                s_mm.inc(i, 1)

            @blk.vector
            def _(vector, oo=oo, sl=sl):
                vector.wait_ge(s_mm.h, s_mm.val)
                vector.tensor_scalar(dq, psN[0][:, :], wsxd[:, oo:oo + 1], None, OP.mult)
                vector.drain()
                s_dve.inc(vector.tensor_tensor(X[oo][:, sl], X[oo][:, sl], dq, OP.add), 1)

    ctx.barrier()




def _switch_pick(ctx, gp, dst_ap, blocks, ts_dram_ap, sem, n_dma=1):
    base = sem.val
    for arm in gp.switch_core_id(n=128):
        r = arm.logical % NCORE
        if r == 0:
            sem.inc(gp.dma_start(out=dst_ap, in_=ts_dram_ap), 16)
        else:
            sem.inc(gp.dma_start(out=dst_ap, in_=blocks(r - 1)), 16)
    sem.val = base + 16


def emit_hs_tile(ctx, vector, hs_tiles, srcY, b, tti, bnd_tile, bnd_cols, gamma, beta):
    base = b * TL + tti * TS
    if tti == 0:
        writes = [[(hs_tiles[c][:, 1:TS], slice(base, base + TS - 1))] for c in range(CH)]
    else:
        writes = [[(hs_tiles[c][:, 0:TS], slice(base - 1, base + TS - 1))] for c in range(CH)]
    i = emit_ln_apply(ctx, vector, writes, srcY, gamma, beta)
    if tti == 0:
        for c in range(CH):
            i = vector.tensor_copy(hs_tiles[c][:, 0:1], bnd_tile[:, bnd_cols(c):bnd_cols(c) + 1])
    return i


def emit_hs_tile_sp(ctx, vector, hs_tiles, stripes, b, tti, bnd_tile, bnd_cols, gamma, beta):
    rstd_b = ctx.sb["SCR0"]
    mrstd_b = ctx.sb["SCR1"]
    base = b * TL + tti * TS
    a = 1 if tti == 0 else 0
    w = TS - a
    s0 = base - 1 + a
    for c in range(CH):
        vector.tensor_tensor(hs_tiles[c][:, a:TS], stripes[c][:, a:TS],
                             rstd_b[:, s0:s0 + w], OP.mult)
    for c in range(CH):
        vector.tensor_tensor(hs_tiles[c][:, a:TS], hs_tiles[c][:, a:TS],
                             mrstd_b[:, s0:s0 + w], OP.subtract)
    for c in range(CH):
        i = vector.tensor_scalar(hs_tiles[c][:, a:TS], hs_tiles[c][:, a:TS],
                                 gamma[:, c:c + 1], beta[:, c:c + 1], OP.mult, OP.add)
    if tti == 0:
        for c in range(CH):
            i = vector.tensor_copy(hs_tiles[c][:, 0:1], bnd_tile[:, bnd_cols(c):bnd_cols(c) + 1])
    return i


def emit_s2(ctx, stage):
    nc, blk = ctx.nc, ctx.block
    X = XTiles(ctx)
    AB = ctx.sbuf("ABUF", (128, 16384))
    Y = [AB[:, 2048 * c:2048 * (c + 1)] for c in range(CH)]
    SCR0 = ctx.sbuf("SCR0", (128, TOK))
    SCR1 = ctx.sbuf("SCR1", (128, TOK))
    SCR2 = ctx.sbuf("SCR2", (128, TOK))
    XQ = [ctx.sbuf(f"XQ{c}", (128, TOK), F16) for c in range(CH)]
    XQf = [XQ[c].bitcast(F32) for c in range(CH)]
    s_dve = ctx.sem("dve")
    s_mm = ctx.sem("mm")
    s_act = ctx.sem("act")
    s_io = ctx.sem("io")
    s_st = ctx.sem("st")
    s_cc = ctx.sem("cc")

    ln1g = ctx.sbuf("ln1g", (128, CH))
    ln1b = ctx.sbuf("ln1b", (128, CH))
    ln2g = ctx.sbuf("ln2g", (128, CH))
    ln2b = ctx.sbuf("ln2b", (128, CH))
    dmy2 = ctx.sbuf("dmy2", (128, 2))
    wkv_w = ctx.sbuf("wkv_w_t", (128, 1))
    wkv_u = ctx.sbuf("wkv_u_t", (128, 1))
    eu_t = ctx.sbuf("eu_t", (128, 1))
    tsv = ctx.sbuf("tsv", (128, 16))

    cc_k_in = ctx.dramt("cc_k_in", (C, TOK))
    cc_vloc = ctx.dramt("cc_vloc", (C, TOK))
    cc_k_out = ctx.dramt("cc_k_out", (NCORE, C, TOK))
    cc_vgat = ctx.dramt("cc_vgat", (NCORE, C, TOK))
    cc_y_in = ctx.dramt("cc_y_in", (128, T * B // 1))
    cc_y_out = ctx.dramt("cc_y_out", (NCORE, 128, T * B))
    cc_hl_in = ctx.dramt("cc_hl_in", (TOK,))
    cc_hl_out = ctx.dramt("cc_hl_out", (NCORE, TOK))
    cc_h2l_in = ctx.dramt("cc_h2l_in", (TOK,))
    cc_h2l_out = ctx.dramt("cc_h2l_out", (NCORE, TOK))
    sigr_sp = ctx.dramt("sigr_sp", (C, TOK))
    x_sp = ctx.dramt("x_sp", (C, TOK))

    @blk.sync
    def _(sync):
        io = ctx.dram
        for nm, t in (("ln1_g", ln1g), ("ln1_b", ln1b), ("ln2_g", ln2g), ("ln2_b", ln2b)):
            for c in range(CH):
                s_io.inc(sync.dma_start(out=t[:, c:c + 1], in_=io[nm][128 * c:128 * (c + 1)]), 16)
        s_io.inc(sync.dma_start(out=wkv_w[:, 0:1], in_=io["wkv_w"][0:128]), 16)
        s_io.inc(sync.dma_start(out=wkv_u[:, 0:1], in_=io["wkv_u"][0:128]), 16)
        s_io.inc(sync.dma_start(out=tsv[:, :], in_=io["ts_pack"][0:TOK]), 16)

    @blk.scalar
    def _(scalar):
        scalar.wait_ge(s_io.h, s_io.val)
        scalar.activation(eu_t[:, 0:1], wkv_u[:, 0:1], AF.Exp)
        scalar.drain()
        s_act.inc(scalar.copy(eu_t[:, 0:1], eu_t[:, 0:1]), 1)

    ctx.barrier()
    if S2_TRUNC == 11:
        return

    emit_ln(ctx, X, Y)
    if S2_TRUNC == 12:
        return

    @blk.gpsimd
    def _(gp):
        for c in range(CH):
            s_st.inc(gp.dma_start(out=x_sp[128 * c:128 * (c + 1), :], in_=X[c][:, :]), 16)
        gp.wait_ge(s_st.h, s_st.val)

    emit_ln_stats(ctx, Y)
    if S2_TRUNC == 13:
        return
    hl_pack = ctx.sbuf("hl_pack", (128, 16))

    @blk.vector
    def _(vector):
        writes = [[(hl_pack[:, 2 * c:2 * c + 1], slice(TL - 1, TL)),
                   (hl_pack[:, 2 * c + 1:2 * c + 2], slice(TOK - 1, TOK))] for c in range(CH)]
        s_dve.inc(emit_ln_apply(ctx, vector, writes, Y, ln1g, ln1b), 1)

    if S2_TRUNC == 14:
        ctx.barrier()
        return

    hl_dbg = ctx.dramt("hl_dbg", (128, 16))

    @blk.gpsimd
    def _(gp):
        gp.wait_ge(s_dve.h, s_dve.val)
        if S2_TRUNC == 151:
            pass
        elif S2_TRUNC == 152:
            s_st.inc(gp.dma_start(out=hl_dbg[:, :], in_=hl_pack[:, :]), 16)
            gp.wait_ge(s_st.h, s_st.val)
        else:
            s_st.inc(gp.dma_start(out=cc_hl_in[:], in_=hl_pack[:, :]), 16)
            gp.wait_ge(s_st.h, s_st.val)
        if S2_TRUNC not in (15, 151, 152):
            s_cc.inc(gp.collective_compute(
                "AllGather", OP.bypass, replica_groups=[list(range(NCORE))],
                ins=[cc_hl_in[:]], outs=[cc_hl_out[:, :]]), 1)
            gp.wait_ge(s_cc.h, s_cc.val)
        if S2_TRUNC not in (15, 151, 152, 16):
            _switch_pick(ctx, gp, hl_pack[:, :], lambda r: cc_hl_out[r, :],
                         ctx.dram["ts_pack"][0:TOK], s_io)
            gp.wait_ge(s_io.h, s_io.val)

    ctx.barrier()
    ctx.tap("s2_hl", hl_pack[:, :])
    if S2_TRUNC in (1, 151, 152, 15, 16):
        return

    hsT = [XQf[c][:, 0:TS] for c in range(CH)]
    wt3 = ctx.sbuf("wt3", (128, CH * 128))
    evk = SCR2[:, 0:TS]
    evr = SCR2[:, TS:2 * TS]
    psN = [ctx.psum(f"ps{i}", (128, TS)) for i in range(4)]

    for b in range(B):
        for tti in range(2):
            @blk.vector
            def _(vector, b=b, tti=tti):
                vector.wait_ge(s_mm.h, s_mm.val)
                i = emit_hs_tile(ctx, vector, hsT, Y, b, tti, hl_pack,
                                 (lambda c, b=b: 2 * c + b), ln1g, ln1b)
                s_dve.inc(i, 1)

            col0 = b * TL + tti * TS
            for tau, (wnm, dst) in enumerate((("Wk", cc_k_in), ("Wv", cc_vloc), ("Wr", None))):
                for oo in range(CH):
                    @blk.sync
                    def _(sync, wnm=wnm, oo=oo):
                        sync.wait_ge(s_mm.h, s_mm.val)
                        for kk in range(CH):
                            s_io.inc(sync.dma_start(
                                out=wt3[:, 128 * kk:128 * (kk + 1)],
                                in_=ctx.dram[wnm][128 * kk:128 * (kk + 1), 128 * oo:128 * (oo + 1)]), 16)

                    @blk.tensor
                    def _(tensor, oo=oo, tau=tau):
                        tensor.wait_ge(s_io.h, s_io.val)
                        tensor.wait_ge(s_dve.h, s_dve.val)
                        if tau == 0 and oo == 0:
                            pass
                        tensor.wait_ge(s_act.h, s_act.val)
                        for kk in range(CH):
                            i = tensor.matmul(psN[0][:, :], wt3[:, 128 * kk:128 * (kk + 1)],
                                              hsT[kk][:, :], start=(kk == 0), stop=(kk == CH - 1))
                        s_mm.inc(i, 1)

                    if wnm == "Wr":
                        @blk.scalar
                        def _(scalar, oo=oo):
                            scalar.wait_ge(s_mm.h, s_mm.val)
                            scalar.wait_ge(s_st.h, s_st.val)
                            scalar.activation(evr, psN[0][:, :], AF.Sigmoid)
                            scalar.drain()
                            s_act.inc(scalar.copy(evr[:, 0:1], evr[:, 0:1]), 1)

                        @blk.gpsimd
                        def _(gp, oo=oo, col0=col0):
                            gp.wait_ge(s_act.h, s_act.val)
                            s_st.inc(gp.dma_start(
                                out=sigr_sp[128 * oo:128 * (oo + 1), col0:col0 + TS], in_=evr), 16)
                    else:
                        @blk.scalar
                        def _(scalar, oo=oo):
                            scalar.wait_ge(s_mm.h, s_mm.val)
                            scalar.wait_ge(s_st.h, s_st.val)
                            scalar.copy(evk, psN[0][:, :])
                            scalar.drain()
                            s_act.inc(scalar.copy(evk[:, 0:1], evk[:, 0:1]), 1)

                        @blk.gpsimd
                        def _(gp, oo=oo, col0=col0, dst=dst):
                            gp.wait_ge(s_act.h, s_act.val)
                            s_st.inc(gp.dma_start(
                                out=dst[128 * oo:128 * (oo + 1), col0:col0 + TS], in_=evk), 16)

    @blk.gpsimd
    def _(gp):
        gp.wait_ge(s_st.h, s_st.val)
        s_cc.inc(gp.collective_compute(
            "AllGather", OP.bypass, replica_groups=[list(range(NCORE))],
            ins=[cc_k_in[:, :]], outs=[cc_k_out[:, :, :]]), 1)
        s_cc.inc(gp.collective_compute(
            "AllGather", OP.bypass, replica_groups=[list(range(NCORE))],
            ins=[cc_vloc[:, :]], outs=[cc_vgat[:, :, :]]), 1)
        gp.wait_ge(s_cc.h, s_cc.val)

    ctx.barrier()
    ctx.tap("s2_k", ctx.dram["cc_k_out"][0, :, :])
    ctx.tap("s2_v", ctx.dram["cc_vgat"][0, :, :])
    ctx.tap("s2_sigr", sigr_sp[:, :])
    if S2_TRUNC == 2:
        return

    emit_wkv(ctx)
    ctx.tap("s2_y", ctx.dram["cc_y_out"][0, :, :])
    if S2_TRUNC == 3:
        return

    ywkv_t = SCR2[:, 0:TOK]

    for c in range(CH):
        @blk.gpsimd
        def _(gp, c=c):
            gp.wait_ge(s_dve.h, s_dve.val)
            io_base = s_io.val
            for arm in gp.switch_core_id(n=128):
                r = arm.logical % NCORE
                s_io.inc(gp.dma_start(out=ywkv_t,
                                      in_=ctx.dram["cc_y_out"][c, :, TOK * r:TOK * (r + 1)]), 16)
            s_io.val = io_base + 16
            s_io.inc(gp.dma_start(out=SCR0[:, :],
                                  in_=sigr_sp[128 * c:128 * (c + 1), :]), 16)

        @blk.vector
        def _(vector, c=c):
            vector.wait_ge(s_io.h, s_io.val)
            vector.tensor_tensor(SCR1[:, :], SCR0[:, :], ywkv_t, OP.mult)
            vector.drain()
            s_dve.inc(vector.tensor_tensor(AB[:, 2048 * c:2048 * (c + 1)],
                                           AB[:, 2048 * c:2048 * (c + 1)], SCR1[:, :], OP.add), 1)

    ctx.barrier()
    ctx.tap("s2_x1", AB[:, 0:2048])
    if S2_TRUNC == 4:
        return

    emit_ln_stats(ctx, Y)
    h2l_pack = ctx.sbuf("h2l_pack", (128, 16))

    @blk.vector
    def _(vector):
        writes = [[(h2l_pack[:, 2 * c:2 * c + 1], slice(TL - 1, TL)),
                   (h2l_pack[:, 2 * c + 1:2 * c + 2], slice(TOK - 1, TOK))] for c in range(CH)]
        s_dve.inc(emit_ln_apply(ctx, vector, writes, Y, ln2g, ln2b), 1)

    @blk.gpsimd
    def _(gp):
        gp.wait_ge(s_dve.h, s_dve.val)
        s_st.inc(gp.dma_start(out=cc_h2l_in[:], in_=h2l_pack[:, :]), 16)
        gp.wait_ge(s_st.h, s_st.val)
        s_cc.inc(gp.collective_compute(
            "AllGather", OP.bypass, replica_groups=[list(range(NCORE))],
            ins=[cc_h2l_in[:]], outs=[cc_h2l_out[:, :]]), 1)
        gp.wait_ge(s_cc.h, s_cc.val)
        _switch_pick(ctx, gp, h2l_pack[:, :], lambda r: cc_h2l_out[r, :],
                     ctx.dram["ts_pack"][0:TOK], s_io)
        for c in range(CH):
            s_io.inc(gp.dma_start(out=X[c][:, :], in_=x_sp[128 * c:128 * (c + 1), :]), 16)
        gp.wait_ge(s_io.h, s_io.val)

    @blk.vector
    def _(vector):
        vector.wait_ge(s_io.h, s_io.val)
        for c in range(CH):
            i = vector.tensor_tensor(X[c][:, :], X[c][:, :], AB[:, 2048 * c:2048 * (c + 1)], OP.add)
        s_dve.inc(i, 1)

    ctx.barrier()
    ctx.tap("s2_x2", X[0][:, :])
    ctx.tap("s2_h2l", h2l_pack[:, :])
    if S2_TRUNC == 5:
        return

    h2sT2 = [[XQf[c][:, 0:TS] for c in range(CH)],
             [XQf[c][:, TS:2 * TS] for c in range(CH)]]
    kkg = [AB[:, 2048 * i:2048 * i + TS] for i in range(8)]
    kv_acc = [AB[:, 2048 * i + TS:2048 * i + 2 * TS] for i in range(8)]
    rr_t = SCR2[:, 0:TS]
    ev2 = SCR2[:, TS:2 * TS]

    for b in range(B):
        @blk.vector
        def _(vector, b=b):
            vector.wait_ge(s_mm.h, s_mm.val)
            for tti in range(2):
                emit_hs_tile(ctx, vector, h2sT2[tti], Y, b, tti, h2l_pack,
                             (lambda c, b=b: 2 * c + b), ln2g, ln2b)
            vector.drain()
            s_dve.inc(vector.tensor_copy(dmy2[:, 0:1], dmy2[:, 1:2]), 1)

        for tti in range(2):
            col0 = b * TL + tti * TS
            h2sT = h2sT2[tti]

            for g in range(4):
                for ock in range(8):
                    gcol = 1024 * g + 128 * ock

                    @blk.sync
                    def _(sync, gcol=gcol):
                        sync.wait_ge(s_mm.h, s_mm.val)
                        for kk in range(CH):
                            s_io.inc(sync.dma_start(
                                out=wt3[:, 128 * kk:128 * (kk + 1)],
                                in_=ctx.dram["Wck"][128 * kk:128 * (kk + 1), gcol:gcol + 128]), 16)

                    @blk.tensor
                    def _(tensor, ock=ock, h2sT=h2sT):
                        tensor.wait_ge(s_io.h, s_io.val)
                        tensor.wait_ge(s_dve.h, s_dve.val)
                        for kk in range(CH):
                            i = tensor.matmul(psN[ock % 2][:, :], wt3[:, 128 * kk:128 * (kk + 1)],
                                              h2sT[kk][:, :], start=(kk == 0), stop=(kk == CH - 1))
                        s_mm.inc(i, 1)

                    @blk.scalar
                    def _(scalar, ock=ock):
                        scalar.wait_ge(s_mm.h, s_mm.val)
                        scalar.activation(ev2, psN[ock % 2][:, :], AF.Relu)
                        scalar.drain()
                        s_act.inc(scalar.copy(ev2[:, 0:1], ev2[:, 0:1]), 1)

                    @blk.vector
                    def _(vector, ock=ock):
                        vector.wait_ge(s_act.h, s_act.val)
                        vector.tensor_tensor(kkg[ock][:, :], ev2, ev2, OP.mult)
                        vector.drain()
                        s_dve.inc(vector.tensor_copy(dmy2[:, 0:1], dmy2[:, 1:2]), 1)

                for oo in range(CH):
                    @blk.sync
                    def _(sync, g=g, oo=oo):
                        sync.wait_ge(s_mm.h, s_mm.val)
                        for ock in range(8):
                            s_io.inc(sync.dma_start(
                                out=wt3[:, 128 * ock:128 * (ock + 1)],
                                in_=ctx.dram["Wcv"][1024 * g + 128 * ock:1024 * g + 128 * (ock + 1),
                                                    128 * oo:128 * (oo + 1)]), 16)

                    @blk.tensor
                    def _(tensor, oo=oo):
                        tensor.wait_ge(s_io.h, s_io.val)
                        tensor.wait_ge(s_dve.h, s_dve.val)
                        for ock in range(8):
                            i = tensor.matmul(psN[2 + oo % 2][:, :], wt3[:, 128 * ock:128 * (ock + 1)],
                                              kkg[ock][:, :], start=(ock == 0), stop=(ock == 7))
                        s_mm.inc(i, 1)

                    @blk.vector
                    def _(vector, g=g, oo=oo):
                        vector.wait_ge(s_mm.h, s_mm.val)
                        if g == 0:
                            vector.tensor_copy(kv_acc[oo][:, :], psN[2 + oo % 2][:, :])
                        else:
                            vector.tensor_tensor(kv_acc[oo][:, :], kv_acc[oo][:, :],
                                                 psN[2 + oo % 2][:, :], OP.add)
                        vector.drain()
                        s_dve.inc(vector.tensor_copy(dmy2[:, 0:1], dmy2[:, 1:2]), 1)

            for oo in range(CH):
                @blk.sync
                def _(sync, oo=oo):
                    sync.wait_ge(s_mm.h, s_mm.val)
                    for kk in range(CH):
                        s_io.inc(sync.dma_start(
                            out=wt3[:, 128 * kk:128 * (kk + 1)],
                            in_=ctx.dram["Wcr"][128 * kk:128 * (kk + 1), 128 * oo:128 * (oo + 1)]), 16)

                @blk.tensor
                def _(tensor, oo=oo, h2sT=h2sT):
                    tensor.wait_ge(s_io.h, s_io.val)
                    tensor.wait_ge(s_dve.h, s_dve.val)
                    for kk in range(CH):
                        i = tensor.matmul(psN[oo % 2][:, :], wt3[:, 128 * kk:128 * (kk + 1)],
                                          h2sT[kk][:, :], start=(kk == 0), stop=(kk == CH - 1))
                    s_mm.inc(i, 1)

                @blk.scalar
                def _(scalar, oo=oo):
                    scalar.wait_ge(s_mm.h, s_mm.val)
                    scalar.activation(rr_t, psN[oo % 2][:, :], AF.Sigmoid)
                    scalar.drain()
                    s_act.inc(scalar.copy(rr_t[:, 0:1], rr_t[:, 0:1]), 1)

                @blk.vector
                def _(vector, oo=oo, col0=col0):
                    vector.wait_ge(s_act.h, s_act.val)
                    vector.tensor_tensor(SCR2[:, 1024:1024 + TS], rr_t, kv_acc[oo][:, :], OP.mult)
                    vector.drain()
                    s_dve.inc(vector.tensor_tensor(X[oo][:, col0:col0 + TS],
                                                   X[oo][:, col0:col0 + TS],
                                                   SCR2[:, 1024:1024 + TS], OP.add), 1)

    ctx.barrier()
    ctx.tap("s2_kk0", kkg[0])
    ctx.tap("s2_kv0", kv_acc[0])
    ctx.tap("s2_rr7", rr_t)


def emit_wkv(ctx):
    nc, blk = ctx.nc, ctx.block
    X = XTiles(ctx)
    SCR0 = ctx.sb["SCR0"]
    SCR1 = ctx.sb["SCR1"]
    SCR2 = ctx.sb["SCR2"]
    XQf = [ctx.sb[f"XQ{c}"].bitcast(F32) for c in range(CH)]
    s_cc = ctx.sem("cc")
    s_wl = ctx.sem("wl")
    s_wa = ctx.sem("wa")
    s_wd = ctx.sem("wd")
    s_wy = ctx.sem("wy")

    E = X[0]; Eh = X[1]
    kseg = X[2]; vseg = X[3]
    Nq = X[4]; Bq = X[5]
    Cq = X[6]; Dq = X[7]
    vehA = XQf[0][:, 0:TL]; vehB = XQf[1][:, 0:TL]
    Gq = SCR2
    numA = XQf[2][:, 0:TL]; numB = XQf[3][:, 0:TL]
    denA = XQf[4][:, 0:TL]; denB = XQf[5][:, 0:TL]
    yA = XQf[6][:, 0:TL]; yB = XQf[7][:, 0:TL]
    gb = SCR0; eun = SCR1
    wkv_w = ctx.sb["wkv_w_t"]; eu_t = ctx.sb["eu_t"]

    stN = ctx.sbuf("stN", (128, 2))
    stB = ctx.sbuf("stB", (128, 2))
    stBp = ctx.sbuf("stBp", (128, 2))
    stG = ctx.sbuf("stG", (128, 2))
    zt = ctx.sbuf("zt", (128, 2))
    mzt = ctx.sbuf("mzt", (128, 2))
    mzw = ctx.sbuf("mzw", (128, 2))
    lnN = ctx.sbuf("lnN", (128, 2))
    zsc = ctx.sbuf("zsc", (128, 2))
    Mt = ctx.sbuf("Mt", (128, 4))
    imt = ctx.sbuf("imt", (128, 2))
    nsq = ctx.sbuf("nsq", (128, 2))
    sct = ctx.sbuf("sct", (128, 2))
    sc2 = ctx.sbuf("wkv_sc2", (128, 2))
    dmy = ctx.sbuf("wkv_dmy", (128, 4))
    El = ctx.sbuf("wkv_El", (128, 16))
    Ehl = ctx.sbuf("wkv_Ehl", (128, 16))
    Nl = ctx.sbuf("wkv_Nl", (128, 16))
    Bl = ctx.sbuf("wkv_Bl", (128, 16))
    SCq = ctx.sbuf("wkv_SCq", (128, 256))
    SCln = ctx.sbuf("wkv_SCln", (128, 256))

    NCHUNK = TL // RB
    NIT = NCORE * NCHUNK

    El3 = El.rearrange("p (t b) -> p t b", b=2)
    Ehl3 = Ehl.rearrange("p (t b) -> p t b", b=2)
    Nl3 = Nl.rearrange("p (t b) -> p t b", b=2)
    Bl3 = Bl.rearrange("p (t b) -> p t b", b=2)
    E4 = E.rearrange("p (b i j) -> p i j b", b=2, j=RB)
    Eh4 = Eh.rearrange("p (b i j) -> p i j b", b=2, j=RB)
    N4 = Nq.rearrange("p (b i j) -> p i j b", b=2, j=RB)
    B4 = Bq.rearrange("p (b i j) -> p i j b", b=2, j=RB)
    SCR = SCq.rearrange("p (b t) -> p t b", b=2)
    G2 = Gq.rearrange("p (b t) -> p t b", b=2)

    def bhalf(buf, b, a, w):
        return buf[:, TL * b + a:TL * b + a + w]

    def mt(j):
        return Mt[:, 2 * (j % 2):2 * (j % 2) + 2]

    @blk.vector
    def _(vector):
        for t in (stN, stB, stBp, stG, zt, mzt):
            vector.memset(t[:, :], 0.0)
        vector.memset(dmy[:, :], 1.0)
        vector.memset(SCq[:, :], 1.0)
        vector.memset(sc2[:, :], 1.0)
        vector.tensor_scalar(mzw[:, 0:1], wkv_w[:, 0:1], 1.0, None, OP.mult)
        vector.tensor_scalar(mzw[:, 1:2], wkv_w[:, 0:1], 1.0, None, OP.mult)

    ctx.barrier()

    @blk.gpsimd
    def _(gp):
        for s in range(NCORE + 1):
            if s > 0:
                gp.wait_ge(s_wa.h, 2 * (s - 1) + 1)
                gp.wait_ge(s_wd.h, 3 * (s - 1) + 1)
            src = s % NCORE
            wl_base = s_wl.val
            for arm in gp.switch_core_id(n=128):
                j = arm.logical % NCORE
                s_wl.inc(gp.dma_start(out=kseg[:, :],
                                      in_=ctx.dram["cc_k_out"][src, 128 * j:128 * (j + 1), :]), 16)
                s_wl.inc(gp.dma_start(out=vseg[:, :],
                                      in_=ctx.dram["cc_vgat"][src, 128 * j:128 * (j + 1), :]), 16)
            s_wl.val = wl_base + 32
            if s > 0:
                gp.wait_ge(s_wd.h, 3 * (s - 1) + 2)
                s_wy.inc(gp.dma_start(out=ctx.dram["cc_y_in"][:, TOK * (s - 1):TOK * (s - 1) + TL],
                                      in_=yA), 16)
                s_wy.inc(gp.dma_start(out=ctx.dram["cc_y_in"][:, TOK * (s - 1) + TL:TOK * s],
                                      in_=yB), 16)
        gp.wait_ge(s_wy.h, s_wy.val)
        s_cc.inc(gp.collective_compute(
            "AllGather", OP.bypass, replica_groups=[list(range(NCORE))],
            ins=[ctx.dram["cc_y_in"][:, :]], outs=[ctx.dram["cc_y_out"][:, :, :]]), 1)
        gp.wait_ge(s_cc.h, s_cc.val)

    @blk.scalar
    def _(scalar):
        for s in range(NCORE + 1):
            if s > 0:
                scalar.wait_ge(s_wd.h, 3 * (s - 1) + 1)
                scalar.activation(SCln[:, :], SCq[:, :], AF.Ln)
                scalar.activation(lnN[:, :], stN[:, :], AF.Ln)
                scalar.drain()
                s_wa.inc(scalar.copy(lnN[:, 0:1], lnN[:, 0:1]), 1)
                scalar.wait_ge(s_wd.h, 3 * (s - 1) + 3)
            scalar.wait_ge(s_wl.h, 32 * (s + 1))
            scalar.activation(E[:, 0:TL], kseg[:, 0:TL], AF.Exp, bias=mzt[:, 0:1])
            scalar.activation(E[:, TL:TOK], kseg[:, TL:TOK], AF.Exp, bias=mzt[:, 1:2])
            scalar.activation(Eh[:, 0:TL], kseg[:, 0:TL], AF.Exp, bias=mzw[:, 0:1])
            scalar.activation(Eh[:, TL:TOK], kseg[:, TL:TOK], AF.Exp,
                                       bias=mzw[:, 1:2])
            scalar.drain()
            s_wa.inc(scalar.copy(Eh[:, 0:1], Eh[:, 0:1]), 1)

    @blk.vector
    def _(vector):
        def dmyop():
            return vector.tensor_scalar(dmy[:, 0:2], dmy[:, 2:4], 1.0, None, OP.mult)

        def emit_veh():
            vector.tensor_tensor(vehA, vseg[:, 0:TL], Eh[:, 0:TL], OP.mult)
            vector.tensor_tensor(vehB, vseg[:, TL:TOK], Eh[:, TL:TOK], OP.mult)

        def emit_bulk():
            vector.reciprocal(gb[:, 0:TL], Nq[:, 0:TL])
            vector.reciprocal(gb[:, TL:TOK], Nq[:, TL:TOK])
            vector.tensor_scalar(eun[:, 0:TL], Eh[:, 0:TL], eu_t[:, 0:1], None, OP.mult)
            vector.tensor_scalar(eun[:, TL:TOK], Eh[:, TL:TOK], eu_t[:, 0:1], None, OP.mult)
            for b in range(B):
                vector.tensor_tensor(bhalf(Cq, b, 1, TL - 1), bhalf(Bq, b, 0, TL - 1),
                                     bhalf(gb, b, 1, TL - 1), OP.mult)
            for b in range(B):
                veh_b = vehA if b == 0 else vehB
                vector.tensor_tensor(bhalf(Dq, b, 0, TL), veh_b,
                                     bhalf(gb, b, 0, TL), OP.mult)
            for b in range(B):
                vector.tensor_tensor(Cq[:, TL * b:TL * b + 1], stBp[:, b:b + 1],
                                     gb[:, TL * b:TL * b + 1], OP.mult)
            vector.drain()
            for b in range(B):
                vector.tensor_tensor_scan(bhalf(Gq, b, 0, TL), bhalf(Cq, b, 0, TL),
                                          bhalf(Dq, b, 0, TL), stG[:, b:b + 1],
                                          OP.mult, OP.add)
            vector.drain()
            for b in range(B):
                vector.tensor_tensor(bhalf(gb, b, 1, TL - 1), bhalf(Gq, b, 0, TL - 1),
                                     bhalf(Bq, b, 0, TL - 1), OP.mult)
            for b in range(B):
                vector.tensor_tensor(gb[:, TL * b:TL * b + 1], stG[:, b:b + 1],
                                     stBp[:, b:b + 1], OP.mult)
            for b in range(B):
                veh_b = vehA if b == 0 else vehB
                vector.tensor_scalar(bhalf(Dq, b, 0, TL), veh_b,
                                     eu_t[:, 0:1], None, OP.mult)
            vector.drain()
            vector.tensor_tensor(numA, gb[:, 0:TL], Dq[:, 0:TL], OP.add)
            vector.tensor_tensor(numB, gb[:, TL:TOK], Dq[:, TL:TOK], OP.add)
            for b in range(B):
                dstd = denA if b == 0 else denB
                vector.tensor_tensor(dstd[:, 1:TL], bhalf(Bq, b, 0, TL - 1),
                                     bhalf(eun, b, 1, TL - 1), OP.add)
            for b in range(B):
                dstd = denA if b == 0 else denB
                vector.tensor_tensor(dstd[:, 0:1], stBp[:, b:b + 1],
                                     eun[:, TL * b:TL * b + 1], OP.add)
            vector.drain()
            vector.reciprocal(denA, denA)
            vector.reciprocal(denB, denB)
            vector.drain()
            vector.tensor_tensor(yA, numA, denA, OP.mult)
            vector.tensor_tensor(yB, numB, denB, OP.mult)
            vector.drain()
            vector.tensor_copy(stG[:, :], G2[:, TL - 1, :])
            vector.drain()

        def emit_zupd():
            vector.tensor_reduce(zsc[:, 0:1], SCln[:, 0:NCHUNK], AX.X, OP.add)
            vector.tensor_reduce(zsc[:, 1:2], SCln[:, NCHUNK:2 * NCHUNK], AX.X, OP.add)
            vector.drain()
            vector.tensor_tensor(zt[:, :], zt[:, :], lnN[:, :], OP.add)
            vector.drain()
            vector.tensor_tensor(zt[:, :], zt[:, :], zsc[:, :], OP.subtract)
            vector.drain()
            vector.tensor_scalar(mzt[:, :], zt[:, :], -1.0, None, OP.mult)
            vector.tensor_tensor(mzw[:, 0:1], wkv_w[:, 0:1], zt[:, 0:1], OP.subtract)
            vector.tensor_tensor(mzw[:, 1:2], wkv_w[:, 0:1], zt[:, 1:2], OP.subtract)
            vector.drain()

        def emit_seg_prep():
            vector.memset(SCq[:, :], 1.0)
            vector.memset(sc2[:, :], 1.0)
            vector.tensor_copy(stBp[:, :], stB[:, :])
            vector.drain()

        vector.wait_ge(s_wa.h, 1)

        with vector.Fori(1, NIT + 1) as i:
            ic = (i - 1) % NCHUNK
            vector.tensor_copy(El3[:, :, :], E4[:, ic, :, :])
            vector.tensor_copy(Ehl3[:, :, :], Eh4[:, ic, :, :])
            vector.tensor_scalar(El3[:, :, 0], El3[:, :, 0], sc2[:, 0:1], None, OP.mult)
            vector.tensor_scalar(El3[:, :, 1], El3[:, :, 1], sc2[:, 1:2], None, OP.mult)
            vector.tensor_scalar(Ehl3[:, :, 0], Ehl3[:, :, 0], sc2[:, 0:1], None, OP.mult)
            vector.tensor_scalar(Ehl3[:, :, 1], Ehl3[:, :, 1], sc2[:, 1:2], None, OP.mult)
            vector.tensor_tensor(mt(0), stN[:, :], El3[:, 0, :], OP.max)
            vector.tensor_tensor(Nl3[:, 0, :], stB[:, :], Ehl3[:, 0, :], OP.add)
            vector.reciprocal(imt[:, :], mt(0))
            vector.tensor_tensor(nsq[:, :], Nl3[:, 0, :], Nl3[:, 0, :], OP.mult)
            vector.tensor_tensor(mt(1), Nl3[:, 0, :], El3[:, 1, :], OP.max)
            vector.tensor_tensor(Bl3[:, 0, :], nsq[:, :], imt[:, :], OP.mult)
            dmyop()
            for j in range(1, RB - 1):
                vector.tensor_tensor(Nl3[:, j, :], Bl3[:, j - 1, :], Ehl3[:, j, :], OP.add)
                vector.reciprocal(imt[:, :], mt(j))
                vector.tensor_tensor(nsq[:, :], Nl3[:, j, :], Nl3[:, j, :], OP.mult)
                vector.tensor_tensor(mt(j + 1), Nl3[:, j, :], El3[:, j + 1, :], OP.max)
                vector.tensor_tensor(Bl3[:, j, :], nsq[:, :], imt[:, :], OP.mult)
                dmyop()
            J = RB - 1
            vector.tensor_tensor(Nl3[:, J, :], Bl3[:, J - 1, :], Ehl3[:, J, :], OP.add)
            vector.reciprocal(imt[:, :], mt(J))
            vector.tensor_copy(Eh4[:, ic, :, :], Ehl3[:, :, :])
            vector.tensor_tensor(nsq[:, :], Nl3[:, J, :], Nl3[:, J, :], OP.mult)
            vector.reciprocal(sct[:, :], Nl3[:, J, :])
            vector.tensor_tensor(Bl3[:, J, :], nsq[:, :], imt[:, :], OP.mult)
            vector.tensor_tensor(sc2[:, :], sc2[:, :], sct[:, :], OP.mult)
            vector.tensor_tensor(stN[:, :], Nl3[:, J, :], sct[:, :], OP.mult)
            vector.tensor_copy(SCR[:, ic, :], sct[:, :])
            vector.tensor_tensor(Bl3[:, J, :], Bl3[:, J, :], sct[:, :], OP.mult)
            vector.tensor_copy(N4[:, ic, :, :], Nl3[:, :, :])
            vector.tensor_copy(stB[:, :], Bl3[:, J, :])
            vector.tensor_copy(B4[:, ic, :, :], Bl3[:, :, :])
            mr = vector.to_reg(i % NCHUNK)
            with vector.If_cmp(mr, 0, "IS_EQ"):
                vector.drain()
                emit_veh()
                vector.drain()
                s_wd.inc(dmyop(), 1)
                rw = vector.to_reg((i >> 2) - 32)
                vector.wait_ge(s_wy.h, rw)
                emit_bulk()
                s_wd.inc(dmyop(), 1)
                r1 = vector.to_reg(i >> 6)
                vector.wait_ge(s_wa.h, r1)
                emit_zupd()
                s_wd.inc(dmyop(), 1)
                r2 = vector.to_reg((i >> 6) + 1)
                vector.wait_ge(s_wa.h, r2)
                emit_seg_prep()
        vector.drain()

    ctx.barrier()


def emit_s5(ctx, stage):
    nc, blk = ctx.nc, ctx.block
    X = XTiles(ctx)
    AB = ctx.sbuf("ABUF", (128, 16384))
    Y = [AB[:, 2048 * c:2048 * (c + 1)] for c in range(CH)]
    SCR0 = ctx.sbuf("SCR0", (128, TOK))
    SCR1 = ctx.sbuf("SCR1", (128, TOK))
    SCR2 = ctx.sbuf("SCR2", (128, TOK))
    XQ = [ctx.sbuf(f"XQ{c}", (128, TOK), F16) for c in range(CH)]
    XQf = [XQ[c].bitcast(F32) for c in range(CH)]
    wt3 = ctx.sbuf("wt3", (128, CH * 128))
    psN = [ctx.psum(f"ps{i}", (128, TS)) for i in range(4)]
    ones = ctx.sbuf("ones_col", (128, 1))
    s_dve = ctx.sem("dve")
    s_mm = ctx.sem("mm")
    s_act = ctx.sem("act")
    s_io = ctx.sem("io")
    s_st = ctx.sem("st")
    s_cc = ctx.sem("cc")

    ba_t = ctx.sbuf("s5_ba", (128, CH))
    bx_t = ctx.sbuf("s5_bx", (128, CH))
    aL = ctx.sbuf("s5_aL", (128, CH))
    zer = ctx.sbuf("s5_zer", (128, TL))
    pk = ctx.sbuf("s5_pk", (128, 4 * CH))
    pkg = ctx.sbuf("s5_pkg", (128, 32 * NCORE))
    Hall = ctx.sbuf("s5_Hall", (128, 16 * (NCORE - 1)))
    t16 = ctx.sbuf("s5_t16", (128, 16))
    ct = ctx.sbuf("s5_c", (128, 16))

    sr = SCR0
    si = SCR1
    tmpe = SCR2
    at_b = [XQf[0][:, 0:TL], XQf[1][:, 0:TL]]
    bt_b = [XQf[2][:, 0:TL], XQf[3][:, 0:TL]]
    yl_b = [XQf[4][:, 0:TL], XQf[5][:, 0:TL]]
    P_b = [XQf[6][:, 0:TL], XQf[7][:, 0:TL]]
    LN8 = math.log(8.0)

    y_sp = ctx.dramt("s5_y_sp", (C, TOK))
    p_sp = ctx.dramt("s5_p_sp", (C, TOK))
    cc_pk_in = ctx.dramt("cc_pk_in", (128 * 4 * CH,))
    cc_pk_out = ctx.dramt("cc_pk_out", (NCORE, 128 * 4 * CH))

    @blk.sync
    def _(sync):
        for nm, t in (("s5_Wba", ba_t), ("s5_Wbx", bx_t)):
            for c in range(CH):
                s_io.inc(sync.dma_start(out=t[:, c:c + 1],
                                        in_=ctx.dram[nm][128 * c:128 * (c + 1)]), 16)
        for c in range(CH):
            s_io.inc(sync.dma_start(out=aL[:, c:c + 1],
                                    in_=ctx.dram["s5_Lam"][128 * c:128 * (c + 1)]), 16)

    @blk.scalar
    def _(scalar):
        scalar.wait_ge(s_io.h, s_io.val)
        scalar.activation(aL[:, :], aL[:, :], AF.Sigmoid)
        scalar.drain()
        s_act.inc(scalar.copy(aL[:, 0:1], aL[:, 0:1]), 1)

    @blk.vector
    def _(vector):
        vector.memset(zer[:, :], 0.0)
        vector.drain()
        s_dve.inc(vector.tensor_copy(zer[:, 0:1], zer[:, 1:2]), 1)

    emit_ln(ctx, X, Y)

    for oo in range(CH):
        @blk.sync
        def _(sync, oo=oo):
            sync.wait_ge(s_mm.h, s_mm.val)
            for kk in range(CH):
                s_io.inc(sync.dma_start(
                    out=wt3[:, 128 * kk:128 * (kk + 1)],
                    in_=ctx.dram["s5_Wa"][128 * kk:128 * (kk + 1), 128 * oo:128 * (oo + 1)]), 16)

        @blk.tensor
        def _(tensor):
            tensor.wait_ge(s_io.h, s_io.val)
            tensor.wait_ge(s_act.h, s_act.val)
            tensor.wait_ge(s_dve.h, s_dve.val)
            for nt in range(NT):
                for kk in range(CH):
                    i = tensor.matmul(psN[nt][:, :], wt3[:, 128 * kk:128 * (kk + 1)],
                                      Y[kk][:, TS * nt:TS * (nt + 1)],
                                      start=(kk == 0), stop=(kk == CH - 1))
            s_mm.inc(i, 1)

        @blk.scalar
        def _(scalar, oo=oo):
            scalar.wait_ge(s_mm.h, s_mm.val)
            for nt in range(NT):
                scalar.activation(sr[:, TS * nt:TS * (nt + 1)], psN[nt][:, :],
                                  AF.Sigmoid, bias=ba_t[:, oo:oo + 1])
            scalar.drain()
            s_act.inc(scalar.copy(sr[:, 0:1], sr[:, 0:1]), 1)

        @blk.sync
        def _(sync, oo=oo):
            sync.wait_ge(s_mm.h, s_mm.val)
            for kk in range(CH):
                s_io.inc(sync.dma_start(
                    out=wt3[:, 128 * kk:128 * (kk + 1)],
                    in_=ctx.dram["s5_Wx"][128 * kk:128 * (kk + 1), 128 * oo:128 * (oo + 1)]), 16)

        @blk.tensor
        def _(tensor):
            tensor.wait_ge(s_io.h, s_io.val)
            tensor.wait_ge(s_act.h, s_act.val)
            for nt in range(NT):
                for kk in range(CH):
                    i = tensor.matmul(psN[nt][:, :], wt3[:, 128 * kk:128 * (kk + 1)],
                                      Y[kk][:, TS * nt:TS * (nt + 1)],
                                      start=(kk == 0), stop=(kk == CH - 1))
            s_mm.inc(i, 1)

        @blk.scalar
        def _(scalar, oo=oo):
            scalar.wait_ge(s_mm.h, s_mm.val)
            for nt in range(NT):
                scalar.activation(si[:, TS * nt:TS * (nt + 1)], psN[nt][:, :],
                                  AF.Sigmoid, bias=bx_t[:, oo:oo + 1])
            scalar.activation(tmpe[:, :], sr[:, :], AF.Exp, scale=-LN8)
            scalar.drain()
            s_act.inc(scalar.copy(tmpe[:, 0:1], tmpe[:, 0:1]), 1)

        @blk.vector
        def _(vector, oo=oo):
            vector.wait_ge(s_act.h, s_act.val)
            vector.tensor_scalar(at_b[0], tmpe[:, 0:TL], aL[:, oo:oo + 1], None, OP.mult)
            vector.tensor_scalar(at_b[1], tmpe[:, TL:TOK], aL[:, oo:oo + 1], None, OP.mult)
            vector.tensor_tensor(bt_b[0], at_b[0], at_b[0], OP.mult)
            vector.tensor_tensor(bt_b[1], at_b[1], at_b[1], OP.mult)
            vector.tensor_scalar(bt_b[0], bt_b[0], -1.0, 1.0, OP.mult, OP.add)
            vector.tensor_scalar(bt_b[1], bt_b[1], -1.0, 1.0, OP.mult, OP.add)
            vector.drain()
            s_dve.inc(vector.tensor_copy(t16[:, 0:1], t16[:, 1:2]), 1)

        @blk.scalar
        def _(scalar):
            scalar.wait_ge(s_dve.h, s_dve.val)
            scalar.activation(tmpe[:, 0:TL], bt_b[0], AF.Sqrt)
            scalar.activation(tmpe[:, TL:TOK], bt_b[1], AF.Sqrt)
            scalar.drain()
            s_act.inc(scalar.copy(tmpe[:, 0:1], tmpe[:, 0:1]), 1)

        @blk.vector
        def _(vector, oo=oo):
            vector.wait_ge(s_act.h, s_act.val)
            vector.wait_ge(s_st.h, s_st.val)
            vector.tensor_tensor(bt_b[0], tmpe[:, 0:TL], si[:, 0:TL], OP.mult)
            vector.tensor_tensor(bt_b[1], tmpe[:, TL:TOK], si[:, TL:TOK], OP.mult)
            vector.tensor_tensor(bt_b[0], bt_b[0], Y[oo][:, 0:TL], OP.mult)
            vector.tensor_tensor(bt_b[1], bt_b[1], Y[oo][:, TL:TOK], OP.mult)
            vector.drain()
            vector.tensor_tensor_scan(yl_b[0], at_b[0], bt_b[0], zer[:, 0:1],
                                      OP.mult, OP.add)
            vector.tensor_tensor_scan(yl_b[1], at_b[1], bt_b[1], zer[:, 0:1],
                                      OP.mult, OP.add)
            vector.tensor_tensor_scan(P_b[0], at_b[0], zer[:, :], ones[:, 0:1],
                                      OP.mult, OP.add)
            vector.tensor_tensor_scan(P_b[1], at_b[1], zer[:, :], ones[:, 0:1],
                                      OP.mult, OP.add)
            vector.drain()
            vector.tensor_copy(pk[:, 4 * oo + 0:4 * oo + 1], P_b[0][:, TL - 1:TL])
            vector.tensor_copy(pk[:, 4 * oo + 1:4 * oo + 2], P_b[1][:, TL - 1:TL])
            vector.tensor_copy(pk[:, 4 * oo + 2:4 * oo + 3], yl_b[0][:, TL - 1:TL])
            vector.tensor_copy(pk[:, 4 * oo + 3:4 * oo + 4], yl_b[1][:, TL - 1:TL])
            vector.drain()
            s_dve.inc(vector.tensor_copy(t16[:, 0:1], t16[:, 1:2]), 1)

        @blk.gpsimd
        def _(gp, oo=oo):
            gp.wait_ge(s_dve.h, s_dve.val)
            s_st.inc(gp.dma_start(out=y_sp[128 * oo:128 * (oo + 1), 0:TL], in_=yl_b[0]), 16)
            s_st.inc(gp.dma_start(out=y_sp[128 * oo:128 * (oo + 1), TL:TOK], in_=yl_b[1]), 16)
            s_st.inc(gp.dma_start(out=p_sp[128 * oo:128 * (oo + 1), 0:TL], in_=P_b[0]), 16)
            s_st.inc(gp.dma_start(out=p_sp[128 * oo:128 * (oo + 1), TL:TOK], in_=P_b[1]), 16)

    @blk.gpsimd
    def _(gp):
        gp.wait_ge(s_st.h, s_st.val)
        s_st.inc(gp.dma_start(out=cc_pk_in[:], in_=pk[:, :]), 16)
        gp.wait_ge(s_st.h, s_st.val)
        s_cc.inc(gp.collective_compute(
            "AllGather", OP.bypass, replica_groups=[list(range(NCORE))],
            ins=[cc_pk_in[:]], outs=[cc_pk_out[:, :]]), 1)
        gp.wait_ge(s_cc.h, s_cc.val)
        for s in range(NCORE):
            s_io.inc(gp.dma_start(out=pkg[:, 32 * s:32 * (s + 1)], in_=cc_pk_out[s, :]), 16)
        gp.wait_ge(s_io.h, s_io.val)

    pkv = pkg.rearrange("p (s o f) -> p s o f", s=NCORE, f=4)
    Hv = Hall.rearrange("p (s o f) -> p s o f", s=NCORE - 1, f=2)
    t16v = t16.rearrange("p (o f) -> p o f", f=2)

    @blk.vector
    def _(vector):
        vector.wait_ge(s_io.h, s_io.val)
        vector.tensor_copy(Hv[:, 0, :, :], pkv[:, 0, :, 2:4])
        vector.drain()
        for s in range(1, NCORE - 1):
            vector.tensor_tensor(t16v[:, :, :], pkv[:, s, :, 0:2], Hv[:, s - 1, :, :], OP.mult)
            vector.drain()
            vector.tensor_tensor(Hv[:, s, :, :], t16v[:, :, :], pkv[:, s, :, 2:4], OP.add)
            vector.drain()
        for arm in vector.switch_core_id(n=128):
            r = arm.logical % NCORE
            if r == 0:
                vector.memset(ct[:, :], 0.0)
            else:
                vector.tensor_copy(ct[:, :], Hall[:, 16 * (r - 1):16 * r])
        vector.drain()
        s_dve.inc(vector.tensor_copy(t16[:, 0:1], t16[:, 1:2]), 1)

    ctx.barrier()

    for oo in range(CH):
        @blk.gpsimd
        def _(gp, oo=oo):
            gp.wait_ge(s_dve.h, s_dve.val)
            s_io.inc(gp.dma_start(out=SCR0[:, :], in_=y_sp[128 * oo:128 * (oo + 1), :]), 16)
            s_io.inc(gp.dma_start(out=SCR1[:, :], in_=p_sp[128 * oo:128 * (oo + 1), :]), 16)

        @blk.vector
        def _(vector, oo=oo):
            vector.wait_ge(s_io.h, s_io.val)
            for b in range(B):
                vector.tensor_scalar(SCR1[:, TL * b:TL * (b + 1)], SCR1[:, TL * b:TL * (b + 1)],
                                     ct[:, 2 * oo + b:2 * oo + b + 1], None, OP.mult)
            vector.drain()
            for b in range(B):
                vector.tensor_tensor(SCR1[:, TL * b:TL * (b + 1)], SCR1[:, TL * b:TL * (b + 1)],
                                     SCR0[:, TL * b:TL * (b + 1)], OP.add)
            vector.drain()
            for b in range(B):
                i = vector.tensor_tensor(X[oo][:, TL * b:TL * (b + 1)],
                                         X[oo][:, TL * b:TL * (b + 1)],
                                         SCR1[:, TL * b:TL * (b + 1)], OP.add)
            vector.drain()
            s_dve.inc(vector.tensor_copy(t16[:, 0:1], t16[:, 1:2]), 1)

    ctx.barrier()


def emit_s4(ctx, stage):
    nc, blk = ctx.nc, ctx.block
    X = XTiles(ctx)
    AB = ctx.sbuf("ABUF", (128, 16384))
    Y = [AB[:, 2048 * c:2048 * (c + 1)] for c in range(CH)]
    SCR0 = ctx.sbuf("SCR0", (128, TOK))
    SCR1 = ctx.sbuf("SCR1", (128, TOK))
    SCR2 = ctx.sbuf("SCR2", (128, TOK))
    XQ = [ctx.sbuf(f"XQ{c}", (128, TOK), F16) for c in range(CH)]
    XQf = [XQ[c].bitcast(F32) for c in range(CH)]
    wt3 = ctx.sbuf("wt3", (128, CH * 128))
    psN = [ctx.psum(f"ps{i}", (128, TS)) for i in range(4)]
    s_dve = ctx.sem("dve")
    s_mm = ctx.sem("mm")
    s_act = ctx.sem("act")
    s_io = ctx.sem("io")
    s_st = ctx.sem("st")

    gw_t = ctx.sbuf("s4_gw", (128, 64))
    gb_t = ctx.sbuf("s4_gb", (128, 8))
    sc8 = ctx.sbuf("s4_sc8", (128, 8))
    g_sfc = ctx.sbuf("s4_sfc", (128, 8))
    g_gs = ctx.sbuf("s4_gs", (128, 8))
    g_eq = ctx.sbuf("s4_eq", (128, 8))
    g_cum = ctx.sbuf("s4_cum", (128, 8))
    g_fm = ctx.sbuf("s4_fm", (128, 8))
    g_t = ctx.sbuf("s4_t", (128, 8))
    g_m = ctx.sbuf("s4_m", (128, 8))
    g_e = ctx.sbuf("s4_e", (128, 16))
    g_msk = ctx.sbuf("s4_msk", (128, 8))
    g_tmp8 = ctx.sbuf("s4_tmp8", (128, 16))
    g_w = ctx.sbuf("s4_w", (128, 8))
    s4o = ctx.sbuf("s4_ones", (128, 8))
    s4z = ctx.sbuf("s4_zc", (128, 1))
    fnc = ctx.sbuf("s4_fnc", (128, 2))
    BIG = 1.0e30

    om_spT = ctx.dramt(f"{stage}_omT", (EMOE, TOK))

    @blk.sync
    def _(sync):
        for kk in range(CH):
            s_io.inc(sync.dma_start(out=gw_t[:, 8 * kk:8 * (kk + 1)],
                                    in_=ctx.dram["s4_gatew"][128 * kk:128 * (kk + 1), :]), 16)
        s_io.inc(sync.dma_start(
            out=gb_t[:, :],
            in_=ctx.dram["s4_gateb"][:].unsqueeze(0).broadcast_to([128, EMOE])), 16)

    @blk.vector
    def _(vector):
        vector.memset(s4o[:, :], 1.0)
        vector.memset(s4z[:, :], 0.0)
        vector.memset(fnc[:, :], 1.0)
        vector.drain()
        s_dve.inc(vector.tensor_copy(fnc[:, 0:1], fnc[:, 1:2]), 1)

    emit_ln(ctx, X, Y)

    gsv = g_sfc.rearrange("p (g two) -> p g two", two=2)
    mv = g_msk.rearrange("p (g two) -> p g two", two=2)

    for t in range(TOK // 128):
        kc, tc = t // 8, t % 8
        @blk.tensor
        def _(tensor, t=t):
            tensor.wait_ge(s_io.h, s_io.val)
            tensor.wait_ge(s_dve.h, s_dve.val)
            for kk in range(CH):
                i = tensor.matmul(psN[0][:, 0:EMOE], Y[kk][:, 128 * t:128 * (t + 1)],
                                  gw_t[:, 8 * kk:8 * (kk + 1)],
                                  start=(kk == 0), stop=(kk == CH - 1))
            s_mm.inc(i, 1)

        @blk.scalar
        def _(scalar):
            scalar.wait_ge(s_mm.h, s_mm.val)
            scalar.activation(sc8[:, :], psN[0][:, 0:EMOE], AF.Sigmoid)
            scalar.drain()
            s_act.inc(scalar.copy(sc8[:, 0:1], sc8[:, 0:1]), 1)

        @blk.vector
        def _(vector, t=t):
            vector.wait_ge(s_act.h, s_act.val)
            vector.wait_ge(s_st.h, s_st.val)
            D = vector.drain
            vector.tensor_tensor(g_sfc[:, :], sc8[:, :], gb_t[:, :], OP.add)
            D()
            vector.tensor_tensor(g_gs[:, 0:4], gsv[:, :, 0], gsv[:, :, 1], OP.add)
            D()
            vector.tensor_reduce(g_m[:, 0:1], g_gs[:, 0:4], AX.X, OP.max)
            D()
            vector.tensor_scalar(g_eq[:, 0:4], g_gs[:, 0:4], g_m[:, 0:1], None, OP.is_equal)
            D()
            vector.tensor_tensor_scan(g_cum[:, 0:4], s4o[:, 0:4], g_eq[:, 0:4],
                                      s4z[:, 0:1], OP.mult, OP.add)
            D()
            vector.tensor_scalar(g_t[:, 0:4], g_cum[:, 0:4], 1.0, None, OP.is_equal)
            D()
            vector.tensor_tensor(g_fm[:, 0:4], g_eq[:, 0:4], g_t[:, 0:4], OP.mult)
            D()
            vector.tensor_scalar(g_t[:, 0:4], g_fm[:, 0:4], BIG, None, OP.mult)
            D()
            vector.tensor_tensor(g_gs[:, 4:8], g_gs[:, 0:4], g_t[:, 0:4], OP.subtract)
            D()
            vector.tensor_reduce(g_m[:, 1:2], g_gs[:, 4:8], AX.X, OP.max)
            D()
            vector.tensor_scalar(g_eq[:, 4:8], g_gs[:, 4:8], g_m[:, 1:2], None, OP.is_equal)
            D()
            vector.tensor_tensor_scan(g_cum[:, 4:8], s4o[:, 0:4], g_eq[:, 4:8],
                                      s4z[:, 0:1], OP.mult, OP.add)
            D()
            vector.tensor_scalar(g_t[:, 4:8], g_cum[:, 4:8], 1.0, None, OP.is_equal)
            D()
            vector.tensor_tensor(g_fm[:, 4:8], g_eq[:, 4:8], g_t[:, 4:8], OP.mult)
            D()
            vector.tensor_tensor(g_t[:, 0:4], g_fm[:, 0:4], g_fm[:, 4:8], OP.add)
            D()
            vector.tensor_copy(mv[:, :, 0], g_t[:, 0:4])
            vector.tensor_copy(mv[:, :, 1], g_t[:, 0:4])
            D()
            vector.tensor_tensor(g_tmp8[:, 0:8], g_sfc[:, :], g_msk[:, :], OP.mult)
            D()
            vector.tensor_reduce(g_m[:, 2:3], g_tmp8[:, 0:8], AX.X, OP.max)
            D()
            vector.tensor_scalar(g_eq[:, 0:8], g_tmp8[:, 0:8], g_m[:, 2:3], None, OP.is_equal)
            D()
            vector.tensor_tensor_scan(g_cum[:, 0:8], s4o[:, 0:8], g_eq[:, 0:8],
                                      s4z[:, 0:1], OP.mult, OP.add)
            D()
            vector.tensor_scalar(g_t[:, 0:8], g_cum[:, 0:8], 1.0, None, OP.is_equal)
            D()
            vector.tensor_tensor(g_e[:, 0:8], g_eq[:, 0:8], g_t[:, 0:8], OP.mult)
            D()
            vector.tensor_tensor(g_t[:, 0:8], g_e[:, 0:8], g_sfc[:, :], OP.mult)
            D()
            vector.tensor_reduce(g_m[:, 4:5], g_t[:, 0:8], AX.X, OP.add)
            vector.tensor_scalar(g_t[:, 0:8], g_e[:, 0:8], BIG, None, OP.mult)
            D()
            vector.tensor_tensor(g_tmp8[:, 8:16], g_tmp8[:, 0:8], g_t[:, 0:8], OP.subtract)
            D()
            vector.tensor_reduce(g_m[:, 3:4], g_tmp8[:, 8:16], AX.X, OP.max)
            D()
            vector.tensor_scalar(g_eq[:, 0:8], g_tmp8[:, 8:16], g_m[:, 3:4], None, OP.is_equal)
            D()
            vector.tensor_tensor_scan(g_cum[:, 0:8], s4o[:, 0:8], g_eq[:, 0:8],
                                      s4z[:, 0:1], OP.mult, OP.add)
            D()
            vector.tensor_scalar(g_t[:, 0:8], g_cum[:, 0:8], 1.0, None, OP.is_equal)
            D()
            vector.tensor_tensor(g_e[:, 8:16], g_eq[:, 0:8], g_t[:, 0:8], OP.mult)
            D()
            vector.tensor_tensor(g_t[:, 0:8], g_e[:, 8:16], g_sfc[:, :], OP.mult)
            D()
            vector.tensor_reduce(g_m[:, 5:6], g_t[:, 0:8], AX.X, OP.add)
            D()
            vector.tensor_tensor(g_m[:, 6:7], g_m[:, 4:5], g_m[:, 5:6], OP.add)
            D()
            vector.tensor_scalar(g_m[:, 6:7], g_m[:, 6:7], 1.0, 1.0e-20, OP.mult, OP.add)
            D()
            vector.reciprocal(g_m[:, 7:8], g_m[:, 6:7])
            D()
            vector.tensor_tensor(g_m[:, 4:5], g_m[:, 4:5], g_m[:, 7:8], OP.mult)
            vector.tensor_tensor(g_m[:, 5:6], g_m[:, 5:6], g_m[:, 7:8], OP.mult)
            D()
            vector.tensor_scalar(g_m[:, 4:5], g_m[:, 4:5], 2.5, None, OP.mult)
            vector.tensor_scalar(g_m[:, 5:6], g_m[:, 5:6], 2.5, None, OP.mult)
            D()
            vector.tensor_scalar(g_w[:, :], g_e[:, 0:8], g_m[:, 4:5], None, OP.mult)
            vector.tensor_scalar(g_t[:, 0:8], g_e[:, 8:16], g_m[:, 5:6], None, OP.mult)
            D()
            vector.tensor_tensor(g_w[:, :], g_w[:, :], g_t[:, 0:8], OP.add)
            D()
            s_dve.inc(vector.tensor_copy(fnc[:, 0:1], fnc[:, 1:2]), 1)

        @blk.gpsimd
        def _(gp, t=t):
            gp.wait_ge(s_dve.h, s_dve.val)
            for e in range(EMOE):
                s_st.inc(gp.dma_start(out=om_spT[e, 128 * t:128 * (t + 1)],
                                      in_=g_w[:, e:e + 1]), 16)

    ctx.barrier()
    ctx.tap(f"{stage}_om", om_spT[:, :])

    h_t = [XQf[ic][:, 0:TS] for ic in range(4)]
    yacc = [XQf[4 + i // 2][:, (i % 2) * TS:(i % 2 + 1) * TS] for i in range(8)]
    wbc = SCR2[:, 0:TS]
    gbuf = SCR0
    ubuf = SCR1

    def mm_block(wsrc, row0, col0, nrow, psdst, moving, movsl, tag):
        @blk.sync
        def _(sync):
            sync.wait_ge(s_mm.h, s_mm.val)
            for kk in range(nrow):
                s_io.inc(sync.dma_start(
                    out=wt3[:, 128 * kk:128 * (kk + 1)],
                    in_=wsrc[row0 + 128 * kk:row0 + 128 * (kk + 1), col0:col0 + 128]), 16)

        @blk.tensor
        def _(tensor):
            tensor.wait_ge(s_io.h, s_io.val)
            tensor.wait_ge(s_dve.h, s_dve.val)
            tensor.wait_ge(s_act.h, s_act.val)
            for kk in range(nrow):
                i = tensor.matmul(psdst[:, :], wt3[:, 128 * kk:128 * (kk + 1)],
                                  moving[kk][:, movsl], start=(kk == 0), stop=(kk == nrow - 1))
            s_mm.inc(i, 1)

    for tt in range(NT):
        tsl = slice(TS * tt, TS * (tt + 1))

        for e in range(EMOE):
            @blk.gpsimd
            def _(gp, e=e, tt=tt):
                gp.wait_ge(s_dve.h, s_dve.val)
                s_io.inc(gp.dma_start(
                    out=wbc,
                    in_=om_spT[e, TS * tt:TS * (tt + 1)].unsqueeze(0).broadcast_to([128, TS])), 16)

            for ic in range(4):
                mm_block(ctx.dram["s4_wg"], 1024 * e, 128 * ic, CH, psN[ic % 2], Y, tsl, "g")

                @blk.scalar
                def _(scalar, ic=ic):
                    scalar.wait_ge(s_mm.h, s_mm.val)
                    scalar.activation(gbuf[:, TS * ic:TS * (ic + 1)], psN[ic % 2][:, :], AF.Silu)
                    scalar.drain()
                    s_act.inc(scalar.copy(gbuf[:, 0:1], gbuf[:, 0:1]), 1)
            for ic in range(4):
                mm_block(ctx.dram["s4_wu"], 1024 * e, 128 * ic, CH, psN[ic % 2], Y, tsl, "u")

                @blk.scalar
                def _(scalar, ic=ic):
                    scalar.wait_ge(s_mm.h, s_mm.val)
                    scalar.activation(ubuf[:, TS * ic:TS * (ic + 1)], psN[ic % 2][:, :], AF.Copy)
                    scalar.drain()
                    s_act.inc(scalar.copy(ubuf[:, 0:1], ubuf[:, 0:1]), 1)

            @blk.vector
            def _(vector):
                vector.wait_ge(s_act.h, s_act.val)
                vector.wait_ge(s_io.h, s_io.val)
                for ic in range(4):
                    vector.tensor_tensor(h_t[ic], gbuf[:, TS * ic:TS * (ic + 1)],
                                         ubuf[:, TS * ic:TS * (ic + 1)], OP.mult)
                for ic in range(4):
                    vector.tensor_tensor(h_t[ic], h_t[ic], wbc, OP.mult)
                vector.drain()
                s_dve.inc(vector.tensor_copy(fnc[:, 0:1], fnc[:, 1:2]), 1)

            for oc in range(CH):
                mm_block(ctx.dram["s4_wd"], 512 * e, 128 * oc, 4, psN[2 + oc % 2],
                         h_t, slice(0, TS), "d")

                @blk.vector
                def _(vector, e=e, oc=oc):
                    vector.wait_ge(s_mm.h, s_mm.val)
                    if e == 0:
                        vector.tensor_copy(yacc[oc], psN[2 + oc % 2][:, :])
                    else:
                        vector.tensor_tensor(yacc[oc], yacc[oc], psN[2 + oc % 2][:, :], OP.add)
                    vector.drain()
                    s_dve.inc(vector.tensor_copy(fnc[:, 0:1], fnc[:, 1:2]), 1)

        for half in range(2):
            for ic4 in range(4):
                ic = 4 * half + ic4
                mm_block(ctx.dram["s4_swg"], 0, 128 * ic, CH, psN[ic4 % 2], Y, tsl, "sg")

                @blk.scalar
                def _(scalar, ic4=ic4):
                    scalar.wait_ge(s_mm.h, s_mm.val)
                    scalar.activation(gbuf[:, TS * ic4:TS * (ic4 + 1)], psN[ic4 % 2][:, :], AF.Silu)
                    scalar.drain()
                    s_act.inc(scalar.copy(gbuf[:, 0:1], gbuf[:, 0:1]), 1)
            for ic4 in range(4):
                ic = 4 * half + ic4
                mm_block(ctx.dram["s4_swu"], 0, 128 * ic, CH, psN[ic4 % 2], Y, tsl, "su")

                @blk.scalar
                def _(scalar, ic4=ic4):
                    scalar.wait_ge(s_mm.h, s_mm.val)
                    scalar.activation(ubuf[:, TS * ic4:TS * (ic4 + 1)], psN[ic4 % 2][:, :], AF.Copy)
                    scalar.drain()
                    s_act.inc(scalar.copy(ubuf[:, 0:1], ubuf[:, 0:1]), 1)

            @blk.vector
            def _(vector):
                vector.wait_ge(s_act.h, s_act.val)
                for ic4 in range(4):
                    vector.tensor_tensor(h_t[ic4], gbuf[:, TS * ic4:TS * (ic4 + 1)],
                                         ubuf[:, TS * ic4:TS * (ic4 + 1)], OP.mult)
                vector.drain()
                s_dve.inc(vector.tensor_copy(fnc[:, 0:1], fnc[:, 1:2]), 1)

            for oc in range(CH):
                mm_block(ctx.dram["s4_swd"], 512 * half, 128 * oc, 4, psN[2 + oc % 2],
                         h_t, slice(0, TS), "sd")

                @blk.vector
                def _(vector, oc=oc):
                    vector.wait_ge(s_mm.h, s_mm.val)
                    vector.tensor_tensor(yacc[oc], yacc[oc], psN[2 + oc % 2][:, :], OP.add)
                    vector.drain()
                    s_dve.inc(vector.tensor_copy(fnc[:, 0:1], fnc[:, 1:2]), 1)

        @blk.vector
        def _(vector, tt=tt, tsl=tsl):
            vector.wait_ge(s_dve.h, s_dve.val)
            for oc in range(CH):
                vector.tensor_scalar(gbuf[:, TS * (oc % 4):TS * (oc % 4 + 1)],
                                     yacc[oc], 0.5, None, OP.mult)
                if oc % 4 == 3 or oc == CH - 1:
                    vector.drain()
                    for oc2 in range(oc - oc % 4, oc + 1):
                        vector.tensor_tensor(X[oc2][:, tsl], X[oc2][:, tsl],
                                             gbuf[:, TS * (oc2 % 4):TS * (oc2 % 4 + 1)], OP.add)
                    vector.drain()
            s_dve.inc(vector.tensor_copy(fnc[:, 0:1], fnc[:, 1:2]), 1)

    ctx.barrier()


def emit_stub(ctx, stage):
    ctx.barrier()


STAGE_EMITTERS = {
    "s1": emit_ffn0,
    "s2": emit_s2,
    "s3": emit_ffn0,
    "s4": emit_s4,
    "s5": emit_s5,
    "s6": emit_s4,
}


def build_program(stages):
    nc = bass.Bass("TRN2", target_bir_lowering=False)

    io = {}
    io["x_in"] = nc.dram_tensor("x_in", [C, TOK], F32, kind="ExternalInput").ap()
    io["x_out"] = nc.dram_tensor("x_out", [C, TOK], F32, kind="ExternalOutput").ap()
    io["ident_in"] = nc.dram_tensor("ident_in", [128, 128], F32, kind="ExternalInput").ap()

    need_ffn0 = ("s1" in stages) or ("s3" in stages)
    if "s2" in stages:
        for nm, shp in (("Wk", (C, C)), ("Wv", (C, C)), ("Wr", (C, C)),
                        ("Wck", (C, CKDIM)), ("Wcv", (CKDIM, C)), ("Wcr", (C, C)),
                        ("ln1_g", (C,)), ("ln1_b", (C,)), ("ln2_g", (C,)), ("ln2_b", (C,)),
                        ("wkv_w", (128,)), ("wkv_u", (128,)), ("ts_pack", (TOK,))):
            io[nm] = nc.dram_tensor(nm, list(shp), F32, kind="ExternalInput").ap()
    if need_ffn0:
        for nm, shp in (("wg_a", (IFFN, C)), ("wu_a", (IFFN, C)), ("wd_a", (C, IFFN)),
                        ("wg_b", (C, IFFN)), ("wu_b", (C, IFFN)), ("wd_b", (IFFN, C)),
                        ("clip_x", (1,)), ("clip_down", (1,))):
            io[nm] = nc.dram_tensor(nm, list(shp), F32, kind="ExternalInput").ap()
    if ("s4" in stages) or ("s6" in stages):
        for nm, shp in (("s4_gatew", (C, EMOE)), ("s4_gateb", (EMOE,)),
                        ("s4_wg", (EMOE * C, IEXP)), ("s4_wu", (EMOE * C, IEXP)),
                        ("s4_wd", (EMOE * IEXP, C)), ("s4_swg", (C, C)),
                        ("s4_swu", (C, C)), ("s4_swd", (C, C))):
            io[nm] = nc.dram_tensor(nm, list(shp), F32, kind="ExternalInput").ap()
    if "s5" in stages:
        for nm, shp in (("s5_Wa", (C, C)), ("s5_Wx", (C, C)), ("s5_Wba", (C,)),
                        ("s5_Wbx", (C,)), ("s5_Lam", (C,))):
            io[nm] = nc.dram_tensor(nm, list(shp), F32, kind="ExternalInput").ap()

    with nc.Block() as block:
        ctx = Ctx(nc, block)
        ctx.dram.update(io)
        ctx.ident = ctx.sbuf("ident", (128, 128))
        s_io = ctx.sem("io")

        X = XTiles(ctx)
        ctx.sbuf("ABUF", (128, 16384))

        @block.sync
        def _(sync):
            s_io.inc(sync.dma_start(out=ctx.ident[:, :], in_=io["ident_in"][:, :]), 16)
            for c in range(CH):
                s_io.inc(sync.dma_start(out=X[c][:, :], in_=io["x_in"][128 * c:128 * (c + 1), :]), 16)

        @block.vector
        def _(vector):
            vector.wait_ge(s_io.h, s_io.val)
            vector.memset(ctx.sbuf("ones_col", (128, 1))[:, :], 1.0)

        if need_ffn0:
            clip_x = ctx.sbuf("clip_x_t", (1, 2))
            clip_d = ctx.sbuf("clip_d_t", (1, 2))

            @block.sync
            def _(sync):
                s_io.inc(sync.dma_start(out=clip_x[0:1, 0:1], in_=io["clip_x"][0:1]), 16)
                s_io.inc(sync.dma_start(out=clip_d[0:1, 0:1], in_=io["clip_down"][0:1]), 16)

        ctx.barrier()

        if need_ffn0:
            emit_ffn0_weight_quant(ctx)

        for st in stages:
            STAGE_EMITTERS[st](ctx, st)

        s_st = ctx.sem("st")

        @block.gpsimd
        def _(gp):
            for c in range(CH):
                s_st.inc(gp.dma_start(out=io["x_out"][128 * c:128 * (c + 1), :], in_=X[c][:, :]), 16)
            gp.wait_ge(s_st.h, s_st.val)

    return nc


def _core_inmap(inputs, core, x_full):
    sl = x_full[:, TL * core:TL * (core + 1), :]
    xfm = np.ascontiguousarray(sl.transpose(2, 0, 1).reshape(C, TOK))
    m = {
        "x_in": xfm,
        "ident_in": np.eye(128, dtype=np.float32),
        "wg_a": np.ascontiguousarray(inputs["L0_ffn_w_gate"]),
        "wu_a": np.ascontiguousarray(inputs["L0_ffn_w_up"]),
        "wd_a": np.ascontiguousarray(inputs["L0_ffn_w_down"]),
        "wg_b": np.ascontiguousarray(inputs["L0_ffn_w_gate"].T),
        "wu_b": np.ascontiguousarray(inputs["L0_ffn_w_up"].T),
        "wd_b": np.ascontiguousarray(inputs["L0_ffn_w_down"].T),
        "clip_x": np.ascontiguousarray(inputs["L0_ffn_clip_x"]),
        "clip_down": np.ascontiguousarray(inputs["L0_ffn_clip_down"]),
    }
    ts = inputs["L0_attn_time_shift"].reshape(C)
    ts_pack = np.zeros((128, 16), dtype=np.float32)
    for c in range(CH):
        ts_pack[:, 2 * c] = ts[128 * c:128 * (c + 1)]
        ts_pack[:, 2 * c + 1] = ts[128 * c:128 * (c + 1)]
    m.update({
        "Wk": np.ascontiguousarray(inputs["L0_attn_Wk"]),
        "Wv": np.ascontiguousarray(inputs["L0_attn_Wv"]),
        "Wr": np.ascontiguousarray(inputs["L0_attn_Wr"]),
        "Wck": np.ascontiguousarray(inputs["L0_attn_Wck"]),
        "Wcv": np.ascontiguousarray(inputs["L0_attn_Wcv"]),
        "Wcr": np.ascontiguousarray(inputs["L0_attn_Wcr"]),
        "ln1_g": np.ascontiguousarray(inputs["L0_attn_ln1_g"]),
        "ln1_b": np.ascontiguousarray(inputs["L0_attn_ln1_b"]),
        "ln2_g": np.ascontiguousarray(inputs["L0_attn_ln2_g"]),
        "ln2_b": np.ascontiguousarray(inputs["L0_attn_ln2_b"]),
        "wkv_w": np.ascontiguousarray(inputs["L0_attn_time_decay"][128 * core:128 * (core + 1)]),
        "wkv_u": np.ascontiguousarray(inputs["L0_attn_time_first"][128 * core:128 * (core + 1)]),
        "ts_pack": np.ascontiguousarray(ts_pack.reshape(TOK)),
    })
    m.update({
        "s4_gatew": np.ascontiguousarray(inputs["L1_ffn_gate_w"]),
        "s4_gateb": np.ascontiguousarray(inputs["L1_ffn_gate_bias"]),
        "s4_wg": np.ascontiguousarray(inputs["L1_ffn_w_gate"].reshape(EMOE * C, IEXP)),
        "s4_wu": np.ascontiguousarray(inputs["L1_ffn_w_up"].reshape(EMOE * C, IEXP)),
        "s4_wd": np.ascontiguousarray(inputs["L1_ffn_w_down"].reshape(EMOE * IEXP, C)),
        "s4_swg": np.ascontiguousarray(inputs["L1_ffn_sw_gate"]),
        "s4_swu": np.ascontiguousarray(inputs["L1_ffn_sw_up"]),
        "s4_swd": np.ascontiguousarray(inputs["L1_ffn_sw_down"]),
    })
    m.update({
        "s5_Wa": np.ascontiguousarray(inputs["L1_attn_Wa"]),
        "s5_Wx": np.ascontiguousarray(inputs["L1_attn_Wx"]),
        "s5_Wba": np.ascontiguousarray(inputs["L1_attn_ba"]),
        "s5_Wbx": np.ascontiguousarray(inputs["L1_attn_bx"]),
        "s5_Lam": np.ascontiguousarray(inputs["L1_attn_Lam"]),
    })
    return m


def kernel(**inputs):
    inputs = {k: np.asarray(v, dtype=np.float32) for k, v in inputs.items()}
    x = inputs["L0_attn_x"]

    for stages in SPLIT:
        nc = build_program(stages)
        in_maps = [_core_inmap(inputs, i, x) for i in range(NCORE)]
        res = run_bass_kernel_spmd(nc, in_maps, core_ids=list(range(NCORE)))
        parts = []
        for i in range(NCORE):
            xo = np.asarray(res.results[i]["x_out"])
            parts.append(xo.reshape(C, B, TL).transpose(1, 2, 0))
        x = np.concatenate(parts, axis=1)

    return np.ascontiguousarray(x)


# version 03 (final)
# speedup vs baseline: 1.2005x; 1.2005x over previous
import math
import numpy as np
import concourse.bass as bass
import concourse.mybir as mybir
from concourse.bass_utils import run_bass_kernel_spmd

F32 = mybir.dt.float32
F16 = mybir.dt.float16
I32 = mybir.dt.int32
U8 = mybir.dt.uint8
OP = mybir.AluOpType
AF = mybir.ActivationFunctionType
AX = mybir.AxisListType

NCORE = 8
B = 2
T = 8192
C = 1024
CH = 8
TOK = 2048
TL = 1024
NT = 4
TS = 512
IFFN = 2816
ICH = 22
CKDIM = 4096
CKCH = 32
EMOE = 8
IEXP = 512
LN_EPS = 1e-5
QMAX = 127.0
RCH = 8
RB = 8

ALL_STAGES = ["s1", "s2", "s3", "s4", "s5", "s6"]
S2_TRUNC = None
SPLIT = [ALL_STAGES]
DEBUG = False


class Sem:
    def __init__(self, handle):
        self.h = handle
        self.val = 0

    def inc(self, instr, n=1):
        instr.then_inc(self.h, n)
        self.val += n
        return instr


class Ctx:
    def __init__(self, nc, block):
        self.nc = nc
        self.block = block
        self.sems = {}
        self.sb = {}
        self.ps = {}
        self.dram = {}
        self._cms = {}
        self.ident = None

    def sem(self, name):
        if name not in self.sems:
            cm = self.nc.semaphore(name)
            self.sems[name] = Sem(cm.__enter__())
            self._cms["sem_" + name] = cm
        return self.sems[name]

    def sbuf(self, name, shape, dtype=F32):
        if name not in self.sb:
            cm = self.nc.sbuf_tensor(name, list(shape), dtype)
            t = cm.__enter__()
            self._cms["sb_" + name] = cm
            self.sb[name] = t.ap() if hasattr(t, "ap") else t
        return self.sb[name]

    def free_sbuf(self, *names):
        for name in names:
            if name in self.sb:
                self._cms.pop("sb_" + name).__exit__(None, None, None)
                del self.sb[name]

    def psum(self, name, shape, dtype=F32):
        if name not in self.ps:
            cm = self.nc.psum_tensor(name, list(shape), dtype)
            t = cm.__enter__()
            self._cms["ps_" + name] = cm
            self.ps[name] = t.ap() if hasattr(t, "ap") else t
        return self.ps[name]

    def dramt(self, name, shape, dtype=F32, kind="Internal"):
        if name not in self.dram:
            self.dram[name] = self.nc.dram_tensor(name, list(shape), dtype, kind=kind).ap()
        return self.dram[name]

    def tap(self, name, src_ap, shape=None, dtype=F32):
        if not DEBUG:
            return
        blk = self.block
        shape = list(shape if shape is not None else src_ap.shape)
        d = self.dramt("tap_" + name, shape, dtype, kind="ExternalOutput")
        s = self.sem("st")
        self.barrier()

        @blk.gpsimd
        def _(gp):
            s.inc(gp.dma_start(out=d[tuple(slice(0, n) for n in shape)], in_=src_ap), 16)
            gp.wait_ge(s.h, s.val)

        self.barrier()

    def barrier(self):
        blk = self.block
        f = self.sem("flow")
        engines = ["vector", "scalar", "tensor", "gpsimd", "sync"]
        base = f.val
        for e in engines:
            @getattr(blk, e)
            def _(eng, e=e):
                if e in ("vector", "scalar", "gpsimd"):
                    eng.drain()
                eng.nop().then_inc(f.h, 1)
        tgt = base + len(engines)
        f.val = tgt
        for e in engines:
            @getattr(blk, e)
            def _(eng):
                eng.wait_ge(f.h, tgt)


def XTiles(ctx):
    return [ctx.sbuf(f"X{c}", (128, TOK)) for c in range(CH)]


def ATiles(ctx):
    return [ctx.sbuf(f"A{c}", (128, TOK)) for c in range(CH)]


def emit_ln_stats(ctx, src):
    nc, blk = ctx.nc, ctx.block
    nsl = TOK // 128
    ones = ctx.sbuf("ones_col", (128, 1))
    sq = ctx.sbuf("SCR2", (128, TOK))
    stat = ctx.sbuf("ln_stat", (128, 64))
    ps_stat = ctx.psum("ps_stat", (128, 64))
    psT = ctx.psum("psT", (128, 128))
    statT = ctx.sbuf("ln_statT", (64, 128))
    rstd_b = ctx.sbuf("SCR0", (128, TOK))
    mrstd_b = ctx.sbuf("SCR1", (128, TOK))
    row_dram = ctx.dramt("ln_rows", (64 * 128,))
    s_dve = ctx.sem("dve")
    s_mm = ctx.sem("mm")
    s_act = ctx.sem("act")
    s_io = ctx.sem("io")

    ctx.barrier()
    dve_base, mm_base = s_dve.val, s_mm.val

    for c in range(CH):
        @blk.vector
        def _(vector, c=c):
            if c > 0:
                vector.wait_ge(s_mm.h, mm_base + c)
            s_dve.inc(vector.tensor_tensor(sq[:, :], src[c][:, :], src[c][:, :], OP.mult), 1)

        @blk.tensor
        def _(tensor, c=c):
            tensor.wait_ge(s_dve.h, dve_base + c + 1)
            for j in range(nsl):
                tensor.matmul(ps_stat[:, j:j + 1], src[c][:, 128 * j:128 * (j + 1)],
                              ones[:, 0:1], start=(c == 0 and j == 0), stop=False)
            for j in range(nsl):
                i = tensor.matmul(ps_stat[:, 32 + j:33 + j], sq[:, 128 * j:128 * (j + 1)],
                                  ones[:, 0:1], start=False,
                                  stop=(c == CH - 1 and j == nsl - 1))
            s_mm.inc(i, 1)

    @blk.vector
    def _(vector):
        vector.wait_ge(s_mm.h, mm_base + CH)
        vector.tensor_scalar(stat[:, 0:nsl], ps_stat[:, 0:nsl], 1.0 / C, None, OP.mult)
        vector.tensor_scalar(stat[:, 32:32 + nsl], ps_stat[:, 32:32 + nsl], 1.0 / C, None, OP.mult)
        vector.tensor_tensor(stat[:, 48:48 + nsl], stat[:, 0:nsl], stat[:, 0:nsl], OP.mult)
        vector.drain()
        vector.tensor_tensor(stat[:, 32:32 + nsl], stat[:, 32:32 + nsl], stat[:, 48:48 + nsl], OP.subtract)
        vector.drain()
        s_dve.inc(vector.tensor_scalar(stat[:, 32:32 + nsl], stat[:, 32:32 + nsl], 1.0,
                                       LN_EPS, OP.mult, OP.add), 1)

    @blk.scalar
    def _(scalar):
        scalar.wait_ge(s_dve.h, s_dve.val)
        scalar.activation(stat[:, 32:32 + nsl], stat[:, 32:32 + nsl], AF.Sqrt)
        scalar.drain()
        s_act.inc(scalar.copy(stat[:, 32:33], stat[:, 32:33]), 1)

    @blk.vector
    def _(vector):
        vector.wait_ge(s_act.h, s_act.val)
        vector.reciprocal(stat[:, 16:16 + nsl], stat[:, 32:32 + nsl])
        vector.drain()
        s_dve.inc(vector.tensor_tensor(stat[:, 0:nsl], stat[:, 0:nsl], stat[:, 16:16 + nsl], OP.mult), 1)

    @blk.tensor
    def _(tensor):
        tensor.wait_ge(s_dve.h, s_dve.val)
        s_mm.inc(tensor.transpose(psT[0:32, :], stat[:, 0:32], ctx.ident[:, :]), 1)

    @blk.vector
    def _(vector):
        vector.wait_ge(s_mm.h, s_mm.val)
        s_dve.inc(vector.tensor_copy(statT[0:32, :], psT[0:32, :]), 1)

    @blk.gpsimd
    def _(gp):
        gp.wait_ge(s_dve.h, s_dve.val)
        s_io.inc(gp.dma_start(out=row_dram[0:32 * 128], in_=statT[0:32, :]), 16)
        gp.wait_ge(s_io.h, s_io.val)
        s_io.inc(gp.dma_start(out=mrstd_b[:, :], in_=row_dram[0:TOK].unsqueeze(0).broadcast_to([128, TOK])), 16)
        s_io.inc(gp.dma_start(out=rstd_b[:, :], in_=row_dram[16 * 128:16 * 128 + TOK].unsqueeze(0).broadcast_to([128, TOK])), 16)

    @blk.vector
    def _(vector):
        vector.wait_ge(s_io.h, s_io.val)
        vector.nop()

    ctx.barrier()


def emit_ln_apply(ctx, vector, writes, src, gamma=None, beta=None):
    rstd_b = ctx.sb["SCR0"]
    mrstd_b = ctx.sb["SCR1"]
    for c in range(CH):
        for d_ap, s_sl in writes[c]:
            vector.tensor_tensor(d_ap, src[c][:, s_sl], rstd_b[:, s_sl], OP.mult)
    for c in range(CH):
        for d_ap, s_sl in writes[c]:
            i = vector.tensor_tensor(d_ap, d_ap, mrstd_b[:, s_sl], OP.subtract)
    if gamma is not None:
        for c in range(CH):
            for d_ap, s_sl in writes[c]:
                i = vector.tensor_scalar(d_ap, d_ap, gamma[:, c:c + 1],
                                         beta[:, c:c + 1], OP.mult, OP.add)
    return i


def emit_ln(ctx, src, dst, gamma=None, beta=None):
    emit_ln_stats(ctx, src)
    blk = ctx.block
    s_dve = ctx.sem("dve")

    @blk.vector
    def _(vector):
        writes = [[(dst[c][:, :], slice(0, TOK))] for c in range(CH)]
        s_dve.inc(emit_ln_apply(ctx, vector, writes, src, gamma, beta), 1)

    ctx.barrier()


def emit_absmax_allreduce(ctx, tiles, cc_tag):
    nc, blk = ctx.nc, ctx.block
    red = ctx.sbuf("red32", (128, 48))
    redT = ctx.sbuf("redT", (1, 128))
    psT = ctx.psum("psT", (128, 128))
    cc_in = ctx.dramt(f"cc_{cc_tag}_in", (8,))
    cc_out = ctx.dramt(f"cc_{cc_tag}_out", (8,))
    s_dve = ctx.sem("dve")
    s_mm = ctx.sem("mm")
    s_io = ctx.sem("io")
    s_cc = ctx.sem("cc")

    ctx.barrier()

    @blk.vector
    def _(vector):
        n = len(tiles)
        for i, t in enumerate(tiles):
            vector.tensor_reduce(red[:, i:i + 1], t, AX.X, OP.max, apply_absolute_value=True)
        step = 1
        while step < n:
            vector.drain()
            for i in range(0, n - step, 2 * step):
                vector.tensor_tensor(red[:, i:i + 1], red[:, i:i + 1],
                                     red[:, i + step:i + step + 1], OP.max)
            step *= 2
        vector.drain()
        s_dve.inc(vector.nop(), 1)

    @blk.tensor
    def _(tensor):
        tensor.wait_ge(s_dve.h, s_dve.val)
        s_mm.inc(tensor.transpose(psT[0:1, :], red[:, 0:1], ctx.ident[:, :]), 1)

    @blk.vector
    def _(vector):
        vector.wait_ge(s_mm.h, s_mm.val)
        vector.tensor_copy(redT[0:1, :], psT[0:1, :])
        vector.drain()
        s_dve.inc(vector.tensor_reduce(redT[0:1, 0:1], redT[0:1, :], AX.X, OP.max), 1)

    @blk.gpsimd
    def _(gp):
        gp.wait_ge(s_dve.h, s_dve.val)
        s_io.inc(gp.dma_start(out=cc_in[0:1], in_=redT[0:1, 0:1]), 16)
        gp.wait_ge(s_io.h, s_io.val)
        s_cc.inc(gp.collective_compute(
            "AllReduce", OP.max, replica_groups=[list(range(NCORE))],
            ins=[cc_in[0:1]], outs=[cc_out[0:1]]), 1)
        gp.wait_ge(s_cc.h, s_cc.val)

    ctx.barrier()


def emit_quant_scales(ctx, cc_tag, clip_tile, prefix):
    nc, blk = ctx.nc, ctx.block
    sc = ctx.sbuf(f"{prefix}_tmp", (1, 4))
    inv_b = ctx.sbuf(f"{prefix}_inv", (128, 1))
    s_b = ctx.sbuf(f"{prefix}_s", (128, 1))
    sc_dram = ctx.dramt(f"{prefix}_dram", (4,))
    s_io = ctx.sem("io")
    s_dve = ctx.sem("dve")
    s_act = ctx.sem("act")

    @blk.gpsimd
    def _(gp):
        s_io.inc(gp.dma_start(out=sc[0:1, 0:1], in_=ctx.dram[f"cc_{cc_tag}_out"][0:1]), 16)

    @blk.scalar
    def _(scalar):
        scalar.wait_ge(s_io.h, s_io.val)
        scalar.activation(sc[0:1, 1:2], clip_tile, AF.Sigmoid)
        scalar.drain()
        s_act.inc(scalar.copy(sc[0:1, 1:2], sc[0:1, 1:2]), 1)

    @blk.vector
    def _(vector):
        vector.wait_ge(s_act.h, s_act.val)
        vector.tensor_tensor(sc[0:1, 0:1], sc[0:1, 0:1], sc[0:1, 1:2], OP.mult)
        vector.drain()
        vector.tensor_scalar(sc[0:1, 0:1], sc[0:1, 0:1], 1.0 / QMAX, None, OP.mult)
        vector.drain()
        s_dve.inc(vector.reciprocal(sc[0:1, 1:2], sc[0:1, 0:1]), 1)

    @blk.gpsimd
    def _(gp):
        gp.wait_ge(s_dve.h, s_dve.val)
        s_io.inc(gp.dma_start(out=sc_dram[0:2], in_=sc[0:1, 0:2]), 16)
        gp.wait_ge(s_io.h, s_io.val)
        s_io.inc(gp.dma_start(out=s_b[:, 0:1], in_=sc_dram[0:1].unsqueeze(0).broadcast_to([128, 1])), 16)
        s_io.inc(gp.dma_start(out=inv_b[:, 0:1], in_=sc_dram[1:2].unsqueeze(0).broadcast_to([128, 1])), 16)
        gp.wait_ge(s_io.h, s_io.val)

    ctx.barrier()


def round_ops(vector, dst, src, inv_ap, qi, qf, qg, drains=True):
    w = src.shape[-1]
    D = (lambda: vector.drain()) if drains else (lambda: None)
    vector.tensor_scalar(qf[:, 0:w], src, inv_ap, None, OP.mult)
    D()
    vector.tensor_copy(qi[:, 0:w], qf[:, 0:w])
    D()
    vector.tensor_tensor(qg[:, 0:w], qf[:, 0:w], qi[:, 0:w], OP.subtract)
    D()
    vector.tensor_scalar(qf[:, 0:w], qg[:, 0:w], 0.5, None, OP.is_gt)
    vector.tensor_scalar(qg[:, 0:w], qg[:, 0:w], -0.5, None, OP.is_lt)
    D()
    vector.tensor_tensor(qf[:, 0:w], qf[:, 0:w], qi[:, 0:w], OP.add)
    D()
    vector.tensor_tensor(qf[:, 0:w], qf[:, 0:w], qg[:, 0:w], OP.subtract)
    D()
    return vector.tensor_scalar(dst, qf[:, 0:w], -128.0, 127.0, OP.max, OP.min)


def quant_ops(vector, dst, src, inv_ap, qi, qf, qg):
    return round_ops(vector, dst, src, inv_ap, qi, qf, qg, drains=True)


def quant2_ops(vector, dst_a, src_a, dst_b, src_b, inv_ap, qia, qfa, qga, qib, qfb, qgb):
    wa = src_a.shape[-1]
    wb = src_b.shape[-1]
    vector.tensor_scalar(qfa[:, 0:wa], src_a, inv_ap, None, OP.mult)
    vector.tensor_scalar(qfb[:, 0:wb], src_b, inv_ap, None, OP.mult)
    vector.tensor_copy(qia[:, 0:wa], qfa[:, 0:wa])
    vector.tensor_copy(qib[:, 0:wb], qfb[:, 0:wb])
    vector.tensor_tensor(qga[:, 0:wa], qfa[:, 0:wa], qia[:, 0:wa], OP.subtract)
    vector.tensor_tensor(qgb[:, 0:wb], qfb[:, 0:wb], qib[:, 0:wb], OP.subtract)
    vector.tensor_scalar(qfa[:, 0:wa], qga[:, 0:wa], 0.5, None, OP.is_gt)
    vector.tensor_scalar(qfb[:, 0:wb], qgb[:, 0:wb], 0.5, None, OP.is_gt)
    vector.tensor_scalar(qga[:, 0:wa], qga[:, 0:wa], -0.5, None, OP.is_lt)
    vector.tensor_scalar(qgb[:, 0:wb], qgb[:, 0:wb], -0.5, None, OP.is_lt)
    vector.tensor_tensor(qfa[:, 0:wa], qfa[:, 0:wa], qia[:, 0:wa], OP.add)
    vector.tensor_tensor(qfb[:, 0:wb], qfb[:, 0:wb], qib[:, 0:wb], OP.add)
    vector.tensor_tensor(qfa[:, 0:wa], qfa[:, 0:wa], qga[:, 0:wa], OP.subtract)
    vector.tensor_tensor(qfb[:, 0:wb], qfb[:, 0:wb], qgb[:, 0:wb], OP.subtract)
    vector.tensor_scalar(dst_a, qfa[:, 0:wa], -128.0, 127.0, OP.max, OP.min)
    return vector.tensor_scalar(dst_b, qfb[:, 0:wb], -128.0, 127.0, OP.max, OP.min)


def emit_ffn0_weight_quant(ctx):
    nc, blk = ctx.nc, ctx.block
    s_io = ctx.sem("io")
    s_dve = ctx.sem("dve")
    s_st = ctx.sem("st")
    AB = ctx.sbuf("ABUF", (128, 16384))
    wa = AB[:, 0:2816]
    wb = AB[:, 2816:5632]
    inv_t = AB[:, 5632:8448]
    qi = AB.bitcast(I32)[:, 8448:11264]
    wg_ = AB[:, 11264:14080]
    wb16 = AB.bitcast(F16)[:, 28160:30976]

    ctx.barrier()
    specs = (
        (IFFN, C, "wg_a", "wg_b", "wq_gate", "wsc_g"),
        (IFFN, C, "wu_a", "wu_b", "wq_up", "wsc_u"),
        (C, IFFN, "wd_a", "wd_b", "wq_down", "wsc_d"),
    )
    for rows, cols, a_in, b_in, q_out, sct in specs:
        a = ctx.dram[a_in]
        b = ctx.dram[b_in]
        q = ctx.dramt(q_out, (cols, rows), F16)
        scd = ctx.dramt(f"{q_out}_sc", (rows,))
        sc_tile = ctx.sbuf(sct, (128, rows // 128))
        np_r = rows // 128

        for g in range(np_r):
            @blk.sync
            def _(sync, g=g, a=a, cols=cols):
                sync.wait_ge(s_dve.h, s_dve.val)
                s_io.inc(sync.dma_start(out=wa[:, 0:cols], in_=a[128 * g:128 * (g + 1), 0:cols]), 16)

            @blk.vector
            def _(vector, g=g, cols=cols, sc_tile=sc_tile):
                vector.wait_ge(s_io.h, s_io.val)
                vector.tensor_reduce(sc_tile[:, g:g + 1], wa[:, 0:cols], AX.X, OP.max,
                                     apply_absolute_value=True)
                vector.drain()
                s_dve.inc(vector.tensor_scalar(sc_tile[:, g:g + 1], sc_tile[:, g:g + 1],
                                               1.0 / QMAX, None, OP.mult), 1)

        @blk.gpsimd
        def _(gp, scd=scd, np_r=np_r, sc_tile=sc_tile):
            gp.wait_ge(s_dve.h, s_dve.val)
            for g in range(np_r):
                s_st.inc(gp.dma_start(out=scd[128 * g:128 * (g + 1)], in_=sc_tile[:, g:g + 1]), 16)
            gp.wait_ge(s_st.h, s_st.val)

        ctx.barrier()

        @blk.gpsimd
        def _(gp, scd=scd, rows=rows):
            s_io.inc(gp.dma_start(out=inv_t[:, 0:rows],
                                  in_=scd[0:rows].unsqueeze(0).broadcast_to([128, rows])), 16)

        @blk.vector
        def _(vector, rows=rows):
            vector.wait_ge(s_io.h, s_io.val)
            s_dve.inc(vector.reciprocal(inv_t[:, 0:rows], inv_t[:, 0:rows]), 1)
            vector.drain()

        for ct in range(cols // 128):
            @blk.sync
            def _(sync, ct=ct, b=b, rows=rows):
                sync.wait_ge(s_dve.h, s_dve.val)
                s_io.inc(sync.dma_start(out=wb[:, 0:rows], in_=b[128 * ct:128 * (ct + 1), 0:rows]), 16)

            @blk.vector
            def _(vector, rows=rows):
                vector.wait_ge(s_io.h, s_io.val)
                vector.wait_ge(s_st.h, s_st.val)
                vector.tensor_tensor(wb[:, 0:rows], wb[:, 0:rows], inv_t[:, 0:rows], OP.mult)
                vector.drain()
                vector.tensor_copy(qi[:, 0:rows], wb[:, 0:rows])
                vector.drain()
                vector.tensor_tensor(wg_[:, 0:rows], wb[:, 0:rows], qi[:, 0:rows], OP.subtract)
                vector.drain()
                vector.tensor_scalar(wb[:, 0:rows], wg_[:, 0:rows], 0.5, None, OP.is_gt)
                vector.tensor_scalar(wg_[:, 0:rows], wg_[:, 0:rows], -0.5, None, OP.is_lt)
                vector.drain()
                vector.tensor_tensor(wb[:, 0:rows], wb[:, 0:rows], qi[:, 0:rows], OP.add)
                vector.drain()
                vector.tensor_tensor(wb[:, 0:rows], wb[:, 0:rows], wg_[:, 0:rows], OP.subtract)
                vector.drain()
                s_dve.inc(vector.tensor_scalar(wb16[:, 0:rows], wb[:, 0:rows], -128.0, 127.0,
                                               OP.max, OP.min), 1)

            @blk.gpsimd
            def _(gp, ct=ct, q=q, rows=rows):
                gp.wait_ge(s_dve.h, s_dve.val)
                s_st.inc(gp.dma_start(out=q[128 * ct:128 * (ct + 1), 0:rows], in_=wb16[:, 0:rows]), 16)

    @blk.gpsimd
    def _(gp):
        gp.wait_ge(s_st.h, s_st.val)

    ctx.barrier()


def emit_ffn0(ctx, stage):
    nc, blk = ctx.nc, ctx.block
    X = XTiles(ctx)
    AB = ctx.sbuf("ABUF", (128, 16384))
    Y = [AB[:, 2048 * c:2048 * (c + 1)] for c in range(CH)]
    emit_ln(ctx, X, Y)
    ctx.tap(f"{stage}_ln0", Y[0])
    ctx.tap(f"{stage}_rstdb", ctx.sb["SCR0"][:, :])
    ctx.tap(f"{stage}_mrstdb", ctx.sb["SCR1"][:, :])

    emit_absmax_allreduce(ctx, [y for y in Y], f"{stage}x")
    clip_x = ctx.sbuf("clip_x_t", (1, 2))
    emit_quant_scales(ctx, f"{stage}x", clip_x[0:1, 0:1], f"{stage}qx")

    s_dve = ctx.sem("dve")
    s_io = ctx.sem("io")
    s_mm = ctx.sem("mm")
    s_act = ctx.sem("act")
    s_st = ctx.sem("st")
    SCR0a = ctx.sbuf("SCR0", (128, TOK))
    SCR1a = ctx.sbuf("SCR1", (128, TOK))
    qi = SCR0a.bitcast(I32)[:, 0:TS]
    qf = SCR0a[:, TS:2 * TS]
    qg = SCR0a[:, 2 * TS:3 * TS]
    qib = SCR0a.bitcast(I32)[:, 3 * TS:4 * TS]
    qfb = SCR1a[:, 0:TS]
    qgb = SCR1a[:, TS:2 * TS]
    inv_x = ctx.sb[f"{stage}qx_inv"]
    sx = ctx.sb[f"{stage}qx_s"]
    XQ = [ctx.sbuf(f"XQ{c}", (128, TOK), F16) for c in range(CH)]

    @blk.vector
    def _(vector):
        subs = [(c, nt) for c in range(CH) for nt in range(NT)]
        for j in range(0, len(subs), 2):
            (ca, na), (cb, nb) = subs[j], subs[j + 1]
            sa = slice(TS * na, TS * (na + 1))
            sb = slice(TS * nb, TS * (nb + 1))
            i = quant2_ops(vector, XQ[ca][:, sa], Y[ca][:, sa],
                           XQ[cb][:, sb], Y[cb][:, sb], inv_x[:, 0:1],
                           qi, qf, qg, qib, qfb, qgb)
        s_dve.inc(i, 1)

    wsxg = ctx.sbuf("wsxg", (128, ICH))
    wsxu = ctx.sbuf("wsxu", (128, ICH))
    wsxd = ctx.sbuf("wsxd", (128, CH))

    @blk.vector
    def _(vector):
        vector.tensor_scalar(wsxg[:, :], ctx.sb["wsc_g"][:, :], sx[:, 0:1], None, OP.mult)
        s_dve.inc(vector.tensor_scalar(wsxu[:, :], ctx.sb["wsc_u"][:, :], sx[:, 0:1], None, OP.mult), 1)

    ctx.barrier()
    ctx.tap(f"{stage}_sc", ctx.sb[f"{stage}qx_tmp"][0:1, 0:2], (1, 2))
    ctx.tap(f"{stage}_xq0", XQ[0][:, :], (128, TOK), F16)
    ctx.tap(f"{stage}_wsxg", wsxg[:, :])

    wq_g = ctx.dram["wq_gate"]
    wq_u = ctx.dram["wq_up"]
    h_spill = ctx.dramt("h_spill", (IFFN, TOK))
    wt3s = ctx.sbuf("wt3", (128, CH * 128))
    wtg = wt3s.bitcast(F16)[:, 0:CH * 128]
    wtu = wt3s.bitcast(F16)[:, CH * 128:2 * CH * 128]
    go = ctx.sbuf("SCR0", (128, TOK))
    uo = ctx.sbuf("SCR1", (128, TOK))
    ho = ctx.sbuf("SCR2", (128, TOK))
    hmax = ctx.sbuf("ffn_hmax", (128, 68))
    psN = [ctx.psum(f"ps{i}", (128, TS)) for i in range(4)]

    for o in range(ICH):
        @blk.sync
        def _(sync, o=o):
            if o > 0:
                sync.wait_ge(s_mm.h, s_mm.val)
            for k in range(CH):
                s_io.inc(sync.dma_start(out=wtg[:, 128 * k:128 * (k + 1)],
                                        in_=wq_g[128 * k:128 * (k + 1), 128 * o:128 * (o + 1)]), 16)
                s_io.inc(sync.dma_start(out=wtu[:, 128 * k:128 * (k + 1)],
                                        in_=wq_u[128 * k:128 * (k + 1), 128 * o:128 * (o + 1)]), 16)

        @blk.tensor
        def _(tensor, o=o):
            tensor.wait_ge(s_io.h, s_io.val)
            tensor.wait_ge(s_dve.h, s_dve.val)
            for nt in range(NT):
                for k in range(CH):
                    i = tensor.matmul(psN[nt][:, :], wtg[:, 128 * k:128 * (k + 1)],
                                      XQ[k][:, TS * nt:TS * (nt + 1)],
                                      start=(k == 0), stop=(k == CH - 1))
            s_mm.inc(i, 1)

        @blk.vector
        def _(vector, o=o):
            vector.wait_ge(s_mm.h, s_mm.val)
            vector.wait_ge(s_st.h, s_st.val)
            for nt in range(NT):
                i = vector.tensor_scalar(go[:, TS * nt:TS * (nt + 1)], psN[nt][:, :],
                                         wsxg[:, o:o + 1], None, OP.mult)
            s_dve.inc(i, 1)

        @blk.tensor
        def _(tensor, o=o):
            tensor.wait_ge(s_dve.h, s_dve.val)
            for nt in range(NT):
                for k in range(CH):
                    i = tensor.matmul(psN[nt][:, :], wtu[:, 128 * k:128 * (k + 1)],
                                      XQ[k][:, TS * nt:TS * (nt + 1)],
                                      start=(k == 0), stop=(k == CH - 1))
            s_mm.inc(i, 1)

        @blk.scalar
        def _(scalar, o=o):
            scalar.wait_ge(s_dve.h, s_dve.val)
            scalar.activation(go[:, :], go[:, :], AF.Silu)
            scalar.drain()
            s_act.inc(scalar.copy(go[:, 0:1], go[:, 0:1]), 1)

        @blk.vector
        def _(vector, o=o):
            vector.wait_ge(s_mm.h, s_mm.val)
            for nt in range(NT):
                vector.tensor_scalar(uo[:, TS * nt:TS * (nt + 1)], psN[nt][:, :],
                                     wsxu[:, o:o + 1], None, OP.mult)
            vector.wait_ge(s_act.h, s_act.val)
            vector.wait_ge(s_st.h, s_st.val)
            for nt in range(NT):
                sl = slice(TS * nt, TS * (nt + 1))
                vector.tensor_tensor(ho[:, sl], go[:, sl], uo[:, sl], OP.mult)
            vector.tensor_reduce(hmax[:, 2 * o:2 * o + 1], ho[:, 0:TOK // 2], AX.X,
                                 OP.max, apply_absolute_value=True)
            i = vector.tensor_reduce(hmax[:, 2 * o + 1:2 * o + 2], ho[:, TOK // 2:TOK], AX.X,
                                     OP.max, apply_absolute_value=True)
            s_dve.inc(i, 1)

        @blk.gpsimd
        def _(gp, o=o):
            gp.wait_ge(s_dve.h, s_dve.val)
            s_st.inc(gp.dma_start(out=h_spill[128 * o:128 * (o + 1), :], in_=ho[:, :]), 16)

    ctx.barrier()
    ctx.tap(f"{stage}_h21", ho[:, :])
    ctx.tap(f"{stage}_hmax", hmax[:, 0:2 * ICH], (128, 2 * ICH))

    emit_absmax_allreduce(ctx, [hmax[:, i:i + 1] for i in range(2 * ICH)], f"{stage}h")
    clip_d = ctx.sbuf("clip_d_t", (1, 2))
    emit_quant_scales(ctx, f"{stage}h", clip_d[0:1, 0:1], f"{stage}qh")
    inv_h = ctx.sb[f"{stage}qh_inv"]
    sh = ctx.sb[f"{stage}qh_s"]

    @blk.vector
    def _(vector):
        vector.tensor_scalar(wsxd[:, :], ctx.sb["wsc_d"][:, :], sh[:, 0:1], None, OP.mult)
        vector.drain()
        s_dve.inc(vector.tensor_scalar(wsxd[:, :], wsxd[:, :], 0.5, None, OP.mult), 1)

    ctx.barrier()

    wq_d = ctx.dram["wq_down"]
    HQ = AB.bitcast(F16)[:, 0:ICH * TS]
    wtd = AB.bitcast(F16)[:, 11264:11264 + ICH * 128]
    hbuf = AB[:, 7040:7552]
    dq = AB[:, 7552:8064]

    for nt in range(NT):
        sl = slice(TS * nt, TS * (nt + 1))

        for kk in range(ICH):
            @blk.sync
            def _(sync, kk=kk, nt=nt, sl=sl):
                sync.wait_ge(s_dve.h, s_dve.val)
                s_io.inc(sync.dma_start(out=hbuf, in_=h_spill[128 * kk:128 * (kk + 1), sl]), 16)

            @blk.vector
            def _(vector, kk=kk):
                vector.wait_ge(s_io.h, s_io.val)
                vector.wait_ge(s_mm.h, s_mm.val)
                s_dve.inc(quant_ops(vector, HQ[:, TS * kk:TS * (kk + 1)], hbuf,
                                    inv_h[:, 0:1], qi, qf, qg), 1)
                vector.drain()

        for oo in range(CH):
            @blk.sync
            def _(sync, oo=oo):
                sync.wait_ge(s_mm.h, s_mm.val)
                for kk in range(ICH):
                    s_io.inc(sync.dma_start(out=wtd[:, 128 * kk:128 * (kk + 1)],
                                            in_=wq_d[128 * kk:128 * (kk + 1), 128 * oo:128 * (oo + 1)]), 16)

            @blk.tensor
            def _(tensor, oo=oo):
                tensor.wait_ge(s_io.h, s_io.val)
                tensor.wait_ge(s_dve.h, s_dve.val)
                for kk in range(ICH):
                    i = tensor.matmul(psN[0][:, :], wtd[:, 128 * kk:128 * (kk + 1)],
                                      HQ[:, TS * kk:TS * (kk + 1)],
                                      start=(kk == 0), stop=(kk == ICH - 1))
                s_mm.inc(i, 1)

            @blk.vector
            def _(vector, oo=oo, sl=sl):
                vector.wait_ge(s_mm.h, s_mm.val)
                vector.tensor_scalar(dq, psN[0][:, :], wsxd[:, oo:oo + 1], None, OP.mult)
                vector.drain()
                s_dve.inc(vector.tensor_tensor(X[oo][:, sl], X[oo][:, sl], dq, OP.add), 1)

    ctx.barrier()




def _switch_pick(ctx, gp, dst_ap, blocks, ts_dram_ap, sem, n_dma=1):
    base = sem.val
    for arm in gp.switch_core_id(n=128):
        r = arm.logical % NCORE
        if r == 0:
            sem.inc(gp.dma_start(out=dst_ap, in_=ts_dram_ap), 16)
        else:
            sem.inc(gp.dma_start(out=dst_ap, in_=blocks(r - 1)), 16)
    sem.val = base + 16


def emit_hs_tile(ctx, vector, hs_tiles, srcY, b, tti, bnd_tile, bnd_cols, gamma, beta):
    base = b * TL + tti * TS
    if tti == 0:
        writes = [[(hs_tiles[c][:, 1:TS], slice(base, base + TS - 1))] for c in range(CH)]
    else:
        writes = [[(hs_tiles[c][:, 0:TS], slice(base - 1, base + TS - 1))] for c in range(CH)]
    i = emit_ln_apply(ctx, vector, writes, srcY, gamma, beta)
    if tti == 0:
        for c in range(CH):
            i = vector.tensor_copy(hs_tiles[c][:, 0:1], bnd_tile[:, bnd_cols(c):bnd_cols(c) + 1])
    return i


def emit_hs_tile_sp(ctx, vector, hs_tiles, stripes, b, tti, bnd_tile, bnd_cols, gamma, beta):
    rstd_b = ctx.sb["SCR0"]
    mrstd_b = ctx.sb["SCR1"]
    base = b * TL + tti * TS
    a = 1 if tti == 0 else 0
    w = TS - a
    s0 = base - 1 + a
    for c in range(CH):
        vector.tensor_tensor(hs_tiles[c][:, a:TS], stripes[c][:, a:TS],
                             rstd_b[:, s0:s0 + w], OP.mult)
    for c in range(CH):
        vector.tensor_tensor(hs_tiles[c][:, a:TS], hs_tiles[c][:, a:TS],
                             mrstd_b[:, s0:s0 + w], OP.subtract)
    for c in range(CH):
        i = vector.tensor_scalar(hs_tiles[c][:, a:TS], hs_tiles[c][:, a:TS],
                                 gamma[:, c:c + 1], beta[:, c:c + 1], OP.mult, OP.add)
    if tti == 0:
        for c in range(CH):
            i = vector.tensor_copy(hs_tiles[c][:, 0:1], bnd_tile[:, bnd_cols(c):bnd_cols(c) + 1])
    return i


def emit_s2(ctx, stage):
    nc, blk = ctx.nc, ctx.block
    X = XTiles(ctx)
    AB = ctx.sbuf("ABUF", (128, 16384))
    Y = [AB[:, 2048 * c:2048 * (c + 1)] for c in range(CH)]
    SCR0 = ctx.sbuf("SCR0", (128, TOK))
    SCR1 = ctx.sbuf("SCR1", (128, TOK))
    SCR2 = ctx.sbuf("SCR2", (128, TOK))
    XQ = [ctx.sbuf(f"XQ{c}", (128, TOK), F16) for c in range(CH)]
    XQf = [XQ[c].bitcast(F32) for c in range(CH)]
    s_dve = ctx.sem("dve")
    s_mm = ctx.sem("mm")
    s_act = ctx.sem("act")
    s_io = ctx.sem("io")
    s_st = ctx.sem("st")
    s_cc = ctx.sem("cc")

    ln1g = ctx.sbuf("ln1g", (128, CH))
    ln1b = ctx.sbuf("ln1b", (128, CH))
    ln2g = ctx.sbuf("ln2g", (128, CH))
    ln2b = ctx.sbuf("ln2b", (128, CH))
    dmy2 = ctx.sbuf("dmy2", (128, 2))
    wkv_w = ctx.sbuf("wkv_w_t", (128, 1))
    wkv_u = ctx.sbuf("wkv_u_t", (128, 1))
    eu_t = ctx.sbuf("eu_t", (128, 1))
    tsv = ctx.sbuf("tsv", (128, 16))

    cc_k_in = ctx.dramt("cc_k_in", (C, TOK))
    cc_vloc = ctx.dramt("cc_vloc", (C, TOK))
    cc_k_out = ctx.dramt("cc_k_out", (NCORE, C, TOK))
    cc_vgat = ctx.dramt("cc_vgat", (NCORE, C, TOK))
    cc_y_in = ctx.dramt("cc_y_in", (128, T * B // 1))
    cc_y_out = ctx.dramt("cc_y_out", (NCORE, 128, T * B))
    cc_hl_in = ctx.dramt("cc_hl_in", (TOK,))
    cc_hl_out = ctx.dramt("cc_hl_out", (NCORE, TOK))
    cc_h2l_in = ctx.dramt("cc_h2l_in", (TOK,))
    cc_h2l_out = ctx.dramt("cc_h2l_out", (NCORE, TOK))
    sigr_sp = ctx.dramt("sigr_sp", (C, TOK))
    x_sp = ctx.dramt("x_sp", (C, TOK))

    @blk.sync
    def _(sync):
        io = ctx.dram
        for nm, t in (("ln1_g", ln1g), ("ln1_b", ln1b), ("ln2_g", ln2g), ("ln2_b", ln2b)):
            for c in range(CH):
                s_io.inc(sync.dma_start(out=t[:, c:c + 1], in_=io[nm][128 * c:128 * (c + 1)]), 16)
        s_io.inc(sync.dma_start(out=wkv_w[:, 0:1], in_=io["wkv_w"][0:128]), 16)
        s_io.inc(sync.dma_start(out=wkv_u[:, 0:1], in_=io["wkv_u"][0:128]), 16)
        s_io.inc(sync.dma_start(out=tsv[:, :], in_=io["ts_pack"][0:TOK]), 16)

    @blk.scalar
    def _(scalar):
        scalar.wait_ge(s_io.h, s_io.val)
        scalar.activation(eu_t[:, 0:1], wkv_u[:, 0:1], AF.Exp)
        scalar.drain()
        s_act.inc(scalar.copy(eu_t[:, 0:1], eu_t[:, 0:1]), 1)

    ctx.barrier()
    if S2_TRUNC == 11:
        return

    emit_ln(ctx, X, Y)
    if S2_TRUNC == 12:
        return

    @blk.gpsimd
    def _(gp):
        for c in range(CH):
            s_st.inc(gp.dma_start(out=x_sp[128 * c:128 * (c + 1), :], in_=X[c][:, :]), 16)
        gp.wait_ge(s_st.h, s_st.val)

    emit_ln_stats(ctx, Y)
    if S2_TRUNC == 13:
        return
    hl_pack = ctx.sbuf("hl_pack", (128, 16))

    @blk.vector
    def _(vector):
        writes = [[(hl_pack[:, 2 * c:2 * c + 1], slice(TL - 1, TL)),
                   (hl_pack[:, 2 * c + 1:2 * c + 2], slice(TOK - 1, TOK))] for c in range(CH)]
        s_dve.inc(emit_ln_apply(ctx, vector, writes, Y, ln1g, ln1b), 1)

    if S2_TRUNC == 14:
        ctx.barrier()
        return

    hl_dbg = ctx.dramt("hl_dbg", (128, 16))

    @blk.gpsimd
    def _(gp):
        gp.wait_ge(s_dve.h, s_dve.val)
        if S2_TRUNC == 151:
            pass
        elif S2_TRUNC == 152:
            s_st.inc(gp.dma_start(out=hl_dbg[:, :], in_=hl_pack[:, :]), 16)
            gp.wait_ge(s_st.h, s_st.val)
        else:
            s_st.inc(gp.dma_start(out=cc_hl_in[:], in_=hl_pack[:, :]), 16)
            gp.wait_ge(s_st.h, s_st.val)
        if S2_TRUNC not in (15, 151, 152):
            s_cc.inc(gp.collective_compute(
                "AllGather", OP.bypass, replica_groups=[list(range(NCORE))],
                ins=[cc_hl_in[:]], outs=[cc_hl_out[:, :]]), 1)
            gp.wait_ge(s_cc.h, s_cc.val)
        if S2_TRUNC not in (15, 151, 152, 16):
            _switch_pick(ctx, gp, hl_pack[:, :], lambda r: cc_hl_out[r, :],
                         ctx.dram["ts_pack"][0:TOK], s_io)
            gp.wait_ge(s_io.h, s_io.val)

    ctx.barrier()
    ctx.tap("s2_hl", hl_pack[:, :])
    if S2_TRUNC in (1, 151, 152, 15, 16):
        return

    hsT = [XQf[c][:, 0:TS] for c in range(CH)]
    wt3 = ctx.sbuf("wt3", (128, CH * 128))
    evk = SCR2[:, 0:TS]
    evr = SCR2[:, TS:2 * TS]
    psN = [ctx.psum(f"ps{i}", (128, TS)) for i in range(4)]

    for b in range(B):
        for tti in range(2):
            @blk.vector
            def _(vector, b=b, tti=tti):
                vector.wait_ge(s_mm.h, s_mm.val)
                i = emit_hs_tile(ctx, vector, hsT, Y, b, tti, hl_pack,
                                 (lambda c, b=b: 2 * c + b), ln1g, ln1b)
                s_dve.inc(i, 1)

            col0 = b * TL + tti * TS
            for tau, (wnm, dst) in enumerate((("Wk", cc_k_in), ("Wv", cc_vloc), ("Wr", None))):
                for oo in range(CH):
                    @blk.sync
                    def _(sync, wnm=wnm, oo=oo):
                        sync.wait_ge(s_mm.h, s_mm.val)
                        for kk in range(CH):
                            s_io.inc(sync.dma_start(
                                out=wt3[:, 128 * kk:128 * (kk + 1)],
                                in_=ctx.dram[wnm][128 * kk:128 * (kk + 1), 128 * oo:128 * (oo + 1)]), 16)

                    @blk.tensor
                    def _(tensor, oo=oo, tau=tau):
                        tensor.wait_ge(s_io.h, s_io.val)
                        tensor.wait_ge(s_dve.h, s_dve.val)
                        if tau == 0 and oo == 0:
                            pass
                        tensor.wait_ge(s_act.h, s_act.val)
                        for kk in range(CH):
                            i = tensor.matmul(psN[0][:, :], wt3[:, 128 * kk:128 * (kk + 1)],
                                              hsT[kk][:, :], start=(kk == 0), stop=(kk == CH - 1))
                        s_mm.inc(i, 1)

                    if wnm == "Wr":
                        @blk.scalar
                        def _(scalar, oo=oo):
                            scalar.wait_ge(s_mm.h, s_mm.val)
                            scalar.wait_ge(s_st.h, s_st.val)
                            scalar.activation(evr, psN[0][:, :], AF.Sigmoid)
                            scalar.drain()
                            s_act.inc(scalar.copy(evr[:, 0:1], evr[:, 0:1]), 1)

                        @blk.gpsimd
                        def _(gp, oo=oo, col0=col0):
                            gp.wait_ge(s_act.h, s_act.val)
                            s_st.inc(gp.dma_start(
                                out=sigr_sp[128 * oo:128 * (oo + 1), col0:col0 + TS], in_=evr), 16)
                    else:
                        @blk.scalar
                        def _(scalar, oo=oo):
                            scalar.wait_ge(s_mm.h, s_mm.val)
                            scalar.wait_ge(s_st.h, s_st.val)
                            scalar.copy(evk, psN[0][:, :])
                            scalar.drain()
                            s_act.inc(scalar.copy(evk[:, 0:1], evk[:, 0:1]), 1)

                        @blk.gpsimd
                        def _(gp, oo=oo, col0=col0, dst=dst):
                            gp.wait_ge(s_act.h, s_act.val)
                            s_st.inc(gp.dma_start(
                                out=dst[128 * oo:128 * (oo + 1), col0:col0 + TS], in_=evk), 16)

    @blk.gpsimd
    def _(gp):
        gp.wait_ge(s_st.h, s_st.val)
        s_cc.inc(gp.collective_compute(
            "AllGather", OP.bypass, replica_groups=[list(range(NCORE))],
            ins=[cc_k_in[:, :]], outs=[cc_k_out[:, :, :]]), 1)
        s_cc.inc(gp.collective_compute(
            "AllGather", OP.bypass, replica_groups=[list(range(NCORE))],
            ins=[cc_vloc[:, :]], outs=[cc_vgat[:, :, :]]), 1)
        gp.wait_ge(s_cc.h, s_cc.val)

    ctx.barrier()
    ctx.tap("s2_k", ctx.dram["cc_k_out"][0, :, :])
    ctx.tap("s2_v", ctx.dram["cc_vgat"][0, :, :])
    ctx.tap("s2_sigr", sigr_sp[:, :])
    if S2_TRUNC == 2:
        return

    emit_wkv(ctx)
    ctx.tap("s2_y", ctx.dram["cc_y_out"][0, :, :])
    if S2_TRUNC == 3:
        return

    ywkv_t = SCR2[:, 0:TOK]

    for c in range(CH):
        @blk.gpsimd
        def _(gp, c=c):
            gp.wait_ge(s_dve.h, s_dve.val)
            io_base = s_io.val
            for arm in gp.switch_core_id(n=128):
                r = arm.logical % NCORE
                s_io.inc(gp.dma_start(out=ywkv_t,
                                      in_=ctx.dram["cc_y_out"][c, :, TOK * r:TOK * (r + 1)]), 16)
            s_io.val = io_base + 16
            s_io.inc(gp.dma_start(out=SCR0[:, :],
                                  in_=sigr_sp[128 * c:128 * (c + 1), :]), 16)

        @blk.vector
        def _(vector, c=c):
            vector.wait_ge(s_io.h, s_io.val)
            vector.tensor_tensor(SCR1[:, :], SCR0[:, :], ywkv_t, OP.mult)
            vector.drain()
            s_dve.inc(vector.tensor_tensor(AB[:, 2048 * c:2048 * (c + 1)],
                                           AB[:, 2048 * c:2048 * (c + 1)], SCR1[:, :], OP.add), 1)

    ctx.barrier()
    ctx.tap("s2_x1", AB[:, 0:2048])
    if S2_TRUNC == 4:
        return

    emit_ln_stats(ctx, Y)
    h2l_pack = ctx.sbuf("h2l_pack", (128, 16))

    @blk.vector
    def _(vector):
        writes = [[(h2l_pack[:, 2 * c:2 * c + 1], slice(TL - 1, TL)),
                   (h2l_pack[:, 2 * c + 1:2 * c + 2], slice(TOK - 1, TOK))] for c in range(CH)]
        s_dve.inc(emit_ln_apply(ctx, vector, writes, Y, ln2g, ln2b), 1)

    @blk.gpsimd
    def _(gp):
        gp.wait_ge(s_dve.h, s_dve.val)
        s_st.inc(gp.dma_start(out=cc_h2l_in[:], in_=h2l_pack[:, :]), 16)
        gp.wait_ge(s_st.h, s_st.val)
        s_cc.inc(gp.collective_compute(
            "AllGather", OP.bypass, replica_groups=[list(range(NCORE))],
            ins=[cc_h2l_in[:]], outs=[cc_h2l_out[:, :]]), 1)
        gp.wait_ge(s_cc.h, s_cc.val)
        _switch_pick(ctx, gp, h2l_pack[:, :], lambda r: cc_h2l_out[r, :],
                     ctx.dram["ts_pack"][0:TOK], s_io)
        for c in range(CH):
            s_io.inc(gp.dma_start(out=X[c][:, :], in_=x_sp[128 * c:128 * (c + 1), :]), 16)
        gp.wait_ge(s_io.h, s_io.val)

    @blk.vector
    def _(vector):
        vector.wait_ge(s_io.h, s_io.val)
        for c in range(CH):
            i = vector.tensor_tensor(X[c][:, :], X[c][:, :], AB[:, 2048 * c:2048 * (c + 1)], OP.add)
        s_dve.inc(i, 1)

    ctx.barrier()
    ctx.tap("s2_x2", X[0][:, :])
    ctx.tap("s2_h2l", h2l_pack[:, :])
    if S2_TRUNC == 5:
        return

    h2sT2 = [[XQf[c][:, 0:TS] for c in range(CH)],
             [XQf[c][:, TS:2 * TS] for c in range(CH)]]
    kkg = [AB[:, 2048 * i:2048 * i + TS] for i in range(8)]
    kv_acc = [AB[:, 2048 * i + TS:2048 * i + 2 * TS] for i in range(8)]
    rr_t = SCR2[:, 0:TS]
    ev2 = SCR2[:, TS:2 * TS]

    for b in range(B):
        @blk.vector
        def _(vector, b=b):
            vector.wait_ge(s_mm.h, s_mm.val)
            for tti in range(2):
                emit_hs_tile(ctx, vector, h2sT2[tti], Y, b, tti, h2l_pack,
                             (lambda c, b=b: 2 * c + b), ln2g, ln2b)
            vector.drain()
            s_dve.inc(vector.tensor_copy(dmy2[:, 0:1], dmy2[:, 1:2]), 1)

        for tti in range(2):
            col0 = b * TL + tti * TS
            h2sT = h2sT2[tti]

            for g in range(4):
                for ock in range(8):
                    gcol = 1024 * g + 128 * ock

                    @blk.sync
                    def _(sync, gcol=gcol):
                        sync.wait_ge(s_mm.h, s_mm.val)
                        for kk in range(CH):
                            s_io.inc(sync.dma_start(
                                out=wt3[:, 128 * kk:128 * (kk + 1)],
                                in_=ctx.dram["Wck"][128 * kk:128 * (kk + 1), gcol:gcol + 128]), 16)

                    @blk.tensor
                    def _(tensor, ock=ock, h2sT=h2sT):
                        tensor.wait_ge(s_io.h, s_io.val)
                        tensor.wait_ge(s_dve.h, s_dve.val)
                        for kk in range(CH):
                            i = tensor.matmul(psN[ock % 2][:, :], wt3[:, 128 * kk:128 * (kk + 1)],
                                              h2sT[kk][:, :], start=(kk == 0), stop=(kk == CH - 1))
                        s_mm.inc(i, 1)

                    @blk.scalar
                    def _(scalar, ock=ock):
                        scalar.wait_ge(s_mm.h, s_mm.val)
                        scalar.activation(ev2, psN[ock % 2][:, :], AF.Relu)
                        scalar.drain()
                        s_act.inc(scalar.copy(ev2[:, 0:1], ev2[:, 0:1]), 1)

                    @blk.vector
                    def _(vector, ock=ock):
                        vector.wait_ge(s_act.h, s_act.val)
                        vector.tensor_tensor(kkg[ock][:, :], ev2, ev2, OP.mult)
                        vector.drain()
                        s_dve.inc(vector.tensor_copy(dmy2[:, 0:1], dmy2[:, 1:2]), 1)

                for oo in range(CH):
                    @blk.sync
                    def _(sync, g=g, oo=oo):
                        sync.wait_ge(s_mm.h, s_mm.val)
                        for ock in range(8):
                            s_io.inc(sync.dma_start(
                                out=wt3[:, 128 * ock:128 * (ock + 1)],
                                in_=ctx.dram["Wcv"][1024 * g + 128 * ock:1024 * g + 128 * (ock + 1),
                                                    128 * oo:128 * (oo + 1)]), 16)

                    @blk.tensor
                    def _(tensor, oo=oo):
                        tensor.wait_ge(s_io.h, s_io.val)
                        tensor.wait_ge(s_dve.h, s_dve.val)
                        for ock in range(8):
                            i = tensor.matmul(psN[2 + oo % 2][:, :], wt3[:, 128 * ock:128 * (ock + 1)],
                                              kkg[ock][:, :], start=(ock == 0), stop=(ock == 7))
                        s_mm.inc(i, 1)

                    @blk.vector
                    def _(vector, g=g, oo=oo):
                        vector.wait_ge(s_mm.h, s_mm.val)
                        if g == 0:
                            vector.tensor_copy(kv_acc[oo][:, :], psN[2 + oo % 2][:, :])
                        else:
                            vector.tensor_tensor(kv_acc[oo][:, :], kv_acc[oo][:, :],
                                                 psN[2 + oo % 2][:, :], OP.add)
                        vector.drain()
                        s_dve.inc(vector.tensor_copy(dmy2[:, 0:1], dmy2[:, 1:2]), 1)

            for oo in range(CH):
                @blk.sync
                def _(sync, oo=oo):
                    sync.wait_ge(s_mm.h, s_mm.val)
                    for kk in range(CH):
                        s_io.inc(sync.dma_start(
                            out=wt3[:, 128 * kk:128 * (kk + 1)],
                            in_=ctx.dram["Wcr"][128 * kk:128 * (kk + 1), 128 * oo:128 * (oo + 1)]), 16)

                @blk.tensor
                def _(tensor, oo=oo, h2sT=h2sT):
                    tensor.wait_ge(s_io.h, s_io.val)
                    tensor.wait_ge(s_dve.h, s_dve.val)
                    for kk in range(CH):
                        i = tensor.matmul(psN[oo % 2][:, :], wt3[:, 128 * kk:128 * (kk + 1)],
                                          h2sT[kk][:, :], start=(kk == 0), stop=(kk == CH - 1))
                    s_mm.inc(i, 1)

                @blk.scalar
                def _(scalar, oo=oo):
                    scalar.wait_ge(s_mm.h, s_mm.val)
                    scalar.activation(rr_t, psN[oo % 2][:, :], AF.Sigmoid)
                    scalar.drain()
                    s_act.inc(scalar.copy(rr_t[:, 0:1], rr_t[:, 0:1]), 1)

                @blk.vector
                def _(vector, oo=oo, col0=col0):
                    vector.wait_ge(s_act.h, s_act.val)
                    vector.tensor_tensor(SCR2[:, 1024:1024 + TS], rr_t, kv_acc[oo][:, :], OP.mult)
                    vector.drain()
                    s_dve.inc(vector.tensor_tensor(X[oo][:, col0:col0 + TS],
                                                   X[oo][:, col0:col0 + TS],
                                                   SCR2[:, 1024:1024 + TS], OP.add), 1)

    ctx.barrier()
    ctx.tap("s2_kk0", kkg[0])
    ctx.tap("s2_kv0", kv_acc[0])
    ctx.tap("s2_rr7", rr_t)


def emit_wkv(ctx):
    nc, blk = ctx.nc, ctx.block
    X = XTiles(ctx)
    SCR0 = ctx.sb["SCR0"]
    SCR1 = ctx.sb["SCR1"]
    SCR2 = ctx.sb["SCR2"]
    XQf = [ctx.sb[f"XQ{c}"].bitcast(F32) for c in range(CH)]
    s_cc = ctx.sem("cc")
    s_wl = ctx.sem("wl")
    s_wa = ctx.sem("wa")
    s_wd = ctx.sem("wd")
    s_wy = ctx.sem("wy")

    E = X[0]; Eh = X[1]
    kseg = X[2]; vseg = X[3]
    Nq = X[4]; Bq = X[5]
    Cq = X[6]; Dq = X[7]
    vehA = XQf[0][:, 0:TL]; vehB = XQf[1][:, 0:TL]
    Gq = SCR2
    numA = XQf[2][:, 0:TL]; numB = XQf[3][:, 0:TL]
    denA = XQf[4][:, 0:TL]; denB = XQf[5][:, 0:TL]
    yA = XQf[6][:, 0:TL]; yB = XQf[7][:, 0:TL]
    gb = SCR0; eun = SCR1
    wkv_w = ctx.sb["wkv_w_t"]; eu_t = ctx.sb["eu_t"]

    stN = ctx.sbuf("stN", (128, 2))
    stB = ctx.sbuf("stB", (128, 2))
    stBp = ctx.sbuf("stBp", (128, 2))
    stG = ctx.sbuf("stG", (128, 2))
    zt = ctx.sbuf("zt", (128, 2))
    mzt = ctx.sbuf("mzt", (128, 2))
    mzw = ctx.sbuf("mzw", (128, 2))
    lnN = ctx.sbuf("lnN", (128, 2))
    zsc = ctx.sbuf("zsc", (128, 2))
    Mt = ctx.sbuf("Mt", (128, 4))
    imt = ctx.sbuf("imt", (128, 2))
    nsq = ctx.sbuf("nsq", (128, 2))
    sct = ctx.sbuf("sct", (128, 2))
    sc2 = ctx.sbuf("wkv_sc2", (128, 2))
    dmy = ctx.sbuf("wkv_dmy", (128, 4))
    El = ctx.sbuf("wkv_El", (128, 16))
    Ehl = ctx.sbuf("wkv_Ehl", (128, 16))
    Nl = ctx.sbuf("wkv_Nl", (128, 16))
    Bl = ctx.sbuf("wkv_Bl", (128, 16))
    SCq = ctx.sbuf("wkv_SCq", (128, 256))
    SCln = ctx.sbuf("wkv_SCln", (128, 256))

    NCHUNK = TL // RB
    NIT = NCORE * NCHUNK

    El3 = El.rearrange("p (t b) -> p t b", b=2)
    Ehl3 = Ehl.rearrange("p (t b) -> p t b", b=2)
    Nl3 = Nl.rearrange("p (t b) -> p t b", b=2)
    Bl3 = Bl.rearrange("p (t b) -> p t b", b=2)
    E4 = E.rearrange("p (b i j) -> p i j b", b=2, j=RB)
    Eh4 = Eh.rearrange("p (b i j) -> p i j b", b=2, j=RB)
    N4 = Nq.rearrange("p (b i j) -> p i j b", b=2, j=RB)
    B4 = Bq.rearrange("p (b i j) -> p i j b", b=2, j=RB)
    SCR = SCq.rearrange("p (b t) -> p t b", b=2)
    G2 = Gq.rearrange("p (b t) -> p t b", b=2)

    def bhalf(buf, b, a, w):
        return buf[:, TL * b + a:TL * b + a + w]

    def mt(j):
        return Mt[:, 2 * (j % 2):2 * (j % 2) + 2]

    @blk.vector
    def _(vector):
        for t in (stN, stB, stBp, stG, zt, mzt):
            vector.memset(t[:, :], 0.0)
        vector.memset(dmy[:, :], 1.0)
        vector.memset(SCq[:, :], 1.0)
        vector.memset(sc2[:, :], 1.0)
        vector.tensor_scalar(mzw[:, 0:1], wkv_w[:, 0:1], 1.0, None, OP.mult)
        vector.tensor_scalar(mzw[:, 1:2], wkv_w[:, 0:1], 1.0, None, OP.mult)

    ctx.barrier()

    @blk.gpsimd
    def _(gp):
        for s in range(NCORE + 1):
            if s > 0:
                gp.wait_ge(s_wa.h, 2 * (s - 1) + 1)
                gp.wait_ge(s_wd.h, 3 * (s - 1) + 1)
            src = s % NCORE
            wl_base = s_wl.val
            for arm in gp.switch_core_id(n=128):
                j = arm.logical % NCORE
                s_wl.inc(gp.dma_start(out=kseg[:, :],
                                      in_=ctx.dram["cc_k_out"][src, 128 * j:128 * (j + 1), :]), 16)
                s_wl.inc(gp.dma_start(out=vseg[:, :],
                                      in_=ctx.dram["cc_vgat"][src, 128 * j:128 * (j + 1), :]), 16)
            s_wl.val = wl_base + 32
            if s > 0:
                gp.wait_ge(s_wd.h, 3 * (s - 1) + 2)
                s_wy.inc(gp.dma_start(out=ctx.dram["cc_y_in"][:, TOK * (s - 1):TOK * (s - 1) + TL],
                                      in_=yA), 16)
                s_wy.inc(gp.dma_start(out=ctx.dram["cc_y_in"][:, TOK * (s - 1) + TL:TOK * s],
                                      in_=yB), 16)
        gp.wait_ge(s_wy.h, s_wy.val)
        s_cc.inc(gp.collective_compute(
            "AllGather", OP.bypass, replica_groups=[list(range(NCORE))],
            ins=[ctx.dram["cc_y_in"][:, :]], outs=[ctx.dram["cc_y_out"][:, :, :]]), 1)
        gp.wait_ge(s_cc.h, s_cc.val)

    @blk.scalar
    def _(scalar):
        for s in range(NCORE + 1):
            if s > 0:
                scalar.wait_ge(s_wd.h, 3 * (s - 1) + 1)
                scalar.activation(SCln[:, :], SCq[:, :], AF.Ln)
                scalar.activation(lnN[:, :], stN[:, :], AF.Ln)
                scalar.drain()
                s_wa.inc(scalar.copy(lnN[:, 0:1], lnN[:, 0:1]), 1)
                scalar.wait_ge(s_wd.h, 3 * (s - 1) + 3)
            scalar.wait_ge(s_wl.h, 32 * (s + 1))
            scalar.activation(E[:, 0:TL], kseg[:, 0:TL], AF.Exp, bias=mzt[:, 0:1])
            scalar.activation(E[:, TL:TOK], kseg[:, TL:TOK], AF.Exp, bias=mzt[:, 1:2])
            scalar.activation(Eh[:, 0:TL], kseg[:, 0:TL], AF.Exp, bias=mzw[:, 0:1])
            scalar.activation(Eh[:, TL:TOK], kseg[:, TL:TOK], AF.Exp,
                                       bias=mzw[:, 1:2])
            scalar.drain()
            s_wa.inc(scalar.copy(Eh[:, 0:1], Eh[:, 0:1]), 1)

    @blk.vector
    def _(vector):
        def dmyop():
            return vector.tensor_scalar(dmy[:, 0:2], dmy[:, 2:4], 1.0, None, OP.mult)

        def emit_veh():
            vector.tensor_tensor(vehA, vseg[:, 0:TL], Eh[:, 0:TL], OP.mult)
            vector.tensor_tensor(vehB, vseg[:, TL:TOK], Eh[:, TL:TOK], OP.mult)

        def emit_bulk():
            vector.reciprocal(gb[:, 0:TL], Nq[:, 0:TL])
            vector.reciprocal(gb[:, TL:TOK], Nq[:, TL:TOK])
            vector.tensor_scalar(eun[:, 0:TL], Eh[:, 0:TL], eu_t[:, 0:1], None, OP.mult)
            vector.tensor_scalar(eun[:, TL:TOK], Eh[:, TL:TOK], eu_t[:, 0:1], None, OP.mult)
            for b in range(B):
                vector.tensor_tensor(bhalf(Cq, b, 1, TL - 1), bhalf(Bq, b, 0, TL - 1),
                                     bhalf(gb, b, 1, TL - 1), OP.mult)
            for b in range(B):
                veh_b = vehA if b == 0 else vehB
                vector.tensor_tensor(bhalf(Dq, b, 0, TL), veh_b,
                                     bhalf(gb, b, 0, TL), OP.mult)
            for b in range(B):
                vector.tensor_tensor(Cq[:, TL * b:TL * b + 1], stBp[:, b:b + 1],
                                     gb[:, TL * b:TL * b + 1], OP.mult)
            vector.drain()
            for b in range(B):
                vector.tensor_tensor_scan(bhalf(Gq, b, 0, TL), bhalf(Cq, b, 0, TL),
                                          bhalf(Dq, b, 0, TL), stG[:, b:b + 1],
                                          OP.mult, OP.add)
            vector.drain()
            for b in range(B):
                vector.tensor_tensor(bhalf(gb, b, 1, TL - 1), bhalf(Gq, b, 0, TL - 1),
                                     bhalf(Bq, b, 0, TL - 1), OP.mult)
            for b in range(B):
                vector.tensor_tensor(gb[:, TL * b:TL * b + 1], stG[:, b:b + 1],
                                     stBp[:, b:b + 1], OP.mult)
            for b in range(B):
                veh_b = vehA if b == 0 else vehB
                vector.tensor_scalar(bhalf(Dq, b, 0, TL), veh_b,
                                     eu_t[:, 0:1], None, OP.mult)
            vector.drain()
            vector.tensor_tensor(numA, gb[:, 0:TL], Dq[:, 0:TL], OP.add)
            vector.tensor_tensor(numB, gb[:, TL:TOK], Dq[:, TL:TOK], OP.add)
            for b in range(B):
                dstd = denA if b == 0 else denB
                vector.tensor_tensor(dstd[:, 1:TL], bhalf(Bq, b, 0, TL - 1),
                                     bhalf(eun, b, 1, TL - 1), OP.add)
            for b in range(B):
                dstd = denA if b == 0 else denB
                vector.tensor_tensor(dstd[:, 0:1], stBp[:, b:b + 1],
                                     eun[:, TL * b:TL * b + 1], OP.add)
            vector.drain()
            vector.reciprocal(denA, denA)
            vector.reciprocal(denB, denB)
            vector.drain()
            vector.tensor_tensor(yA, numA, denA, OP.mult)
            vector.tensor_tensor(yB, numB, denB, OP.mult)
            vector.drain()
            vector.tensor_copy(stG[:, :], G2[:, TL - 1, :])
            vector.drain()

        def emit_zupd():
            vector.tensor_reduce(zsc[:, 0:1], SCln[:, 0:NCHUNK], AX.X, OP.add)
            vector.tensor_reduce(zsc[:, 1:2], SCln[:, NCHUNK:2 * NCHUNK], AX.X, OP.add)
            vector.drain()
            vector.tensor_tensor(zt[:, :], zt[:, :], lnN[:, :], OP.add)
            vector.drain()
            vector.tensor_tensor(zt[:, :], zt[:, :], zsc[:, :], OP.subtract)
            vector.drain()
            vector.tensor_scalar(mzt[:, :], zt[:, :], -1.0, None, OP.mult)
            vector.tensor_tensor(mzw[:, 0:1], wkv_w[:, 0:1], zt[:, 0:1], OP.subtract)
            vector.tensor_tensor(mzw[:, 1:2], wkv_w[:, 0:1], zt[:, 1:2], OP.subtract)
            vector.drain()

        def emit_seg_prep():
            vector.memset(SCq[:, :], 1.0)
            vector.memset(sc2[:, :], 1.0)
            vector.tensor_copy(stBp[:, :], stB[:, :])
            vector.drain()

        vector.wait_ge(s_wa.h, 1)

        with vector.Fori(1, NIT + 1) as i:
            ic = (i - 1) % NCHUNK
            vector.tensor_copy(El3[:, :, :], E4[:, ic, :, :])
            vector.tensor_copy(Ehl3[:, :, :], Eh4[:, ic, :, :])
            vector.tensor_scalar(El3[:, :, 0], El3[:, :, 0], sc2[:, 0:1], None, OP.mult)
            vector.tensor_scalar(El3[:, :, 1], El3[:, :, 1], sc2[:, 1:2], None, OP.mult)
            vector.tensor_scalar(Ehl3[:, :, 0], Ehl3[:, :, 0], sc2[:, 0:1], None, OP.mult)
            vector.tensor_scalar(Ehl3[:, :, 1], Ehl3[:, :, 1], sc2[:, 1:2], None, OP.mult)
            vector.tensor_tensor(mt(0), stN[:, :], El3[:, 0, :], OP.max)
            vector.tensor_tensor(Nl3[:, 0, :], stB[:, :], Ehl3[:, 0, :], OP.add)
            vector.reciprocal(imt[:, :], mt(0))
            vector.tensor_tensor(nsq[:, :], Nl3[:, 0, :], Nl3[:, 0, :], OP.mult)
            vector.tensor_tensor(mt(1), Nl3[:, 0, :], El3[:, 1, :], OP.max)
            vector.tensor_tensor(Bl3[:, 0, :], nsq[:, :], imt[:, :], OP.mult)
            dmyop()
            for j in range(1, RB - 1):
                vector.tensor_tensor(Nl3[:, j, :], Bl3[:, j - 1, :], Ehl3[:, j, :], OP.add)
                vector.reciprocal(imt[:, :], mt(j))
                vector.tensor_tensor(nsq[:, :], Nl3[:, j, :], Nl3[:, j, :], OP.mult)
                vector.tensor_tensor(mt(j + 1), Nl3[:, j, :], El3[:, j + 1, :], OP.max)
                vector.tensor_tensor(Bl3[:, j, :], nsq[:, :], imt[:, :], OP.mult)
                dmyop()
            J = RB - 1
            vector.tensor_tensor(Nl3[:, J, :], Bl3[:, J - 1, :], Ehl3[:, J, :], OP.add)
            vector.reciprocal(imt[:, :], mt(J))
            vector.tensor_copy(Eh4[:, ic, :, :], Ehl3[:, :, :])
            vector.tensor_tensor(nsq[:, :], Nl3[:, J, :], Nl3[:, J, :], OP.mult)
            vector.reciprocal(sct[:, :], Nl3[:, J, :])
            vector.tensor_tensor(Bl3[:, J, :], nsq[:, :], imt[:, :], OP.mult)
            vector.tensor_tensor(sc2[:, :], sc2[:, :], sct[:, :], OP.mult)
            vector.tensor_tensor(stN[:, :], Nl3[:, J, :], sct[:, :], OP.mult)
            vector.tensor_copy(SCR[:, ic, :], sct[:, :])
            vector.tensor_tensor(Bl3[:, J, :], Bl3[:, J, :], sct[:, :], OP.mult)
            vector.tensor_copy(N4[:, ic, :, :], Nl3[:, :, :])
            vector.tensor_copy(stB[:, :], Bl3[:, J, :])
            vector.tensor_copy(B4[:, ic, :, :], Bl3[:, :, :])
            mr = vector.to_reg(i % NCHUNK)
            with vector.If_cmp(mr, 0, "IS_EQ"):
                vector.drain()
                emit_veh()
                vector.drain()
                s_wd.inc(dmyop(), 1)
                rw = vector.to_reg((i >> 2) - 32)
                vector.wait_ge(s_wy.h, rw)
                emit_bulk()
                s_wd.inc(dmyop(), 1)
                r1 = vector.to_reg(i >> 6)
                vector.wait_ge(s_wa.h, r1)
                emit_zupd()
                s_wd.inc(dmyop(), 1)
                r2 = vector.to_reg((i >> 6) + 1)
                vector.wait_ge(s_wa.h, r2)
                emit_seg_prep()
        vector.drain()

    ctx.barrier()


def emit_s5(ctx, stage):
    nc, blk = ctx.nc, ctx.block
    X = XTiles(ctx)
    AB = ctx.sbuf("ABUF", (128, 16384))
    Y = [AB[:, 2048 * c:2048 * (c + 1)] for c in range(CH)]
    SCR0 = ctx.sbuf("SCR0", (128, TOK))
    SCR1 = ctx.sbuf("SCR1", (128, TOK))
    SCR2 = ctx.sbuf("SCR2", (128, TOK))
    XQ = [ctx.sbuf(f"XQ{c}", (128, TOK), F16) for c in range(CH)]
    XQf = [XQ[c].bitcast(F32) for c in range(CH)]
    wt3 = ctx.sbuf("wt3", (128, CH * 128))
    psN = [ctx.psum(f"ps{i}", (128, TS)) for i in range(4)]
    ones = ctx.sbuf("ones_col", (128, 1))
    s_dve = ctx.sem("dve")
    s_mm = ctx.sem("mm")
    s_act = ctx.sem("act")
    s_io = ctx.sem("io")
    s_st = ctx.sem("st")
    s_cc = ctx.sem("cc")

    ba_t = ctx.sbuf("s5_ba", (128, CH))
    bx_t = ctx.sbuf("s5_bx", (128, CH))
    aL = ctx.sbuf("s5_aL", (128, CH))
    zer = ctx.sbuf("s5_zer", (128, TL))
    pk = ctx.sbuf("s5_pk", (128, 4 * CH))
    pkg = ctx.sbuf("s5_pkg", (128, 32 * NCORE))
    Hall = ctx.sbuf("s5_Hall", (128, 16 * (NCORE - 1)))
    t16 = ctx.sbuf("s5_t16", (128, 16))
    ct = ctx.sbuf("s5_c", (128, 16))

    sr = SCR0
    si = SCR1
    tmpe = SCR2
    at_b = [XQf[0][:, 0:TL], XQf[1][:, 0:TL]]
    bt_b = [XQf[2][:, 0:TL], XQf[3][:, 0:TL]]
    yl_b = [XQf[4][:, 0:TL], XQf[5][:, 0:TL]]
    P_b = [XQf[6][:, 0:TL], XQf[7][:, 0:TL]]
    LN8 = math.log(8.0)

    y_sp = ctx.dramt("s5_y_sp", (C, TOK))
    p_sp = ctx.dramt("s5_p_sp", (C, TOK))
    cc_pk_in = ctx.dramt("cc_pk_in", (128 * 4 * CH,))
    cc_pk_out = ctx.dramt("cc_pk_out", (NCORE, 128 * 4 * CH))

    @blk.sync
    def _(sync):
        for nm, t in (("s5_Wba", ba_t), ("s5_Wbx", bx_t)):
            for c in range(CH):
                s_io.inc(sync.dma_start(out=t[:, c:c + 1],
                                        in_=ctx.dram[nm][128 * c:128 * (c + 1)]), 16)
        for c in range(CH):
            s_io.inc(sync.dma_start(out=aL[:, c:c + 1],
                                    in_=ctx.dram["s5_Lam"][128 * c:128 * (c + 1)]), 16)

    @blk.scalar
    def _(scalar):
        scalar.wait_ge(s_io.h, s_io.val)
        scalar.activation(aL[:, :], aL[:, :], AF.Sigmoid)
        scalar.drain()
        s_act.inc(scalar.copy(aL[:, 0:1], aL[:, 0:1]), 1)

    @blk.vector
    def _(vector):
        vector.memset(zer[:, :], 0.0)
        vector.drain()
        s_dve.inc(vector.tensor_copy(zer[:, 0:1], zer[:, 1:2]), 1)

    emit_ln(ctx, X, Y)

    for oo in range(CH):
        @blk.sync
        def _(sync, oo=oo):
            sync.wait_ge(s_mm.h, s_mm.val)
            for kk in range(CH):
                s_io.inc(sync.dma_start(
                    out=wt3[:, 128 * kk:128 * (kk + 1)],
                    in_=ctx.dram["s5_Wa"][128 * kk:128 * (kk + 1), 128 * oo:128 * (oo + 1)]), 16)

        @blk.tensor
        def _(tensor):
            tensor.wait_ge(s_io.h, s_io.val)
            tensor.wait_ge(s_act.h, s_act.val)
            tensor.wait_ge(s_dve.h, s_dve.val)
            for nt in range(NT):
                for kk in range(CH):
                    i = tensor.matmul(psN[nt][:, :], wt3[:, 128 * kk:128 * (kk + 1)],
                                      Y[kk][:, TS * nt:TS * (nt + 1)],
                                      start=(kk == 0), stop=(kk == CH - 1))
            s_mm.inc(i, 1)

        @blk.scalar
        def _(scalar, oo=oo):
            scalar.wait_ge(s_mm.h, s_mm.val)
            for nt in range(NT):
                scalar.activation(sr[:, TS * nt:TS * (nt + 1)], psN[nt][:, :],
                                  AF.Sigmoid, bias=ba_t[:, oo:oo + 1])
            scalar.drain()
            s_act.inc(scalar.copy(sr[:, 0:1], sr[:, 0:1]), 1)

        @blk.sync
        def _(sync, oo=oo):
            sync.wait_ge(s_mm.h, s_mm.val)
            for kk in range(CH):
                s_io.inc(sync.dma_start(
                    out=wt3[:, 128 * kk:128 * (kk + 1)],
                    in_=ctx.dram["s5_Wx"][128 * kk:128 * (kk + 1), 128 * oo:128 * (oo + 1)]), 16)

        @blk.tensor
        def _(tensor):
            tensor.wait_ge(s_io.h, s_io.val)
            tensor.wait_ge(s_act.h, s_act.val)
            for nt in range(NT):
                for kk in range(CH):
                    i = tensor.matmul(psN[nt][:, :], wt3[:, 128 * kk:128 * (kk + 1)],
                                      Y[kk][:, TS * nt:TS * (nt + 1)],
                                      start=(kk == 0), stop=(kk == CH - 1))
            s_mm.inc(i, 1)

        @blk.scalar
        def _(scalar, oo=oo):
            scalar.wait_ge(s_mm.h, s_mm.val)
            for nt in range(NT):
                scalar.activation(si[:, TS * nt:TS * (nt + 1)], psN[nt][:, :],
                                  AF.Sigmoid, bias=bx_t[:, oo:oo + 1])
            scalar.activation(tmpe[:, :], sr[:, :], AF.Exp, scale=-LN8)
            scalar.drain()
            s_act.inc(scalar.copy(tmpe[:, 0:1], tmpe[:, 0:1]), 1)

        @blk.vector
        def _(vector, oo=oo):
            vector.wait_ge(s_act.h, s_act.val)
            vector.tensor_scalar(at_b[0], tmpe[:, 0:TL], aL[:, oo:oo + 1], None, OP.mult)
            vector.tensor_scalar(at_b[1], tmpe[:, TL:TOK], aL[:, oo:oo + 1], None, OP.mult)
            vector.tensor_tensor(bt_b[0], at_b[0], at_b[0], OP.mult)
            vector.tensor_tensor(bt_b[1], at_b[1], at_b[1], OP.mult)
            vector.tensor_scalar(bt_b[0], bt_b[0], -1.0, 1.0, OP.mult, OP.add)
            vector.tensor_scalar(bt_b[1], bt_b[1], -1.0, 1.0, OP.mult, OP.add)
            vector.drain()
            s_dve.inc(vector.tensor_copy(t16[:, 0:1], t16[:, 1:2]), 1)

        @blk.scalar
        def _(scalar):
            scalar.wait_ge(s_dve.h, s_dve.val)
            scalar.activation(tmpe[:, 0:TL], bt_b[0], AF.Sqrt)
            scalar.activation(tmpe[:, TL:TOK], bt_b[1], AF.Sqrt)
            scalar.drain()
            s_act.inc(scalar.copy(tmpe[:, 0:1], tmpe[:, 0:1]), 1)

        @blk.vector
        def _(vector, oo=oo):
            vector.wait_ge(s_act.h, s_act.val)
            vector.wait_ge(s_st.h, s_st.val)
            vector.tensor_tensor(bt_b[0], tmpe[:, 0:TL], si[:, 0:TL], OP.mult)
            vector.tensor_tensor(bt_b[1], tmpe[:, TL:TOK], si[:, TL:TOK], OP.mult)
            vector.tensor_tensor(bt_b[0], bt_b[0], Y[oo][:, 0:TL], OP.mult)
            vector.tensor_tensor(bt_b[1], bt_b[1], Y[oo][:, TL:TOK], OP.mult)
            vector.drain()
            vector.tensor_tensor_scan(yl_b[0], at_b[0], bt_b[0], zer[:, 0:1],
                                      OP.mult, OP.add)
            vector.tensor_tensor_scan(yl_b[1], at_b[1], bt_b[1], zer[:, 0:1],
                                      OP.mult, OP.add)
            vector.tensor_tensor_scan(P_b[0], at_b[0], zer[:, :], ones[:, 0:1],
                                      OP.mult, OP.add)
            vector.tensor_tensor_scan(P_b[1], at_b[1], zer[:, :], ones[:, 0:1],
                                      OP.mult, OP.add)
            vector.drain()
            vector.tensor_copy(pk[:, 4 * oo + 0:4 * oo + 1], P_b[0][:, TL - 1:TL])
            vector.tensor_copy(pk[:, 4 * oo + 1:4 * oo + 2], P_b[1][:, TL - 1:TL])
            vector.tensor_copy(pk[:, 4 * oo + 2:4 * oo + 3], yl_b[0][:, TL - 1:TL])
            vector.tensor_copy(pk[:, 4 * oo + 3:4 * oo + 4], yl_b[1][:, TL - 1:TL])
            vector.drain()
            s_dve.inc(vector.tensor_copy(t16[:, 0:1], t16[:, 1:2]), 1)

        @blk.gpsimd
        def _(gp, oo=oo):
            gp.wait_ge(s_dve.h, s_dve.val)
            s_st.inc(gp.dma_start(out=y_sp[128 * oo:128 * (oo + 1), 0:TL], in_=yl_b[0]), 16)
            s_st.inc(gp.dma_start(out=y_sp[128 * oo:128 * (oo + 1), TL:TOK], in_=yl_b[1]), 16)
            s_st.inc(gp.dma_start(out=p_sp[128 * oo:128 * (oo + 1), 0:TL], in_=P_b[0]), 16)
            s_st.inc(gp.dma_start(out=p_sp[128 * oo:128 * (oo + 1), TL:TOK], in_=P_b[1]), 16)

    @blk.gpsimd
    def _(gp):
        gp.wait_ge(s_st.h, s_st.val)
        s_st.inc(gp.dma_start(out=cc_pk_in[:], in_=pk[:, :]), 16)
        gp.wait_ge(s_st.h, s_st.val)
        s_cc.inc(gp.collective_compute(
            "AllGather", OP.bypass, replica_groups=[list(range(NCORE))],
            ins=[cc_pk_in[:]], outs=[cc_pk_out[:, :]]), 1)
        gp.wait_ge(s_cc.h, s_cc.val)
        for s in range(NCORE):
            s_io.inc(gp.dma_start(out=pkg[:, 32 * s:32 * (s + 1)], in_=cc_pk_out[s, :]), 16)
        gp.wait_ge(s_io.h, s_io.val)

    pkv = pkg.rearrange("p (s o f) -> p s o f", s=NCORE, f=4)
    Hv = Hall.rearrange("p (s o f) -> p s o f", s=NCORE - 1, f=2)
    t16v = t16.rearrange("p (o f) -> p o f", f=2)

    @blk.vector
    def _(vector):
        vector.wait_ge(s_io.h, s_io.val)
        vector.tensor_copy(Hv[:, 0, :, :], pkv[:, 0, :, 2:4])
        vector.drain()
        for s in range(1, NCORE - 1):
            vector.tensor_tensor(t16v[:, :, :], pkv[:, s, :, 0:2], Hv[:, s - 1, :, :], OP.mult)
            vector.drain()
            vector.tensor_tensor(Hv[:, s, :, :], t16v[:, :, :], pkv[:, s, :, 2:4], OP.add)
            vector.drain()
        for arm in vector.switch_core_id(n=128):
            r = arm.logical % NCORE
            if r == 0:
                vector.memset(ct[:, :], 0.0)
            else:
                vector.tensor_copy(ct[:, :], Hall[:, 16 * (r - 1):16 * r])
        vector.drain()
        s_dve.inc(vector.tensor_copy(t16[:, 0:1], t16[:, 1:2]), 1)

    ctx.barrier()

    for oo in range(CH):
        @blk.gpsimd
        def _(gp, oo=oo):
            gp.wait_ge(s_dve.h, s_dve.val)
            s_io.inc(gp.dma_start(out=SCR0[:, :], in_=y_sp[128 * oo:128 * (oo + 1), :]), 16)
            s_io.inc(gp.dma_start(out=SCR1[:, :], in_=p_sp[128 * oo:128 * (oo + 1), :]), 16)

        @blk.vector
        def _(vector, oo=oo):
            vector.wait_ge(s_io.h, s_io.val)
            for b in range(B):
                vector.tensor_scalar(SCR1[:, TL * b:TL * (b + 1)], SCR1[:, TL * b:TL * (b + 1)],
                                     ct[:, 2 * oo + b:2 * oo + b + 1], None, OP.mult)
            vector.drain()
            for b in range(B):
                vector.tensor_tensor(SCR1[:, TL * b:TL * (b + 1)], SCR1[:, TL * b:TL * (b + 1)],
                                     SCR0[:, TL * b:TL * (b + 1)], OP.add)
            vector.drain()
            for b in range(B):
                i = vector.tensor_tensor(X[oo][:, TL * b:TL * (b + 1)],
                                         X[oo][:, TL * b:TL * (b + 1)],
                                         SCR1[:, TL * b:TL * (b + 1)], OP.add)
            vector.drain()
            s_dve.inc(vector.tensor_copy(t16[:, 0:1], t16[:, 1:2]), 1)

    ctx.barrier()


def emit_s4(ctx, stage):
    nc, blk = ctx.nc, ctx.block
    X = XTiles(ctx)
    AB = ctx.sbuf("ABUF", (128, 16384))
    Y = [AB[:, 2048 * c:2048 * (c + 1)] for c in range(CH)]
    SCR0 = ctx.sbuf("SCR0", (128, TOK))
    SCR1 = ctx.sbuf("SCR1", (128, TOK))
    SCR2 = ctx.sbuf("SCR2", (128, TOK))
    XQ = [ctx.sbuf(f"XQ{c}", (128, TOK), F16) for c in range(CH)]
    XQf = [XQ[c].bitcast(F32) for c in range(CH)]
    wt3 = ctx.sbuf("wt3", (128, CH * 128))
    psN = [ctx.psum(f"ps{i}", (128, TS)) for i in range(4)]
    s_dve = ctx.sem("dve")
    s_mm = ctx.sem("mm")
    s_act = ctx.sem("act")
    s_io = ctx.sem("io")
    s_st = ctx.sem("st")

    gw_t = ctx.sbuf("s4_gw", (128, 64))
    gb_t = ctx.sbuf("s4_gb", (128, 8))
    sc8 = ctx.sbuf("s4_sc8", (128, 8))
    g_sfc = ctx.sbuf("s4_sfc", (128, 8))
    g_gs = ctx.sbuf("s4_gs", (128, 8))
    g_eq = ctx.sbuf("s4_eq", (128, 8))
    g_cum = ctx.sbuf("s4_cum", (128, 8))
    g_fm = ctx.sbuf("s4_fm", (128, 8))
    g_t = ctx.sbuf("s4_t", (128, 8))
    g_m = ctx.sbuf("s4_m", (128, 8))
    g_e = ctx.sbuf("s4_e", (128, 16))
    g_msk = ctx.sbuf("s4_msk", (128, 8))
    g_tmp8 = ctx.sbuf("s4_tmp8", (128, 16))
    g_w = ctx.sbuf("s4_w", (128, 8))
    s4o = ctx.sbuf("s4_ones", (128, 8))
    s4z = ctx.sbuf("s4_zc", (128, 1))
    fnc = ctx.sbuf("s4_fnc", (128, 2))
    BIG = 1.0e30

    om_spT = ctx.dramt(f"{stage}_omT", (EMOE, TOK))

    @blk.sync
    def _(sync):
        for kk in range(CH):
            s_io.inc(sync.dma_start(out=gw_t[:, 8 * kk:8 * (kk + 1)],
                                    in_=ctx.dram["s4_gatew"][128 * kk:128 * (kk + 1), :]), 16)
        s_io.inc(sync.dma_start(
            out=gb_t[:, :],
            in_=ctx.dram["s4_gateb"][:].unsqueeze(0).broadcast_to([128, EMOE])), 16)

    @blk.vector
    def _(vector):
        vector.memset(s4o[:, :], 1.0)
        vector.memset(s4z[:, :], 0.0)
        vector.memset(fnc[:, :], 1.0)
        vector.drain()
        s_dve.inc(vector.tensor_copy(fnc[:, 0:1], fnc[:, 1:2]), 1)

    emit_ln(ctx, X, Y)

    gsv = g_sfc.rearrange("p (g two) -> p g two", two=2)
    mv = g_msk.rearrange("p (g two) -> p g two", two=2)

    for t in range(TOK // 128):
        kc, tc = t // 8, t % 8
        @blk.tensor
        def _(tensor, t=t):
            tensor.wait_ge(s_io.h, s_io.val)
            tensor.wait_ge(s_dve.h, s_dve.val)
            for kk in range(CH):
                i = tensor.matmul(psN[0][:, 0:EMOE], Y[kk][:, 128 * t:128 * (t + 1)],
                                  gw_t[:, 8 * kk:8 * (kk + 1)],
                                  start=(kk == 0), stop=(kk == CH - 1))
            s_mm.inc(i, 1)

        @blk.scalar
        def _(scalar):
            scalar.wait_ge(s_mm.h, s_mm.val)
            scalar.activation(sc8[:, :], psN[0][:, 0:EMOE], AF.Sigmoid)
            scalar.drain()
            s_act.inc(scalar.copy(sc8[:, 0:1], sc8[:, 0:1]), 1)

        @blk.vector
        def _(vector, t=t):
            vector.wait_ge(s_act.h, s_act.val)
            vector.wait_ge(s_st.h, s_st.val)
            D = vector.drain
            vector.tensor_tensor(g_sfc[:, :], sc8[:, :], gb_t[:, :], OP.add)
            D()
            vector.tensor_tensor(g_gs[:, 0:4], gsv[:, :, 0], gsv[:, :, 1], OP.add)
            D()
            vector.tensor_reduce(g_m[:, 0:1], g_gs[:, 0:4], AX.X, OP.max)
            D()
            vector.tensor_scalar(g_eq[:, 0:4], g_gs[:, 0:4], g_m[:, 0:1], None, OP.is_equal)
            D()
            vector.tensor_tensor_scan(g_cum[:, 0:4], s4o[:, 0:4], g_eq[:, 0:4],
                                      s4z[:, 0:1], OP.mult, OP.add)
            D()
            vector.tensor_scalar(g_t[:, 0:4], g_cum[:, 0:4], 1.0, None, OP.is_equal)
            D()
            vector.tensor_tensor(g_fm[:, 0:4], g_eq[:, 0:4], g_t[:, 0:4], OP.mult)
            D()
            vector.tensor_scalar(g_t[:, 0:4], g_fm[:, 0:4], BIG, None, OP.mult)
            D()
            vector.tensor_tensor(g_gs[:, 4:8], g_gs[:, 0:4], g_t[:, 0:4], OP.subtract)
            D()
            vector.tensor_reduce(g_m[:, 1:2], g_gs[:, 4:8], AX.X, OP.max)
            D()
            vector.tensor_scalar(g_eq[:, 4:8], g_gs[:, 4:8], g_m[:, 1:2], None, OP.is_equal)
            D()
            vector.tensor_tensor_scan(g_cum[:, 4:8], s4o[:, 0:4], g_eq[:, 4:8],
                                      s4z[:, 0:1], OP.mult, OP.add)
            D()
            vector.tensor_scalar(g_t[:, 4:8], g_cum[:, 4:8], 1.0, None, OP.is_equal)
            D()
            vector.tensor_tensor(g_fm[:, 4:8], g_eq[:, 4:8], g_t[:, 4:8], OP.mult)
            D()
            vector.tensor_tensor(g_t[:, 0:4], g_fm[:, 0:4], g_fm[:, 4:8], OP.add)
            D()
            vector.tensor_copy(mv[:, :, 0], g_t[:, 0:4])
            vector.tensor_copy(mv[:, :, 1], g_t[:, 0:4])
            D()
            vector.tensor_tensor(g_tmp8[:, 0:8], g_sfc[:, :], g_msk[:, :], OP.mult)
            D()
            vector.tensor_reduce(g_m[:, 2:3], g_tmp8[:, 0:8], AX.X, OP.max)
            D()
            vector.tensor_scalar(g_eq[:, 0:8], g_tmp8[:, 0:8], g_m[:, 2:3], None, OP.is_equal)
            D()
            vector.tensor_tensor_scan(g_cum[:, 0:8], s4o[:, 0:8], g_eq[:, 0:8],
                                      s4z[:, 0:1], OP.mult, OP.add)
            D()
            vector.tensor_scalar(g_t[:, 0:8], g_cum[:, 0:8], 1.0, None, OP.is_equal)
            D()
            vector.tensor_tensor(g_e[:, 0:8], g_eq[:, 0:8], g_t[:, 0:8], OP.mult)
            D()
            vector.tensor_tensor(g_t[:, 0:8], g_e[:, 0:8], g_sfc[:, :], OP.mult)
            D()
            vector.tensor_reduce(g_m[:, 4:5], g_t[:, 0:8], AX.X, OP.add)
            vector.tensor_scalar(g_t[:, 0:8], g_e[:, 0:8], BIG, None, OP.mult)
            D()
            vector.tensor_tensor(g_tmp8[:, 8:16], g_tmp8[:, 0:8], g_t[:, 0:8], OP.subtract)
            D()
            vector.tensor_reduce(g_m[:, 3:4], g_tmp8[:, 8:16], AX.X, OP.max)
            D()
            vector.tensor_scalar(g_eq[:, 0:8], g_tmp8[:, 8:16], g_m[:, 3:4], None, OP.is_equal)
            D()
            vector.tensor_tensor_scan(g_cum[:, 0:8], s4o[:, 0:8], g_eq[:, 0:8],
                                      s4z[:, 0:1], OP.mult, OP.add)
            D()
            vector.tensor_scalar(g_t[:, 0:8], g_cum[:, 0:8], 1.0, None, OP.is_equal)
            D()
            vector.tensor_tensor(g_e[:, 8:16], g_eq[:, 0:8], g_t[:, 0:8], OP.mult)
            D()
            vector.tensor_tensor(g_t[:, 0:8], g_e[:, 8:16], g_sfc[:, :], OP.mult)
            D()
            vector.tensor_reduce(g_m[:, 5:6], g_t[:, 0:8], AX.X, OP.add)
            D()
            vector.tensor_tensor(g_m[:, 6:7], g_m[:, 4:5], g_m[:, 5:6], OP.add)
            D()
            vector.tensor_scalar(g_m[:, 6:7], g_m[:, 6:7], 1.0, 1.0e-20, OP.mult, OP.add)
            D()
            vector.reciprocal(g_m[:, 7:8], g_m[:, 6:7])
            D()
            vector.tensor_tensor(g_m[:, 4:5], g_m[:, 4:5], g_m[:, 7:8], OP.mult)
            vector.tensor_tensor(g_m[:, 5:6], g_m[:, 5:6], g_m[:, 7:8], OP.mult)
            D()
            vector.tensor_scalar(g_m[:, 4:5], g_m[:, 4:5], 2.5, None, OP.mult)
            vector.tensor_scalar(g_m[:, 5:6], g_m[:, 5:6], 2.5, None, OP.mult)
            D()
            vector.tensor_scalar(g_w[:, :], g_e[:, 0:8], g_m[:, 4:5], None, OP.mult)
            vector.tensor_scalar(g_t[:, 0:8], g_e[:, 8:16], g_m[:, 5:6], None, OP.mult)
            D()
            vector.tensor_tensor(g_w[:, :], g_w[:, :], g_t[:, 0:8], OP.add)
            D()
            s_dve.inc(vector.tensor_copy(fnc[:, 0:1], fnc[:, 1:2]), 1)

        @blk.gpsimd
        def _(gp, t=t):
            gp.wait_ge(s_dve.h, s_dve.val)
            for e in range(EMOE):
                s_st.inc(gp.dma_start(out=om_spT[e, 128 * t:128 * (t + 1)],
                                      in_=g_w[:, e:e + 1]), 16)

    ctx.barrier()
    ctx.tap(f"{stage}_om", om_spT[:, :])

    h_t = [XQf[ic][:, 0:TS] for ic in range(4)]
    yacc = [XQf[4 + i // 2][:, (i % 2) * TS:(i % 2 + 1) * TS] for i in range(8)]
    wbc = SCR2[:, 0:TS]
    gbuf = SCR0
    ubuf = SCR1

    wt3b = SCR2[:, TS:TS + 1024]
    mm_i = [0]

    def mm_block(wsrc, row0, col0, nrow, psdst, moving, movsl, tag, dve_lag=0):
        slab = wt3 if mm_i[0] % 2 == 0 else wt3b
        mm_i[0] += 1

        @blk.sync
        def _(sync):
            sync.wait_ge(s_mm.h, max(0, s_mm.val - 1))
            for kk in range(nrow):
                s_io.inc(sync.dma_start(
                    out=slab[:, 128 * kk:128 * (kk + 1)],
                    in_=wsrc[row0 + 128 * kk:row0 + 128 * (kk + 1), col0:col0 + 128]), 16)

        @blk.tensor
        def _(tensor):
            tensor.wait_ge(s_io.h, s_io.val)
            tensor.wait_ge(s_dve.h, max(0, s_dve.val - dve_lag))
            tensor.wait_ge(s_act.h, max(0, s_act.val - 1))
            for kk in range(nrow):
                i = tensor.matmul(psdst[:, :], slab[:, 128 * kk:128 * (kk + 1)],
                                  moving[kk][:, movsl], start=(kk == 0), stop=(kk == nrow - 1))
            s_mm.inc(i, 1)

    for tt in range(NT):
        tsl = slice(TS * tt, TS * (tt + 1))

        for e in range(EMOE):
            @blk.gpsimd
            def _(gp, e=e, tt=tt):
                gp.wait_ge(s_dve.h, s_dve.val)
                s_io.inc(gp.dma_start(
                    out=wbc,
                    in_=om_spT[e, TS * tt:TS * (tt + 1)].unsqueeze(0).broadcast_to([128, TS])), 16)

            for ic in range(4):
                mm_block(ctx.dram["s4_wg"], 1024 * e, 128 * ic, CH, psN[ic % 2], Y, tsl, "g", dve_lag=(0 if ic == 0 else 1))

                @blk.scalar
                def _(scalar, ic=ic):
                    scalar.wait_ge(s_mm.h, s_mm.val)
                    scalar.activation(gbuf[:, TS * ic:TS * (ic + 1)], psN[ic % 2][:, :], AF.Silu)
                    scalar.drain()
                    s_act.inc(scalar.copy(gbuf[:, 0:1], gbuf[:, 0:1]), 1)
            for ic in range(4):
                mm_block(ctx.dram["s4_wu"], 1024 * e, 128 * ic, CH, psN[ic % 2], Y, tsl, "u", dve_lag=1)

                @blk.scalar
                def _(scalar, ic=ic):
                    scalar.wait_ge(s_mm.h, s_mm.val)
                    scalar.activation(ubuf[:, TS * ic:TS * (ic + 1)], psN[ic % 2][:, :], AF.Copy)
                    scalar.drain()
                    s_act.inc(scalar.copy(ubuf[:, 0:1], ubuf[:, 0:1]), 1)

            @blk.vector
            def _(vector):
                vector.wait_ge(s_act.h, s_act.val)
                vector.wait_ge(s_io.h, s_io.val)
                for ic in range(4):
                    vector.tensor_tensor(h_t[ic], gbuf[:, TS * ic:TS * (ic + 1)],
                                         ubuf[:, TS * ic:TS * (ic + 1)], OP.mult)
                for ic in range(4):
                    vector.tensor_tensor(h_t[ic], h_t[ic], wbc, OP.mult)
                vector.drain()
                s_dve.inc(vector.tensor_copy(fnc[:, 0:1], fnc[:, 1:2]), 1)

            for oc in range(CH):
                mm_block(ctx.dram["s4_wd"], 512 * e, 128 * oc, 4, psN[2 + oc % 2],
                         h_t, slice(0, TS), "d", dve_lag=(0 if oc == 0 else 1))

                @blk.vector
                def _(vector, e=e, oc=oc):
                    vector.wait_ge(s_mm.h, s_mm.val)
                    if e == 0:
                        vector.tensor_copy(yacc[oc], psN[2 + oc % 2][:, :])
                    else:
                        vector.tensor_tensor(yacc[oc], yacc[oc], psN[2 + oc % 2][:, :], OP.add)
                    vector.drain()
                    s_dve.inc(vector.tensor_copy(fnc[:, 0:1], fnc[:, 1:2]), 1)

        for half in range(2):
            for ic4 in range(4):
                ic = 4 * half + ic4
                mm_block(ctx.dram["s4_swg"], 0, 128 * ic, CH, psN[ic4 % 2], Y, tsl, "sg", dve_lag=(0 if ic4 == 0 else 1))

                @blk.scalar
                def _(scalar, ic4=ic4):
                    scalar.wait_ge(s_mm.h, s_mm.val)
                    scalar.activation(gbuf[:, TS * ic4:TS * (ic4 + 1)], psN[ic4 % 2][:, :], AF.Silu)
                    scalar.drain()
                    s_act.inc(scalar.copy(gbuf[:, 0:1], gbuf[:, 0:1]), 1)
            for ic4 in range(4):
                ic = 4 * half + ic4
                mm_block(ctx.dram["s4_swu"], 0, 128 * ic, CH, psN[ic4 % 2], Y, tsl, "su", dve_lag=1)

                @blk.scalar
                def _(scalar, ic4=ic4):
                    scalar.wait_ge(s_mm.h, s_mm.val)
                    scalar.activation(ubuf[:, TS * ic4:TS * (ic4 + 1)], psN[ic4 % 2][:, :], AF.Copy)
                    scalar.drain()
                    s_act.inc(scalar.copy(ubuf[:, 0:1], ubuf[:, 0:1]), 1)

            @blk.vector
            def _(vector):
                vector.wait_ge(s_act.h, s_act.val)
                for ic4 in range(4):
                    vector.tensor_tensor(h_t[ic4], gbuf[:, TS * ic4:TS * (ic4 + 1)],
                                         ubuf[:, TS * ic4:TS * (ic4 + 1)], OP.mult)
                vector.drain()
                s_dve.inc(vector.tensor_copy(fnc[:, 0:1], fnc[:, 1:2]), 1)

            for oc in range(CH):
                mm_block(ctx.dram["s4_swd"], 512 * half, 128 * oc, 4, psN[2 + oc % 2],
                         h_t, slice(0, TS), "sd", dve_lag=(0 if oc == 0 else 1))

                @blk.vector
                def _(vector, oc=oc):
                    vector.wait_ge(s_mm.h, s_mm.val)
                    vector.tensor_tensor(yacc[oc], yacc[oc], psN[2 + oc % 2][:, :], OP.add)
                    vector.drain()
                    s_dve.inc(vector.tensor_copy(fnc[:, 0:1], fnc[:, 1:2]), 1)

        @blk.vector
        def _(vector, tt=tt, tsl=tsl):
            vector.wait_ge(s_dve.h, s_dve.val)
            for oc in range(CH):
                vector.tensor_scalar(gbuf[:, TS * (oc % 4):TS * (oc % 4 + 1)],
                                     yacc[oc], 0.5, None, OP.mult)
                if oc % 4 == 3 or oc == CH - 1:
                    vector.drain()
                    for oc2 in range(oc - oc % 4, oc + 1):
                        vector.tensor_tensor(X[oc2][:, tsl], X[oc2][:, tsl],
                                             gbuf[:, TS * (oc2 % 4):TS * (oc2 % 4 + 1)], OP.add)
                    vector.drain()
            s_dve.inc(vector.tensor_copy(fnc[:, 0:1], fnc[:, 1:2]), 1)

    ctx.barrier()


def emit_stub(ctx, stage):
    ctx.barrier()


STAGE_EMITTERS = {
    "s1": emit_ffn0,
    "s2": emit_s2,
    "s3": emit_ffn0,
    "s4": emit_s4,
    "s5": emit_s5,
    "s6": emit_s4,
}


def build_program(stages):
    nc = bass.Bass("TRN2", target_bir_lowering=False)

    io = {}
    io["x_in"] = nc.dram_tensor("x_in", [C, TOK], F32, kind="ExternalInput").ap()
    io["x_out"] = nc.dram_tensor("x_out", [C, TOK], F32, kind="ExternalOutput").ap()
    io["ident_in"] = nc.dram_tensor("ident_in", [128, 128], F32, kind="ExternalInput").ap()

    need_ffn0 = ("s1" in stages) or ("s3" in stages)
    if "s2" in stages:
        for nm, shp in (("Wk", (C, C)), ("Wv", (C, C)), ("Wr", (C, C)),
                        ("Wck", (C, CKDIM)), ("Wcv", (CKDIM, C)), ("Wcr", (C, C)),
                        ("ln1_g", (C,)), ("ln1_b", (C,)), ("ln2_g", (C,)), ("ln2_b", (C,)),
                        ("wkv_w", (128,)), ("wkv_u", (128,)), ("ts_pack", (TOK,))):
            io[nm] = nc.dram_tensor(nm, list(shp), F32, kind="ExternalInput").ap()
    if need_ffn0:
        for nm, shp in (("wg_a", (IFFN, C)), ("wu_a", (IFFN, C)), ("wd_a", (C, IFFN)),
                        ("wg_b", (C, IFFN)), ("wu_b", (C, IFFN)), ("wd_b", (IFFN, C)),
                        ("clip_x", (1,)), ("clip_down", (1,))):
            io[nm] = nc.dram_tensor(nm, list(shp), F32, kind="ExternalInput").ap()
    if ("s4" in stages) or ("s6" in stages):
        for nm, shp in (("s4_gatew", (C, EMOE)), ("s4_gateb", (EMOE,)),
                        ("s4_wg", (EMOE * C, IEXP)), ("s4_wu", (EMOE * C, IEXP)),
                        ("s4_wd", (EMOE * IEXP, C)), ("s4_swg", (C, C)),
                        ("s4_swu", (C, C)), ("s4_swd", (C, C))):
            io[nm] = nc.dram_tensor(nm, list(shp), F32, kind="ExternalInput").ap()
    if "s5" in stages:
        for nm, shp in (("s5_Wa", (C, C)), ("s5_Wx", (C, C)), ("s5_Wba", (C,)),
                        ("s5_Wbx", (C,)), ("s5_Lam", (C,))):
            io[nm] = nc.dram_tensor(nm, list(shp), F32, kind="ExternalInput").ap()

    with nc.Block() as block:
        ctx = Ctx(nc, block)
        ctx.dram.update(io)
        ctx.ident = ctx.sbuf("ident", (128, 128))
        s_io = ctx.sem("io")

        X = XTiles(ctx)
        ctx.sbuf("ABUF", (128, 16384))

        @block.sync
        def _(sync):
            s_io.inc(sync.dma_start(out=ctx.ident[:, :], in_=io["ident_in"][:, :]), 16)
            for c in range(CH):
                s_io.inc(sync.dma_start(out=X[c][:, :], in_=io["x_in"][128 * c:128 * (c + 1), :]), 16)

        @block.vector
        def _(vector):
            vector.wait_ge(s_io.h, s_io.val)
            vector.memset(ctx.sbuf("ones_col", (128, 1))[:, :], 1.0)

        if need_ffn0:
            clip_x = ctx.sbuf("clip_x_t", (1, 2))
            clip_d = ctx.sbuf("clip_d_t", (1, 2))

            @block.sync
            def _(sync):
                s_io.inc(sync.dma_start(out=clip_x[0:1, 0:1], in_=io["clip_x"][0:1]), 16)
                s_io.inc(sync.dma_start(out=clip_d[0:1, 0:1], in_=io["clip_down"][0:1]), 16)

        ctx.barrier()

        if need_ffn0:
            emit_ffn0_weight_quant(ctx)

        for st in stages:
            STAGE_EMITTERS[st](ctx, st)

        s_st = ctx.sem("st")

        @block.gpsimd
        def _(gp):
            for c in range(CH):
                s_st.inc(gp.dma_start(out=io["x_out"][128 * c:128 * (c + 1), :], in_=X[c][:, :]), 16)
            gp.wait_ge(s_st.h, s_st.val)

    return nc


def _core_inmap(inputs, core, x_full):
    sl = x_full[:, TL * core:TL * (core + 1), :]
    xfm = np.ascontiguousarray(sl.transpose(2, 0, 1).reshape(C, TOK))
    m = {
        "x_in": xfm,
        "ident_in": np.eye(128, dtype=np.float32),
        "wg_a": np.ascontiguousarray(inputs["L0_ffn_w_gate"]),
        "wu_a": np.ascontiguousarray(inputs["L0_ffn_w_up"]),
        "wd_a": np.ascontiguousarray(inputs["L0_ffn_w_down"]),
        "wg_b": np.ascontiguousarray(inputs["L0_ffn_w_gate"].T),
        "wu_b": np.ascontiguousarray(inputs["L0_ffn_w_up"].T),
        "wd_b": np.ascontiguousarray(inputs["L0_ffn_w_down"].T),
        "clip_x": np.ascontiguousarray(inputs["L0_ffn_clip_x"]),
        "clip_down": np.ascontiguousarray(inputs["L0_ffn_clip_down"]),
    }
    ts = inputs["L0_attn_time_shift"].reshape(C)
    ts_pack = np.zeros((128, 16), dtype=np.float32)
    for c in range(CH):
        ts_pack[:, 2 * c] = ts[128 * c:128 * (c + 1)]
        ts_pack[:, 2 * c + 1] = ts[128 * c:128 * (c + 1)]
    m.update({
        "Wk": np.ascontiguousarray(inputs["L0_attn_Wk"]),
        "Wv": np.ascontiguousarray(inputs["L0_attn_Wv"]),
        "Wr": np.ascontiguousarray(inputs["L0_attn_Wr"]),
        "Wck": np.ascontiguousarray(inputs["L0_attn_Wck"]),
        "Wcv": np.ascontiguousarray(inputs["L0_attn_Wcv"]),
        "Wcr": np.ascontiguousarray(inputs["L0_attn_Wcr"]),
        "ln1_g": np.ascontiguousarray(inputs["L0_attn_ln1_g"]),
        "ln1_b": np.ascontiguousarray(inputs["L0_attn_ln1_b"]),
        "ln2_g": np.ascontiguousarray(inputs["L0_attn_ln2_g"]),
        "ln2_b": np.ascontiguousarray(inputs["L0_attn_ln2_b"]),
        "wkv_w": np.ascontiguousarray(inputs["L0_attn_time_decay"][128 * core:128 * (core + 1)]),
        "wkv_u": np.ascontiguousarray(inputs["L0_attn_time_first"][128 * core:128 * (core + 1)]),
        "ts_pack": np.ascontiguousarray(ts_pack.reshape(TOK)),
    })
    m.update({
        "s4_gatew": np.ascontiguousarray(inputs["L1_ffn_gate_w"]),
        "s4_gateb": np.ascontiguousarray(inputs["L1_ffn_gate_bias"]),
        "s4_wg": np.ascontiguousarray(inputs["L1_ffn_w_gate"].reshape(EMOE * C, IEXP)),
        "s4_wu": np.ascontiguousarray(inputs["L1_ffn_w_up"].reshape(EMOE * C, IEXP)),
        "s4_wd": np.ascontiguousarray(inputs["L1_ffn_w_down"].reshape(EMOE * IEXP, C)),
        "s4_swg": np.ascontiguousarray(inputs["L1_ffn_sw_gate"]),
        "s4_swu": np.ascontiguousarray(inputs["L1_ffn_sw_up"]),
        "s4_swd": np.ascontiguousarray(inputs["L1_ffn_sw_down"]),
    })
    m.update({
        "s5_Wa": np.ascontiguousarray(inputs["L1_attn_Wa"]),
        "s5_Wx": np.ascontiguousarray(inputs["L1_attn_Wx"]),
        "s5_Wba": np.ascontiguousarray(inputs["L1_attn_ba"]),
        "s5_Wbx": np.ascontiguousarray(inputs["L1_attn_bx"]),
        "s5_Lam": np.ascontiguousarray(inputs["L1_attn_Lam"]),
    })
    return m


def kernel(**inputs):
    inputs = {k: np.asarray(v, dtype=np.float32) for k, v in inputs.items()}
    x = inputs["L0_attn_x"]

    for stages in SPLIT:
        nc = build_program(stages)
        in_maps = [_core_inmap(inputs, i, x) for i in range(NCORE)]
        res = run_bass_kernel_spmd(nc, in_maps, core_ids=list(range(NCORE)))
        parts = []
        for i in range(NCORE):
            xo = np.asarray(res.results[i]["x_out"])
            parts.append(xo.reshape(C, B, TL).transpose(1, 2, 0))
        x = np.concatenate(parts, axis=1)

    return np.ascontiguousarray(x)


# version 04
# speedup vs baseline: 1.2578x; 1.0477x over previous
import math
import numpy as np
import concourse.bass as bass
import concourse.mybir as mybir
from concourse.bass_utils import run_bass_kernel_spmd

F32 = mybir.dt.float32
F16 = mybir.dt.float16
I32 = mybir.dt.int32
U8 = mybir.dt.uint8
OP = mybir.AluOpType
AF = mybir.ActivationFunctionType
AX = mybir.AxisListType

NCORE = 8
B = 2
T = 8192
C = 1024
CH = 8
TOK = 2048
TL = 1024
NT = 4
TS = 512
IFFN = 2816
ICH = 22
CKDIM = 4096
CKCH = 32
EMOE = 8
IEXP = 512
LN_EPS = 1e-5
QMAX = 127.0
RCH = 8
RB = 8

ALL_STAGES = ["s1", "s2", "s3", "s4", "s5", "s6"]
S2_TRUNC = None
SPLIT = [ALL_STAGES]
DEBUG = False


class Sem:
    def __init__(self, handle):
        self.h = handle
        self.val = 0

    def inc(self, instr, n=1):
        instr.then_inc(self.h, n)
        self.val += n
        return instr


class Ctx:
    def __init__(self, nc, block):
        self.nc = nc
        self.block = block
        self.sems = {}
        self.sb = {}
        self.ps = {}
        self.dram = {}
        self._cms = {}
        self.ident = None

    def sem(self, name):
        if name not in self.sems:
            cm = self.nc.semaphore(name)
            self.sems[name] = Sem(cm.__enter__())
            self._cms["sem_" + name] = cm
        return self.sems[name]

    def sbuf(self, name, shape, dtype=F32):
        if name not in self.sb:
            cm = self.nc.sbuf_tensor(name, list(shape), dtype)
            t = cm.__enter__()
            self._cms["sb_" + name] = cm
            self.sb[name] = t.ap() if hasattr(t, "ap") else t
        return self.sb[name]

    def free_sbuf(self, *names):
        for name in names:
            if name in self.sb:
                self._cms.pop("sb_" + name).__exit__(None, None, None)
                del self.sb[name]

    def psum(self, name, shape, dtype=F32):
        if name not in self.ps:
            cm = self.nc.psum_tensor(name, list(shape), dtype)
            t = cm.__enter__()
            self._cms["ps_" + name] = cm
            self.ps[name] = t.ap() if hasattr(t, "ap") else t
        return self.ps[name]

    def dramt(self, name, shape, dtype=F32, kind="Internal"):
        if name not in self.dram:
            self.dram[name] = self.nc.dram_tensor(name, list(shape), dtype, kind=kind).ap()
        return self.dram[name]

    def tap(self, name, src_ap, shape=None, dtype=F32):
        if not DEBUG:
            return
        blk = self.block
        shape = list(shape if shape is not None else src_ap.shape)
        d = self.dramt("tap_" + name, shape, dtype, kind="ExternalOutput")
        s = self.sem("st")
        self.barrier()

        @blk.gpsimd
        def _(gp):
            s.inc(gp.dma_start(out=d[tuple(slice(0, n) for n in shape)], in_=src_ap), 16)
            gp.wait_ge(s.h, s.val)

        self.barrier()

    def barrier(self):
        blk = self.block
        f = self.sem("flow")
        engines = ["vector", "scalar", "tensor", "gpsimd", "sync"]
        base = f.val
        for e in engines:
            @getattr(blk, e)
            def _(eng, e=e):
                if e in ("vector", "scalar", "gpsimd"):
                    eng.drain()
                eng.nop().then_inc(f.h, 1)
        tgt = base + len(engines)
        f.val = tgt
        for e in engines:
            @getattr(blk, e)
            def _(eng):
                eng.wait_ge(f.h, tgt)


def XTiles(ctx):
    return [ctx.sbuf(f"X{c}", (128, TOK)) for c in range(CH)]


def ATiles(ctx):
    return [ctx.sbuf(f"A{c}", (128, TOK)) for c in range(CH)]


def emit_ln_stats(ctx, src):
    nc, blk = ctx.nc, ctx.block
    nsl = TOK // 128
    ones = ctx.sbuf("ones_col", (128, 1))
    sq = ctx.sbuf("SCR2", (128, TOK))
    stat = ctx.sbuf("ln_stat", (128, 64))
    ps_stat = ctx.psum("ps_stat", (128, 64))
    psT = ctx.psum("psT", (128, 128))
    statT = ctx.sbuf("ln_statT", (64, 128))
    rstd_b = ctx.sbuf("SCR0", (128, TOK))
    mrstd_b = ctx.sbuf("SCR1", (128, TOK))
    row_dram = ctx.dramt("ln_rows", (64 * 128,))
    s_dve = ctx.sem("dve")
    s_mm = ctx.sem("mm")
    s_act = ctx.sem("act")
    s_io = ctx.sem("io")

    ctx.barrier()
    dve_base, mm_base = s_dve.val, s_mm.val

    for c in range(CH):
        @blk.vector
        def _(vector, c=c):
            if c > 0:
                vector.wait_ge(s_mm.h, mm_base + c)
            s_dve.inc(vector.tensor_tensor(sq[:, :], src[c][:, :], src[c][:, :], OP.mult), 1)

        @blk.tensor
        def _(tensor, c=c):
            tensor.wait_ge(s_dve.h, dve_base + c + 1)
            for j in range(nsl):
                tensor.matmul(ps_stat[:, j:j + 1], src[c][:, 128 * j:128 * (j + 1)],
                              ones[:, 0:1], start=(c == 0 and j == 0), stop=False)
            for j in range(nsl):
                i = tensor.matmul(ps_stat[:, 32 + j:33 + j], sq[:, 128 * j:128 * (j + 1)],
                                  ones[:, 0:1], start=False,
                                  stop=(c == CH - 1 and j == nsl - 1))
            s_mm.inc(i, 1)

    @blk.vector
    def _(vector):
        vector.wait_ge(s_mm.h, mm_base + CH)
        vector.tensor_scalar(stat[:, 0:nsl], ps_stat[:, 0:nsl], 1.0 / C, None, OP.mult)
        vector.tensor_scalar(stat[:, 32:32 + nsl], ps_stat[:, 32:32 + nsl], 1.0 / C, None, OP.mult)
        vector.tensor_tensor(stat[:, 48:48 + nsl], stat[:, 0:nsl], stat[:, 0:nsl], OP.mult)
        vector.drain()
        vector.tensor_tensor(stat[:, 32:32 + nsl], stat[:, 32:32 + nsl], stat[:, 48:48 + nsl], OP.subtract)
        vector.drain()
        s_dve.inc(vector.tensor_scalar(stat[:, 32:32 + nsl], stat[:, 32:32 + nsl], 1.0,
                                       LN_EPS, OP.mult, OP.add), 1)

    @blk.scalar
    def _(scalar):
        scalar.wait_ge(s_dve.h, s_dve.val)
        scalar.activation(stat[:, 32:32 + nsl], stat[:, 32:32 + nsl], AF.Sqrt)
        scalar.drain()
        s_act.inc(scalar.copy(stat[:, 32:33], stat[:, 32:33]), 1)

    @blk.vector
    def _(vector):
        vector.wait_ge(s_act.h, s_act.val)
        vector.reciprocal(stat[:, 16:16 + nsl], stat[:, 32:32 + nsl])
        vector.drain()
        s_dve.inc(vector.tensor_tensor(stat[:, 0:nsl], stat[:, 0:nsl], stat[:, 16:16 + nsl], OP.mult), 1)

    @blk.tensor
    def _(tensor):
        tensor.wait_ge(s_dve.h, s_dve.val)
        s_mm.inc(tensor.transpose(psT[0:32, :], stat[:, 0:32], ctx.ident[:, :]), 1)

    @blk.vector
    def _(vector):
        vector.wait_ge(s_mm.h, s_mm.val)
        s_dve.inc(vector.tensor_copy(statT[0:32, :], psT[0:32, :]), 1)

    @blk.gpsimd
    def _(gp):
        gp.wait_ge(s_dve.h, s_dve.val)
        s_io.inc(gp.dma_start(out=row_dram[0:32 * 128], in_=statT[0:32, :]), 16)
        gp.wait_ge(s_io.h, s_io.val)
        s_io.inc(gp.dma_start(out=mrstd_b[:, :], in_=row_dram[0:TOK].unsqueeze(0).broadcast_to([128, TOK])), 16)
        s_io.inc(gp.dma_start(out=rstd_b[:, :], in_=row_dram[16 * 128:16 * 128 + TOK].unsqueeze(0).broadcast_to([128, TOK])), 16)

    @blk.vector
    def _(vector):
        vector.wait_ge(s_io.h, s_io.val)
        vector.nop()

    ctx.barrier()


def emit_ln_apply(ctx, vector, writes, src, gamma=None, beta=None):
    rstd_b = ctx.sb["SCR0"]
    mrstd_b = ctx.sb["SCR1"]
    for c in range(CH):
        for d_ap, s_sl in writes[c]:
            vector.tensor_tensor(d_ap, src[c][:, s_sl], rstd_b[:, s_sl], OP.mult)
    for c in range(CH):
        for d_ap, s_sl in writes[c]:
            i = vector.tensor_tensor(d_ap, d_ap, mrstd_b[:, s_sl], OP.subtract)
    if gamma is not None:
        for c in range(CH):
            for d_ap, s_sl in writes[c]:
                i = vector.tensor_scalar(d_ap, d_ap, gamma[:, c:c + 1],
                                         beta[:, c:c + 1], OP.mult, OP.add)
    return i


def emit_ln(ctx, src, dst, gamma=None, beta=None):
    emit_ln_stats(ctx, src)
    blk = ctx.block
    s_dve = ctx.sem("dve")

    @blk.vector
    def _(vector):
        writes = [[(dst[c][:, :], slice(0, TOK))] for c in range(CH)]
        s_dve.inc(emit_ln_apply(ctx, vector, writes, src, gamma, beta), 1)

    ctx.barrier()


def emit_absmax_allreduce(ctx, tiles, cc_tag):
    nc, blk = ctx.nc, ctx.block
    red = ctx.sbuf("red32", (128, 48))
    redT = ctx.sbuf("redT", (1, 128))
    psT = ctx.psum("psT", (128, 128))
    cc_in = ctx.dramt(f"cc_{cc_tag}_in", (8,))
    cc_out = ctx.dramt(f"cc_{cc_tag}_out", (8,))
    s_dve = ctx.sem("dve")
    s_mm = ctx.sem("mm")
    s_io = ctx.sem("io")
    s_cc = ctx.sem("cc")

    ctx.barrier()

    @blk.vector
    def _(vector):
        n = len(tiles)
        for i, t in enumerate(tiles):
            vector.tensor_reduce(red[:, i:i + 1], t, AX.X, OP.max, apply_absolute_value=True)
        step = 1
        while step < n:
            vector.drain()
            for i in range(0, n - step, 2 * step):
                vector.tensor_tensor(red[:, i:i + 1], red[:, i:i + 1],
                                     red[:, i + step:i + step + 1], OP.max)
            step *= 2
        vector.drain()
        s_dve.inc(vector.nop(), 1)

    @blk.tensor
    def _(tensor):
        tensor.wait_ge(s_dve.h, s_dve.val)
        s_mm.inc(tensor.transpose(psT[0:1, :], red[:, 0:1], ctx.ident[:, :]), 1)

    @blk.vector
    def _(vector):
        vector.wait_ge(s_mm.h, s_mm.val)
        vector.tensor_copy(redT[0:1, :], psT[0:1, :])
        vector.drain()
        s_dve.inc(vector.tensor_reduce(redT[0:1, 0:1], redT[0:1, :], AX.X, OP.max), 1)

    @blk.gpsimd
    def _(gp):
        gp.wait_ge(s_dve.h, s_dve.val)
        s_io.inc(gp.dma_start(out=cc_in[0:1], in_=redT[0:1, 0:1]), 16)
        gp.wait_ge(s_io.h, s_io.val)
        s_cc.inc(gp.collective_compute(
            "AllReduce", OP.max, replica_groups=[list(range(NCORE))],
            ins=[cc_in[0:1]], outs=[cc_out[0:1]]), 1)
        gp.wait_ge(s_cc.h, s_cc.val)

    ctx.barrier()


def emit_quant_scales(ctx, cc_tag, clip_tile, prefix):
    nc, blk = ctx.nc, ctx.block
    sc = ctx.sbuf(f"{prefix}_tmp", (1, 4))
    inv_b = ctx.sbuf(f"{prefix}_inv", (128, 1))
    s_b = ctx.sbuf(f"{prefix}_s", (128, 1))
    sc_dram = ctx.dramt(f"{prefix}_dram", (4,))
    s_io = ctx.sem("io")
    s_dve = ctx.sem("dve")
    s_act = ctx.sem("act")

    @blk.gpsimd
    def _(gp):
        s_io.inc(gp.dma_start(out=sc[0:1, 0:1], in_=ctx.dram[f"cc_{cc_tag}_out"][0:1]), 16)

    @blk.scalar
    def _(scalar):
        scalar.wait_ge(s_io.h, s_io.val)
        scalar.activation(sc[0:1, 1:2], clip_tile, AF.Sigmoid)
        scalar.drain()
        s_act.inc(scalar.copy(sc[0:1, 1:2], sc[0:1, 1:2]), 1)

    @blk.vector
    def _(vector):
        vector.wait_ge(s_act.h, s_act.val)
        vector.tensor_tensor(sc[0:1, 0:1], sc[0:1, 0:1], sc[0:1, 1:2], OP.mult)
        vector.drain()
        vector.tensor_scalar(sc[0:1, 0:1], sc[0:1, 0:1], 1.0 / QMAX, None, OP.mult)
        vector.drain()
        s_dve.inc(vector.reciprocal(sc[0:1, 1:2], sc[0:1, 0:1]), 1)

    @blk.gpsimd
    def _(gp):
        gp.wait_ge(s_dve.h, s_dve.val)
        s_io.inc(gp.dma_start(out=sc_dram[0:2], in_=sc[0:1, 0:2]), 16)
        gp.wait_ge(s_io.h, s_io.val)
        s_io.inc(gp.dma_start(out=s_b[:, 0:1], in_=sc_dram[0:1].unsqueeze(0).broadcast_to([128, 1])), 16)
        s_io.inc(gp.dma_start(out=inv_b[:, 0:1], in_=sc_dram[1:2].unsqueeze(0).broadcast_to([128, 1])), 16)
        gp.wait_ge(s_io.h, s_io.val)

    ctx.barrier()


def round_ops(vector, dst, src, inv_ap, qi, qf, qg, drains=True):
    w = src.shape[-1]
    D = (lambda: vector.drain()) if drains else (lambda: None)
    vector.tensor_scalar(qf[:, 0:w], src, inv_ap, None, OP.mult)
    D()
    vector.tensor_copy(qi[:, 0:w], qf[:, 0:w])
    D()
    vector.tensor_tensor(qg[:, 0:w], qf[:, 0:w], qi[:, 0:w], OP.subtract)
    D()
    vector.tensor_scalar(qf[:, 0:w], qg[:, 0:w], 0.5, None, OP.is_gt)
    vector.tensor_scalar(qg[:, 0:w], qg[:, 0:w], -0.5, None, OP.is_lt)
    D()
    vector.tensor_tensor(qf[:, 0:w], qf[:, 0:w], qi[:, 0:w], OP.add)
    D()
    vector.tensor_tensor(qf[:, 0:w], qf[:, 0:w], qg[:, 0:w], OP.subtract)
    D()
    return vector.tensor_scalar(dst, qf[:, 0:w], -128.0, 127.0, OP.max, OP.min)


def quant_ops(vector, dst, src, inv_ap, qi, qf, qg):
    return round_ops(vector, dst, src, inv_ap, qi, qf, qg, drains=True)


def quant2_ops(vector, dst_a, src_a, dst_b, src_b, inv_ap, qia, qfa, qga, qib, qfb, qgb):
    wa = src_a.shape[-1]
    wb = src_b.shape[-1]
    vector.tensor_scalar(qfa[:, 0:wa], src_a, inv_ap, None, OP.mult)
    vector.tensor_scalar(qfb[:, 0:wb], src_b, inv_ap, None, OP.mult)
    vector.tensor_copy(qia[:, 0:wa], qfa[:, 0:wa])
    vector.tensor_copy(qib[:, 0:wb], qfb[:, 0:wb])
    vector.tensor_tensor(qga[:, 0:wa], qfa[:, 0:wa], qia[:, 0:wa], OP.subtract)
    vector.tensor_tensor(qgb[:, 0:wb], qfb[:, 0:wb], qib[:, 0:wb], OP.subtract)
    vector.tensor_scalar(qfa[:, 0:wa], qga[:, 0:wa], 0.5, None, OP.is_gt)
    vector.tensor_scalar(qfb[:, 0:wb], qgb[:, 0:wb], 0.5, None, OP.is_gt)
    vector.tensor_scalar(qga[:, 0:wa], qga[:, 0:wa], -0.5, None, OP.is_lt)
    vector.tensor_scalar(qgb[:, 0:wb], qgb[:, 0:wb], -0.5, None, OP.is_lt)
    vector.tensor_tensor(qfa[:, 0:wa], qfa[:, 0:wa], qia[:, 0:wa], OP.add)
    vector.tensor_tensor(qfb[:, 0:wb], qfb[:, 0:wb], qib[:, 0:wb], OP.add)
    vector.tensor_tensor(qfa[:, 0:wa], qfa[:, 0:wa], qga[:, 0:wa], OP.subtract)
    vector.tensor_tensor(qfb[:, 0:wb], qfb[:, 0:wb], qgb[:, 0:wb], OP.subtract)
    vector.tensor_scalar(dst_a, qfa[:, 0:wa], -128.0, 127.0, OP.max, OP.min)
    return vector.tensor_scalar(dst_b, qfb[:, 0:wb], -128.0, 127.0, OP.max, OP.min)


def emit_ffn0_weight_quant(ctx):
    nc, blk = ctx.nc, ctx.block
    s_io = ctx.sem("io")
    s_dve = ctx.sem("dve")
    s_st = ctx.sem("st")
    AB = ctx.sbuf("ABUF", (128, 16384))
    wa = AB[:, 0:2816]
    wb = AB[:, 2816:5632]
    inv_t = AB[:, 5632:8448]
    qi = AB.bitcast(I32)[:, 8448:11264]
    wg_ = AB[:, 11264:14080]
    wb16 = AB.bitcast(F16)[:, 28160:30976]

    ctx.barrier()
    specs = (
        (IFFN, C, "wg_a", "wg_b", "wq_gate", "wsc_g"),
        (IFFN, C, "wu_a", "wu_b", "wq_up", "wsc_u"),
        (C, IFFN, "wd_a", "wd_b", "wq_down", "wsc_d"),
    )
    for rows, cols, a_in, b_in, q_out, sct in specs:
        a = ctx.dram[a_in]
        b = ctx.dram[b_in]
        q = ctx.dramt(q_out, (cols, rows), F16)
        scd = ctx.dramt(f"{q_out}_sc", (rows,))
        sc_tile = ctx.sbuf(sct, (128, rows // 128))
        np_r = rows // 128

        for g in range(np_r):
            @blk.sync
            def _(sync, g=g, a=a, cols=cols):
                sync.wait_ge(s_dve.h, s_dve.val)
                s_io.inc(sync.dma_start(out=wa[:, 0:cols], in_=a[128 * g:128 * (g + 1), 0:cols]), 16)

            @blk.vector
            def _(vector, g=g, cols=cols, sc_tile=sc_tile):
                vector.wait_ge(s_io.h, s_io.val)
                vector.tensor_reduce(sc_tile[:, g:g + 1], wa[:, 0:cols], AX.X, OP.max,
                                     apply_absolute_value=True)
                vector.drain()
                s_dve.inc(vector.tensor_scalar(sc_tile[:, g:g + 1], sc_tile[:, g:g + 1],
                                               1.0 / QMAX, None, OP.mult), 1)

        @blk.gpsimd
        def _(gp, scd=scd, np_r=np_r, sc_tile=sc_tile):
            gp.wait_ge(s_dve.h, s_dve.val)
            for g in range(np_r):
                s_st.inc(gp.dma_start(out=scd[128 * g:128 * (g + 1)], in_=sc_tile[:, g:g + 1]), 16)
            gp.wait_ge(s_st.h, s_st.val)

        ctx.barrier()

        @blk.gpsimd
        def _(gp, scd=scd, rows=rows):
            s_io.inc(gp.dma_start(out=inv_t[:, 0:rows],
                                  in_=scd[0:rows].unsqueeze(0).broadcast_to([128, rows])), 16)

        @blk.vector
        def _(vector, rows=rows):
            vector.wait_ge(s_io.h, s_io.val)
            s_dve.inc(vector.reciprocal(inv_t[:, 0:rows], inv_t[:, 0:rows]), 1)
            vector.drain()

        for ct in range(cols // 128):
            @blk.sync
            def _(sync, ct=ct, b=b, rows=rows):
                sync.wait_ge(s_dve.h, s_dve.val)
                s_io.inc(sync.dma_start(out=wb[:, 0:rows], in_=b[128 * ct:128 * (ct + 1), 0:rows]), 16)

            @blk.vector
            def _(vector, rows=rows):
                vector.wait_ge(s_io.h, s_io.val)
                vector.wait_ge(s_st.h, s_st.val)
                vector.tensor_tensor(wb[:, 0:rows], wb[:, 0:rows], inv_t[:, 0:rows], OP.mult)
                vector.drain()
                vector.tensor_copy(qi[:, 0:rows], wb[:, 0:rows])
                vector.drain()
                vector.tensor_tensor(wg_[:, 0:rows], wb[:, 0:rows], qi[:, 0:rows], OP.subtract)
                vector.drain()
                vector.tensor_scalar(wb[:, 0:rows], wg_[:, 0:rows], 0.5, None, OP.is_gt)
                vector.tensor_scalar(wg_[:, 0:rows], wg_[:, 0:rows], -0.5, None, OP.is_lt)
                vector.drain()
                vector.tensor_tensor(wb[:, 0:rows], wb[:, 0:rows], qi[:, 0:rows], OP.add)
                vector.drain()
                vector.tensor_tensor(wb[:, 0:rows], wb[:, 0:rows], wg_[:, 0:rows], OP.subtract)
                vector.drain()
                s_dve.inc(vector.tensor_scalar(wb16[:, 0:rows], wb[:, 0:rows], -128.0, 127.0,
                                               OP.max, OP.min), 1)

            @blk.gpsimd
            def _(gp, ct=ct, q=q, rows=rows):
                gp.wait_ge(s_dve.h, s_dve.val)
                s_st.inc(gp.dma_start(out=q[128 * ct:128 * (ct + 1), 0:rows], in_=wb16[:, 0:rows]), 16)

    @blk.gpsimd
    def _(gp):
        gp.wait_ge(s_st.h, s_st.val)

    ctx.barrier()


def emit_ffn0(ctx, stage):
    nc, blk = ctx.nc, ctx.block
    X = XTiles(ctx)
    AB = ctx.sbuf("ABUF", (128, 16384))
    Y = [AB[:, 2048 * c:2048 * (c + 1)] for c in range(CH)]
    emit_ln(ctx, X, Y)
    ctx.tap(f"{stage}_ln0", Y[0])
    ctx.tap(f"{stage}_rstdb", ctx.sb["SCR0"][:, :])
    ctx.tap(f"{stage}_mrstdb", ctx.sb["SCR1"][:, :])

    emit_absmax_allreduce(ctx, [y for y in Y], f"{stage}x")
    clip_x = ctx.sbuf("clip_x_t", (1, 2))
    emit_quant_scales(ctx, f"{stage}x", clip_x[0:1, 0:1], f"{stage}qx")

    s_dve = ctx.sem("dve")
    s_io = ctx.sem("io")
    s_mm = ctx.sem("mm")
    s_act = ctx.sem("act")
    s_st = ctx.sem("st")
    SCR0a = ctx.sbuf("SCR0", (128, TOK))
    SCR1a = ctx.sbuf("SCR1", (128, TOK))
    qi = SCR0a.bitcast(I32)[:, 0:TS]
    qf = SCR0a[:, TS:2 * TS]
    qg = SCR0a[:, 2 * TS:3 * TS]
    qib = SCR0a.bitcast(I32)[:, 3 * TS:4 * TS]
    qfb = SCR1a[:, 0:TS]
    qgb = SCR1a[:, TS:2 * TS]
    inv_x = ctx.sb[f"{stage}qx_inv"]
    sx = ctx.sb[f"{stage}qx_s"]
    XQ = [ctx.sbuf(f"XQ{c}", (128, TOK), F16) for c in range(CH)]

    @blk.vector
    def _(vector):
        subs = [(c, nt) for c in range(CH) for nt in range(NT)]
        for j in range(0, len(subs), 2):
            (ca, na), (cb, nb) = subs[j], subs[j + 1]
            sa = slice(TS * na, TS * (na + 1))
            sb = slice(TS * nb, TS * (nb + 1))
            i = quant2_ops(vector, XQ[ca][:, sa], Y[ca][:, sa],
                           XQ[cb][:, sb], Y[cb][:, sb], inv_x[:, 0:1],
                           qi, qf, qg, qib, qfb, qgb)
        s_dve.inc(i, 1)

    wsxg = ctx.sbuf("wsxg", (128, ICH))
    wsxu = ctx.sbuf("wsxu", (128, ICH))
    wsxd = ctx.sbuf("wsxd", (128, CH))

    @blk.vector
    def _(vector):
        vector.tensor_scalar(wsxg[:, :], ctx.sb["wsc_g"][:, :], sx[:, 0:1], None, OP.mult)
        s_dve.inc(vector.tensor_scalar(wsxu[:, :], ctx.sb["wsc_u"][:, :], sx[:, 0:1], None, OP.mult), 1)

    ctx.barrier()
    ctx.tap(f"{stage}_sc", ctx.sb[f"{stage}qx_tmp"][0:1, 0:2], (1, 2))
    ctx.tap(f"{stage}_xq0", XQ[0][:, :], (128, TOK), F16)
    ctx.tap(f"{stage}_wsxg", wsxg[:, :])

    wq_g = ctx.dram["wq_gate"]
    wq_u = ctx.dram["wq_up"]
    h_spill = ctx.dramt("h_spill", (IFFN, TOK))
    wt3s = ctx.sbuf("wt3", (128, CH * 128))
    wtg = wt3s.bitcast(F16)[:, 0:CH * 128]
    wtu = wt3s.bitcast(F16)[:, CH * 128:2 * CH * 128]
    go = ctx.sbuf("SCR0", (128, TOK))
    uo = ctx.sbuf("SCR1", (128, TOK))
    ho = ctx.sbuf("SCR2", (128, TOK))
    hmax = ctx.sbuf("ffn_hmax", (128, 68))
    psN = [ctx.psum(f"ps{i}", (128, TS)) for i in range(4)]

    for o in range(ICH):
        @blk.sync
        def _(sync, o=o):
            if o > 0:
                sync.wait_ge(s_mm.h, s_mm.val)
            for k in range(CH):
                s_io.inc(sync.dma_start(out=wtg[:, 128 * k:128 * (k + 1)],
                                        in_=wq_g[128 * k:128 * (k + 1), 128 * o:128 * (o + 1)]), 16)
                s_io.inc(sync.dma_start(out=wtu[:, 128 * k:128 * (k + 1)],
                                        in_=wq_u[128 * k:128 * (k + 1), 128 * o:128 * (o + 1)]), 16)

        @blk.tensor
        def _(tensor, o=o):
            tensor.wait_ge(s_io.h, s_io.val)
            tensor.wait_ge(s_dve.h, s_dve.val)
            for nt in range(NT):
                for k in range(CH):
                    i = tensor.matmul(psN[nt][:, :], wtg[:, 128 * k:128 * (k + 1)],
                                      XQ[k][:, TS * nt:TS * (nt + 1)],
                                      start=(k == 0), stop=(k == CH - 1))
            s_mm.inc(i, 1)

        @blk.vector
        def _(vector, o=o):
            vector.wait_ge(s_mm.h, s_mm.val)
            vector.wait_ge(s_st.h, s_st.val)
            for nt in range(NT):
                i = vector.tensor_scalar(go[:, TS * nt:TS * (nt + 1)], psN[nt][:, :],
                                         wsxg[:, o:o + 1], None, OP.mult)
            s_dve.inc(i, 1)

        @blk.tensor
        def _(tensor, o=o):
            tensor.wait_ge(s_dve.h, s_dve.val)
            for nt in range(NT):
                for k in range(CH):
                    i = tensor.matmul(psN[nt][:, :], wtu[:, 128 * k:128 * (k + 1)],
                                      XQ[k][:, TS * nt:TS * (nt + 1)],
                                      start=(k == 0), stop=(k == CH - 1))
            s_mm.inc(i, 1)

        @blk.scalar
        def _(scalar, o=o):
            scalar.wait_ge(s_dve.h, s_dve.val)
            scalar.activation(go[:, :], go[:, :], AF.Silu)
            scalar.drain()
            s_act.inc(scalar.copy(go[:, 0:1], go[:, 0:1]), 1)

        @blk.vector
        def _(vector, o=o):
            vector.wait_ge(s_mm.h, s_mm.val)
            for nt in range(NT):
                vector.tensor_scalar(uo[:, TS * nt:TS * (nt + 1)], psN[nt][:, :],
                                     wsxu[:, o:o + 1], None, OP.mult)
            vector.wait_ge(s_act.h, s_act.val)
            vector.wait_ge(s_st.h, s_st.val)
            for nt in range(NT):
                sl = slice(TS * nt, TS * (nt + 1))
                vector.tensor_tensor(ho[:, sl], go[:, sl], uo[:, sl], OP.mult)
            vector.tensor_reduce(hmax[:, 2 * o:2 * o + 1], ho[:, 0:TOK // 2], AX.X,
                                 OP.max, apply_absolute_value=True)
            i = vector.tensor_reduce(hmax[:, 2 * o + 1:2 * o + 2], ho[:, TOK // 2:TOK], AX.X,
                                     OP.max, apply_absolute_value=True)
            s_dve.inc(i, 1)

        @blk.gpsimd
        def _(gp, o=o):
            gp.wait_ge(s_dve.h, s_dve.val)
            s_st.inc(gp.dma_start(out=h_spill[128 * o:128 * (o + 1), :], in_=ho[:, :]), 16)

    ctx.barrier()
    ctx.tap(f"{stage}_h21", ho[:, :])
    ctx.tap(f"{stage}_hmax", hmax[:, 0:2 * ICH], (128, 2 * ICH))

    emit_absmax_allreduce(ctx, [hmax[:, i:i + 1] for i in range(2 * ICH)], f"{stage}h")
    clip_d = ctx.sbuf("clip_d_t", (1, 2))
    emit_quant_scales(ctx, f"{stage}h", clip_d[0:1, 0:1], f"{stage}qh")
    inv_h = ctx.sb[f"{stage}qh_inv"]
    sh = ctx.sb[f"{stage}qh_s"]

    @blk.vector
    def _(vector):
        vector.tensor_scalar(wsxd[:, :], ctx.sb["wsc_d"][:, :], sh[:, 0:1], None, OP.mult)
        vector.drain()
        s_dve.inc(vector.tensor_scalar(wsxd[:, :], wsxd[:, :], 0.5, None, OP.mult), 1)

    ctx.barrier()

    wq_d = ctx.dram["wq_down"]
    HQ = AB.bitcast(F16)[:, 0:ICH * TS]
    wtd = AB.bitcast(F16)[:, 11264:11264 + ICH * 128]
    hbufs = [AB[:, 7040:7552], AB[:, 8064:8576]]
    dq = AB[:, 7552:8064]

    for nt in range(NT):
        sl = slice(TS * nt, TS * (nt + 1))

        for kk in range(ICH):
            hbuf = hbufs[kk % 2]

            @blk.sync
            def _(sync, kk=kk, nt=nt, sl=sl, hbuf=hbuf):
                sync.wait_ge(s_dve.h, max(0, s_dve.val - 1))
                s_io.inc(sync.dma_start(out=hbuf, in_=h_spill[128 * kk:128 * (kk + 1), sl]), 16)

            @blk.vector
            def _(vector, kk=kk, hbuf=hbuf):
                vector.wait_ge(s_io.h, s_io.val)
                vector.wait_ge(s_mm.h, s_mm.val)
                s_dve.inc(quant_ops(vector, HQ[:, TS * kk:TS * (kk + 1)], hbuf,
                                    inv_h[:, 0:1], qi, qf, qg), 1)
                vector.drain()

        for oo in range(CH):
            @blk.sync
            def _(sync, oo=oo):
                sync.wait_ge(s_mm.h, s_mm.val)
                for kk in range(ICH):
                    s_io.inc(sync.dma_start(out=wtd[:, 128 * kk:128 * (kk + 1)],
                                            in_=wq_d[128 * kk:128 * (kk + 1), 128 * oo:128 * (oo + 1)]), 16)

            @blk.tensor
            def _(tensor, oo=oo):
                tensor.wait_ge(s_io.h, s_io.val)
                tensor.wait_ge(s_dve.h, s_dve.val)
                for kk in range(ICH):
                    i = tensor.matmul(psN[0][:, :], wtd[:, 128 * kk:128 * (kk + 1)],
                                      HQ[:, TS * kk:TS * (kk + 1)],
                                      start=(kk == 0), stop=(kk == ICH - 1))
                s_mm.inc(i, 1)

            @blk.vector
            def _(vector, oo=oo, sl=sl):
                vector.wait_ge(s_mm.h, s_mm.val)
                vector.tensor_scalar(dq, psN[0][:, :], wsxd[:, oo:oo + 1], None, OP.mult)
                vector.drain()
                s_dve.inc(vector.tensor_tensor(X[oo][:, sl], X[oo][:, sl], dq, OP.add), 1)

    ctx.barrier()




def _switch_pick(ctx, gp, dst_ap, blocks, ts_dram_ap, sem, n_dma=1):
    base = sem.val
    for arm in gp.switch_core_id(n=128):
        r = arm.logical % NCORE
        if r == 0:
            sem.inc(gp.dma_start(out=dst_ap, in_=ts_dram_ap), 16)
        else:
            sem.inc(gp.dma_start(out=dst_ap, in_=blocks(r - 1)), 16)
    sem.val = base + 16


def emit_hs_tile(ctx, vector, hs_tiles, srcY, b, tti, bnd_tile, bnd_cols, gamma, beta):
    base = b * TL + tti * TS
    if tti == 0:
        writes = [[(hs_tiles[c][:, 1:TS], slice(base, base + TS - 1))] for c in range(CH)]
    else:
        writes = [[(hs_tiles[c][:, 0:TS], slice(base - 1, base + TS - 1))] for c in range(CH)]
    i = emit_ln_apply(ctx, vector, writes, srcY, gamma, beta)
    if tti == 0:
        for c in range(CH):
            i = vector.tensor_copy(hs_tiles[c][:, 0:1], bnd_tile[:, bnd_cols(c):bnd_cols(c) + 1])
    return i


def emit_hs_tile_sp(ctx, vector, hs_tiles, stripes, b, tti, bnd_tile, bnd_cols, gamma, beta):
    rstd_b = ctx.sb["SCR0"]
    mrstd_b = ctx.sb["SCR1"]
    base = b * TL + tti * TS
    a = 1 if tti == 0 else 0
    w = TS - a
    s0 = base - 1 + a
    for c in range(CH):
        vector.tensor_tensor(hs_tiles[c][:, a:TS], stripes[c][:, a:TS],
                             rstd_b[:, s0:s0 + w], OP.mult)
    for c in range(CH):
        vector.tensor_tensor(hs_tiles[c][:, a:TS], hs_tiles[c][:, a:TS],
                             mrstd_b[:, s0:s0 + w], OP.subtract)
    for c in range(CH):
        i = vector.tensor_scalar(hs_tiles[c][:, a:TS], hs_tiles[c][:, a:TS],
                                 gamma[:, c:c + 1], beta[:, c:c + 1], OP.mult, OP.add)
    if tti == 0:
        for c in range(CH):
            i = vector.tensor_copy(hs_tiles[c][:, 0:1], bnd_tile[:, bnd_cols(c):bnd_cols(c) + 1])
    return i


def emit_s2(ctx, stage):
    nc, blk = ctx.nc, ctx.block
    X = XTiles(ctx)
    AB = ctx.sbuf("ABUF", (128, 16384))
    Y = [AB[:, 2048 * c:2048 * (c + 1)] for c in range(CH)]
    SCR0 = ctx.sbuf("SCR0", (128, TOK))
    SCR1 = ctx.sbuf("SCR1", (128, TOK))
    SCR2 = ctx.sbuf("SCR2", (128, TOK))
    XQ = [ctx.sbuf(f"XQ{c}", (128, TOK), F16) for c in range(CH)]
    XQf = [XQ[c].bitcast(F32) for c in range(CH)]
    s_dve = ctx.sem("dve")
    s_mm = ctx.sem("mm")
    s_act = ctx.sem("act")
    s_io = ctx.sem("io")
    s_st = ctx.sem("st")
    s_cc = ctx.sem("cc")

    ln1g = ctx.sbuf("ln1g", (128, CH))
    ln1b = ctx.sbuf("ln1b", (128, CH))
    ln2g = ctx.sbuf("ln2g", (128, CH))
    ln2b = ctx.sbuf("ln2b", (128, CH))
    dmy2 = ctx.sbuf("dmy2", (128, 2))
    wkv_w = ctx.sbuf("wkv_w_t", (128, 1))
    wkv_u = ctx.sbuf("wkv_u_t", (128, 1))
    eu_t = ctx.sbuf("eu_t", (128, 1))
    tsv = ctx.sbuf("tsv", (128, 16))

    cc_k_in = ctx.dramt("cc_k_in", (C, TOK))
    cc_vloc = ctx.dramt("cc_vloc", (C, TOK))
    cc_k_out = ctx.dramt("cc_k_out", (NCORE, C, TOK))
    cc_vgat = ctx.dramt("cc_vgat", (NCORE, C, TOK))
    cc_y_in = ctx.dramt("cc_y_in", (128, T * B // 1))
    cc_y_out = ctx.dramt("cc_y_out", (NCORE, 128, T * B))
    cc_hl_in = ctx.dramt("cc_hl_in", (TOK,))
    cc_hl_out = ctx.dramt("cc_hl_out", (NCORE, TOK))
    cc_h2l_in = ctx.dramt("cc_h2l_in", (TOK,))
    cc_h2l_out = ctx.dramt("cc_h2l_out", (NCORE, TOK))
    sigr_sp = ctx.dramt("sigr_sp", (C, TOK))
    x_sp = ctx.dramt("x_sp", (C, TOK))

    @blk.sync
    def _(sync):
        io = ctx.dram
        for nm, t in (("ln1_g", ln1g), ("ln1_b", ln1b), ("ln2_g", ln2g), ("ln2_b", ln2b)):
            for c in range(CH):
                s_io.inc(sync.dma_start(out=t[:, c:c + 1], in_=io[nm][128 * c:128 * (c + 1)]), 16)
        s_io.inc(sync.dma_start(out=wkv_w[:, 0:1], in_=io["wkv_w"][0:128]), 16)
        s_io.inc(sync.dma_start(out=wkv_u[:, 0:1], in_=io["wkv_u"][0:128]), 16)
        s_io.inc(sync.dma_start(out=tsv[:, :], in_=io["ts_pack"][0:TOK]), 16)

    @blk.scalar
    def _(scalar):
        scalar.wait_ge(s_io.h, s_io.val)
        scalar.activation(eu_t[:, 0:1], wkv_u[:, 0:1], AF.Exp)
        scalar.drain()
        s_act.inc(scalar.copy(eu_t[:, 0:1], eu_t[:, 0:1]), 1)

    ctx.barrier()
    if S2_TRUNC == 11:
        return

    emit_ln(ctx, X, Y)
    if S2_TRUNC == 12:
        return

    @blk.gpsimd
    def _(gp):
        for c in range(CH):
            s_st.inc(gp.dma_start(out=x_sp[128 * c:128 * (c + 1), :], in_=X[c][:, :]), 16)
        gp.wait_ge(s_st.h, s_st.val)

    emit_ln_stats(ctx, Y)
    if S2_TRUNC == 13:
        return
    hl_pack = ctx.sbuf("hl_pack", (128, 16))

    @blk.vector
    def _(vector):
        writes = [[(hl_pack[:, 2 * c:2 * c + 1], slice(TL - 1, TL)),
                   (hl_pack[:, 2 * c + 1:2 * c + 2], slice(TOK - 1, TOK))] for c in range(CH)]
        s_dve.inc(emit_ln_apply(ctx, vector, writes, Y, ln1g, ln1b), 1)

    if S2_TRUNC == 14:
        ctx.barrier()
        return

    hl_dbg = ctx.dramt("hl_dbg", (128, 16))

    @blk.gpsimd
    def _(gp):
        gp.wait_ge(s_dve.h, s_dve.val)
        if S2_TRUNC == 151:
            pass
        elif S2_TRUNC == 152:
            s_st.inc(gp.dma_start(out=hl_dbg[:, :], in_=hl_pack[:, :]), 16)
            gp.wait_ge(s_st.h, s_st.val)
        else:
            s_st.inc(gp.dma_start(out=cc_hl_in[:], in_=hl_pack[:, :]), 16)
            gp.wait_ge(s_st.h, s_st.val)
        if S2_TRUNC not in (15, 151, 152):
            s_cc.inc(gp.collective_compute(
                "AllGather", OP.bypass, replica_groups=[list(range(NCORE))],
                ins=[cc_hl_in[:]], outs=[cc_hl_out[:, :]]), 1)
            gp.wait_ge(s_cc.h, s_cc.val)
        if S2_TRUNC not in (15, 151, 152, 16):
            _switch_pick(ctx, gp, hl_pack[:, :], lambda r: cc_hl_out[r, :],
                         ctx.dram["ts_pack"][0:TOK], s_io)
            gp.wait_ge(s_io.h, s_io.val)

    ctx.barrier()
    ctx.tap("s2_hl", hl_pack[:, :])
    if S2_TRUNC in (1, 151, 152, 15, 16):
        return

    hsT = [XQf[c][:, 0:TS] for c in range(CH)]
    wt3 = ctx.sbuf("wt3", (128, CH * 128))
    wt3k = ctx.sbuf("wt3k", (128, CH * 128))
    evk = SCR2[:, 0:TS]
    evr = SCR2[:, TS:2 * TS]
    psN = [ctx.psum(f"ps{i}", (128, TS)) for i in range(4)]
    kv_i = [0]

    for b in range(B):
        for tti in range(2):
            @blk.vector
            def _(vector, b=b, tti=tti):
                vector.wait_ge(s_mm.h, s_mm.val)
                i = emit_hs_tile(ctx, vector, hsT, Y, b, tti, hl_pack,
                                 (lambda c, b=b: 2 * c + b), ln1g, ln1b)
                s_dve.inc(i, 1)

            col0 = b * TL + tti * TS
            for tau, (wnm, dst) in enumerate((("Wk", cc_k_in), ("Wv", cc_vloc), ("Wr", None))):
                for oo in range(CH):
                    slab = wt3 if kv_i[0] % 2 == 0 else wt3k
                    bank = psN[kv_i[0] % 2]
                    kv_i[0] += 1

                    @blk.sync
                    def _(sync, wnm=wnm, oo=oo, slab=slab):
                        sync.wait_ge(s_mm.h, max(0, s_mm.val - 1))
                        for kk in range(CH):
                            s_io.inc(sync.dma_start(
                                out=slab[:, 128 * kk:128 * (kk + 1)],
                                in_=ctx.dram[wnm][128 * kk:128 * (kk + 1), 128 * oo:128 * (oo + 1)]), 16)

                    @blk.tensor
                    def _(tensor, slab=slab, bank=bank):
                        tensor.wait_ge(s_io.h, s_io.val)
                        tensor.wait_ge(s_dve.h, s_dve.val)
                        tensor.wait_ge(s_act.h, max(0, s_act.val - 1))
                        for kk in range(CH):
                            i = tensor.matmul(bank[:, :], slab[:, 128 * kk:128 * (kk + 1)],
                                              hsT[kk][:, :], start=(kk == 0), stop=(kk == CH - 1))
                        s_mm.inc(i, 1)

                    if wnm == "Wr":
                        @blk.scalar
                        def _(scalar, oo=oo, bank=bank):
                            scalar.wait_ge(s_mm.h, s_mm.val)
                            scalar.wait_ge(s_st.h, s_st.val)
                            scalar.activation(evr, bank[:, :], AF.Sigmoid)
                            scalar.drain()
                            s_act.inc(scalar.copy(evr[:, 0:1], evr[:, 0:1]), 1)

                        @blk.gpsimd
                        def _(gp, oo=oo, col0=col0):
                            gp.wait_ge(s_act.h, s_act.val)
                            s_st.inc(gp.dma_start(
                                out=sigr_sp[128 * oo:128 * (oo + 1), col0:col0 + TS], in_=evr), 16)
                    else:
                        @blk.scalar
                        def _(scalar, oo=oo, bank=bank):
                            scalar.wait_ge(s_mm.h, s_mm.val)
                            scalar.wait_ge(s_st.h, s_st.val)
                            scalar.copy(evk, bank[:, :])
                            scalar.drain()
                            s_act.inc(scalar.copy(evk[:, 0:1], evk[:, 0:1]), 1)

                        @blk.gpsimd
                        def _(gp, oo=oo, col0=col0, dst=dst):
                            gp.wait_ge(s_act.h, s_act.val)
                            s_st.inc(gp.dma_start(
                                out=dst[128 * oo:128 * (oo + 1), col0:col0 + TS], in_=evk), 16)

    @blk.gpsimd
    def _(gp):
        gp.wait_ge(s_st.h, s_st.val)
        s_cc.inc(gp.collective_compute(
            "AllGather", OP.bypass, replica_groups=[list(range(NCORE))],
            ins=[cc_k_in[:, :]], outs=[cc_k_out[:, :, :]]), 1)
        s_cc.inc(gp.collective_compute(
            "AllGather", OP.bypass, replica_groups=[list(range(NCORE))],
            ins=[cc_vloc[:, :]], outs=[cc_vgat[:, :, :]]), 1)
        gp.wait_ge(s_cc.h, s_cc.val)

    ctx.barrier()
    ctx.tap("s2_k", ctx.dram["cc_k_out"][0, :, :])
    ctx.tap("s2_v", ctx.dram["cc_vgat"][0, :, :])
    ctx.tap("s2_sigr", sigr_sp[:, :])
    if S2_TRUNC == 2:
        return

    emit_wkv(ctx)
    ctx.tap("s2_y", ctx.dram["cc_y_out"][0, :, :])
    if S2_TRUNC == 3:
        return

    ywkv_t = SCR2[:, 0:TOK]

    for c in range(CH):
        @blk.gpsimd
        def _(gp, c=c):
            gp.wait_ge(s_dve.h, s_dve.val)
            io_base = s_io.val
            for arm in gp.switch_core_id(n=128):
                r = arm.logical % NCORE
                s_io.inc(gp.dma_start(out=ywkv_t,
                                      in_=ctx.dram["cc_y_out"][c, :, TOK * r:TOK * (r + 1)]), 16)
            s_io.val = io_base + 16
            s_io.inc(gp.dma_start(out=SCR0[:, :],
                                  in_=sigr_sp[128 * c:128 * (c + 1), :]), 16)

        @blk.vector
        def _(vector, c=c):
            vector.wait_ge(s_io.h, s_io.val)
            vector.tensor_tensor(SCR1[:, :], SCR0[:, :], ywkv_t, OP.mult)
            vector.drain()
            s_dve.inc(vector.tensor_tensor(AB[:, 2048 * c:2048 * (c + 1)],
                                           AB[:, 2048 * c:2048 * (c + 1)], SCR1[:, :], OP.add), 1)

    ctx.barrier()
    ctx.tap("s2_x1", AB[:, 0:2048])
    if S2_TRUNC == 4:
        return

    emit_ln_stats(ctx, Y)
    h2l_pack = ctx.sbuf("h2l_pack", (128, 16))

    @blk.vector
    def _(vector):
        writes = [[(h2l_pack[:, 2 * c:2 * c + 1], slice(TL - 1, TL)),
                   (h2l_pack[:, 2 * c + 1:2 * c + 2], slice(TOK - 1, TOK))] for c in range(CH)]
        s_dve.inc(emit_ln_apply(ctx, vector, writes, Y, ln2g, ln2b), 1)

    @blk.gpsimd
    def _(gp):
        gp.wait_ge(s_dve.h, s_dve.val)
        s_st.inc(gp.dma_start(out=cc_h2l_in[:], in_=h2l_pack[:, :]), 16)
        gp.wait_ge(s_st.h, s_st.val)
        s_cc.inc(gp.collective_compute(
            "AllGather", OP.bypass, replica_groups=[list(range(NCORE))],
            ins=[cc_h2l_in[:]], outs=[cc_h2l_out[:, :]]), 1)
        gp.wait_ge(s_cc.h, s_cc.val)
        _switch_pick(ctx, gp, h2l_pack[:, :], lambda r: cc_h2l_out[r, :],
                     ctx.dram["ts_pack"][0:TOK], s_io)
        for c in range(CH):
            s_io.inc(gp.dma_start(out=X[c][:, :], in_=x_sp[128 * c:128 * (c + 1), :]), 16)
        gp.wait_ge(s_io.h, s_io.val)

    @blk.vector
    def _(vector):
        vector.wait_ge(s_io.h, s_io.val)
        for c in range(CH):
            i = vector.tensor_tensor(X[c][:, :], X[c][:, :], AB[:, 2048 * c:2048 * (c + 1)], OP.add)
        s_dve.inc(i, 1)

    ctx.barrier()
    ctx.tap("s2_x2", X[0][:, :])
    ctx.tap("s2_h2l", h2l_pack[:, :])
    if S2_TRUNC == 5:
        return

    h2sT2 = [[XQf[c][:, 0:TS] for c in range(CH)],
             [XQf[c][:, TS:2 * TS] for c in range(CH)]]
    kkg = [AB[:, 2048 * i:2048 * i + TS] for i in range(8)]
    kv_acc = [AB[:, 2048 * i + TS:2048 * i + 2 * TS] for i in range(8)]
    rr_t = SCR2[:, 0:TS]
    ev2 = SCR2[:, TS:2 * TS]

    for b in range(B):
        @blk.vector
        def _(vector, b=b):
            vector.wait_ge(s_mm.h, s_mm.val)
            for tti in range(2):
                emit_hs_tile(ctx, vector, h2sT2[tti], Y, b, tti, h2l_pack,
                             (lambda c, b=b: 2 * c + b), ln2g, ln2b)
            vector.drain()
            s_dve.inc(vector.tensor_copy(dmy2[:, 0:1], dmy2[:, 1:2]), 1)

        for tti in range(2):
            col0 = b * TL + tti * TS
            h2sT = h2sT2[tti]

            for g in range(4):
                for ock in range(8):
                    gcol = 1024 * g + 128 * ock

                    @blk.sync
                    def _(sync, gcol=gcol):
                        sync.wait_ge(s_mm.h, s_mm.val)
                        for kk in range(CH):
                            s_io.inc(sync.dma_start(
                                out=wt3[:, 128 * kk:128 * (kk + 1)],
                                in_=ctx.dram["Wck"][128 * kk:128 * (kk + 1), gcol:gcol + 128]), 16)

                    @blk.tensor
                    def _(tensor, ock=ock, h2sT=h2sT):
                        tensor.wait_ge(s_io.h, s_io.val)
                        tensor.wait_ge(s_dve.h, s_dve.val)
                        for kk in range(CH):
                            i = tensor.matmul(psN[ock % 2][:, :], wt3[:, 128 * kk:128 * (kk + 1)],
                                              h2sT[kk][:, :], start=(kk == 0), stop=(kk == CH - 1))
                        s_mm.inc(i, 1)

                    @blk.scalar
                    def _(scalar, ock=ock):
                        scalar.wait_ge(s_mm.h, s_mm.val)
                        scalar.activation(ev2, psN[ock % 2][:, :], AF.Relu)
                        scalar.drain()
                        s_act.inc(scalar.copy(ev2[:, 0:1], ev2[:, 0:1]), 1)

                    @blk.vector
                    def _(vector, ock=ock):
                        vector.wait_ge(s_act.h, s_act.val)
                        vector.tensor_tensor(kkg[ock][:, :], ev2, ev2, OP.mult)
                        vector.drain()
                        s_dve.inc(vector.tensor_copy(dmy2[:, 0:1], dmy2[:, 1:2]), 1)

                for oo in range(CH):
                    @blk.sync
                    def _(sync, g=g, oo=oo):
                        sync.wait_ge(s_mm.h, s_mm.val)
                        for ock in range(8):
                            s_io.inc(sync.dma_start(
                                out=wt3[:, 128 * ock:128 * (ock + 1)],
                                in_=ctx.dram["Wcv"][1024 * g + 128 * ock:1024 * g + 128 * (ock + 1),
                                                    128 * oo:128 * (oo + 1)]), 16)

                    @blk.tensor
                    def _(tensor, oo=oo):
                        tensor.wait_ge(s_io.h, s_io.val)
                        tensor.wait_ge(s_dve.h, s_dve.val)
                        for ock in range(8):
                            i = tensor.matmul(psN[2 + oo % 2][:, :], wt3[:, 128 * ock:128 * (ock + 1)],
                                              kkg[ock][:, :], start=(ock == 0), stop=(ock == 7))
                        s_mm.inc(i, 1)

                    @blk.vector
                    def _(vector, g=g, oo=oo):
                        vector.wait_ge(s_mm.h, s_mm.val)
                        if g == 0:
                            vector.tensor_copy(kv_acc[oo][:, :], psN[2 + oo % 2][:, :])
                        else:
                            vector.tensor_tensor(kv_acc[oo][:, :], kv_acc[oo][:, :],
                                                 psN[2 + oo % 2][:, :], OP.add)
                        vector.drain()
                        s_dve.inc(vector.tensor_copy(dmy2[:, 0:1], dmy2[:, 1:2]), 1)

            for oo in range(CH):
                @blk.sync
                def _(sync, oo=oo):
                    sync.wait_ge(s_mm.h, s_mm.val)
                    for kk in range(CH):
                        s_io.inc(sync.dma_start(
                            out=wt3[:, 128 * kk:128 * (kk + 1)],
                            in_=ctx.dram["Wcr"][128 * kk:128 * (kk + 1), 128 * oo:128 * (oo + 1)]), 16)

                @blk.tensor
                def _(tensor, oo=oo, h2sT=h2sT):
                    tensor.wait_ge(s_io.h, s_io.val)
                    tensor.wait_ge(s_dve.h, s_dve.val)
                    for kk in range(CH):
                        i = tensor.matmul(psN[oo % 2][:, :], wt3[:, 128 * kk:128 * (kk + 1)],
                                          h2sT[kk][:, :], start=(kk == 0), stop=(kk == CH - 1))
                    s_mm.inc(i, 1)

                @blk.scalar
                def _(scalar, oo=oo):
                    scalar.wait_ge(s_mm.h, s_mm.val)
                    scalar.activation(rr_t, psN[oo % 2][:, :], AF.Sigmoid)
                    scalar.drain()
                    s_act.inc(scalar.copy(rr_t[:, 0:1], rr_t[:, 0:1]), 1)

                @blk.vector
                def _(vector, oo=oo, col0=col0):
                    vector.wait_ge(s_act.h, s_act.val)
                    vector.tensor_tensor(SCR2[:, 1024:1024 + TS], rr_t, kv_acc[oo][:, :], OP.mult)
                    vector.drain()
                    s_dve.inc(vector.tensor_tensor(X[oo][:, col0:col0 + TS],
                                                   X[oo][:, col0:col0 + TS],
                                                   SCR2[:, 1024:1024 + TS], OP.add), 1)

    ctx.barrier()
    ctx.tap("s2_kk0", kkg[0])
    ctx.tap("s2_kv0", kv_acc[0])
    ctx.tap("s2_rr7", rr_t)


def emit_wkv(ctx):
    nc, blk = ctx.nc, ctx.block
    X = XTiles(ctx)
    SCR0 = ctx.sb["SCR0"]
    SCR1 = ctx.sb["SCR1"]
    SCR2 = ctx.sb["SCR2"]
    XQf = [ctx.sb[f"XQ{c}"].bitcast(F32) for c in range(CH)]
    s_cc = ctx.sem("cc")
    s_wl = ctx.sem("wl")
    s_wa = ctx.sem("wa")
    s_wd = ctx.sem("wd")
    s_wy = ctx.sem("wy")

    E = X[0]; Eh = X[1]
    kseg = X[2]; vseg = X[3]
    Nq = X[4]; Bq = X[5]
    Cq = X[6]; Dq = X[7]
    vehA = XQf[0][:, 0:TL]; vehB = XQf[1][:, 0:TL]
    Gq = SCR2
    numA = XQf[2][:, 0:TL]; numB = XQf[3][:, 0:TL]
    denA = XQf[4][:, 0:TL]; denB = XQf[5][:, 0:TL]
    yA = XQf[6][:, 0:TL]; yB = XQf[7][:, 0:TL]
    gb = SCR0; eun = SCR1
    wkv_w = ctx.sb["wkv_w_t"]; eu_t = ctx.sb["eu_t"]

    stN = ctx.sbuf("stN", (128, 2))
    stB = ctx.sbuf("stB", (128, 2))
    stBp = ctx.sbuf("stBp", (128, 2))
    stG = ctx.sbuf("stG", (128, 2))
    zt = ctx.sbuf("zt", (128, 2))
    mzt = ctx.sbuf("mzt", (128, 2))
    mzw = ctx.sbuf("mzw", (128, 2))
    lnN = ctx.sbuf("lnN", (128, 2))
    zsc = ctx.sbuf("zsc", (128, 2))
    Mt = ctx.sbuf("Mt", (128, 4))
    imt = ctx.sbuf("imt", (128, 2))
    nsq = ctx.sbuf("nsq", (128, 2))
    sct = ctx.sbuf("sct", (128, 2))
    sc2 = ctx.sbuf("wkv_sc2", (128, 2))
    dmy = ctx.sbuf("wkv_dmy", (128, 4))
    El = ctx.sbuf("wkv_El", (128, 16))
    Ehl = ctx.sbuf("wkv_Ehl", (128, 16))
    Nl = ctx.sbuf("wkv_Nl", (128, 16))
    Bl = ctx.sbuf("wkv_Bl", (128, 16))
    SCq = ctx.sbuf("wkv_SCq", (128, 256))
    SCln = ctx.sbuf("wkv_SCln", (128, 256))

    NCHUNK = TL // RB
    NIT = NCORE * NCHUNK

    El3 = El.rearrange("p (t b) -> p t b", b=2)
    Ehl3 = Ehl.rearrange("p (t b) -> p t b", b=2)
    Nl3 = Nl.rearrange("p (t b) -> p t b", b=2)
    Bl3 = Bl.rearrange("p (t b) -> p t b", b=2)
    E4 = E.rearrange("p (b i j) -> p i j b", b=2, j=RB)
    Eh4 = Eh.rearrange("p (b i j) -> p i j b", b=2, j=RB)
    N4 = Nq.rearrange("p (b i j) -> p i j b", b=2, j=RB)
    B4 = Bq.rearrange("p (b i j) -> p i j b", b=2, j=RB)
    SCR = SCq.rearrange("p (b t) -> p t b", b=2)
    G2 = Gq.rearrange("p (b t) -> p t b", b=2)

    def bhalf(buf, b, a, w):
        return buf[:, TL * b + a:TL * b + a + w]

    def mt(j):
        return Mt[:, 2 * (j % 2):2 * (j % 2) + 2]

    @blk.vector
    def _(vector):
        for t in (stN, stB, stBp, stG, zt, mzt):
            vector.memset(t[:, :], 0.0)
        vector.memset(dmy[:, :], 1.0)
        vector.memset(SCq[:, :], 1.0)
        vector.memset(sc2[:, :], 1.0)
        vector.tensor_scalar(mzw[:, 0:1], wkv_w[:, 0:1], 1.0, None, OP.mult)
        vector.tensor_scalar(mzw[:, 1:2], wkv_w[:, 0:1], 1.0, None, OP.mult)

    ctx.barrier()

    @blk.gpsimd
    def _(gp):
        for s in range(NCORE + 1):
            if s > 0:
                gp.wait_ge(s_wa.h, 2 * (s - 1) + 1)
                gp.wait_ge(s_wd.h, 3 * (s - 1) + 1)
            src = s % NCORE
            wl_base = s_wl.val
            for arm in gp.switch_core_id(n=128):
                j = arm.logical % NCORE
                s_wl.inc(gp.dma_start(out=kseg[:, :],
                                      in_=ctx.dram["cc_k_out"][src, 128 * j:128 * (j + 1), :]), 16)
                s_wl.inc(gp.dma_start(out=vseg[:, :],
                                      in_=ctx.dram["cc_vgat"][src, 128 * j:128 * (j + 1), :]), 16)
            s_wl.val = wl_base + 32
            if s > 0:
                gp.wait_ge(s_wd.h, 3 * (s - 1) + 2)
                s_wy.inc(gp.dma_start(out=ctx.dram["cc_y_in"][:, TOK * (s - 1):TOK * (s - 1) + TL],
                                      in_=yA), 16)
                s_wy.inc(gp.dma_start(out=ctx.dram["cc_y_in"][:, TOK * (s - 1) + TL:TOK * s],
                                      in_=yB), 16)
        gp.wait_ge(s_wy.h, s_wy.val)
        s_cc.inc(gp.collective_compute(
            "AllGather", OP.bypass, replica_groups=[list(range(NCORE))],
            ins=[ctx.dram["cc_y_in"][:, :]], outs=[ctx.dram["cc_y_out"][:, :, :]]), 1)
        gp.wait_ge(s_cc.h, s_cc.val)

    @blk.scalar
    def _(scalar):
        for s in range(NCORE + 1):
            if s > 0:
                scalar.wait_ge(s_wd.h, 3 * (s - 1) + 1)
                scalar.activation(SCln[:, :], SCq[:, :], AF.Ln)
                scalar.activation(lnN[:, :], stN[:, :], AF.Ln)
                scalar.drain()
                s_wa.inc(scalar.copy(lnN[:, 0:1], lnN[:, 0:1]), 1)
                scalar.wait_ge(s_wd.h, 3 * (s - 1) + 3)
            scalar.wait_ge(s_wl.h, 32 * (s + 1))
            scalar.activation(E[:, 0:TL], kseg[:, 0:TL], AF.Exp, bias=mzt[:, 0:1])
            scalar.activation(E[:, TL:TOK], kseg[:, TL:TOK], AF.Exp, bias=mzt[:, 1:2])
            scalar.activation(Eh[:, 0:TL], kseg[:, 0:TL], AF.Exp, bias=mzw[:, 0:1])
            scalar.activation(Eh[:, TL:TOK], kseg[:, TL:TOK], AF.Exp,
                                       bias=mzw[:, 1:2])
            scalar.drain()
            s_wa.inc(scalar.copy(Eh[:, 0:1], Eh[:, 0:1]), 1)

    @blk.vector
    def _(vector):
        def dmyop():
            return vector.tensor_scalar(dmy[:, 0:2], dmy[:, 2:4], 1.0, None, OP.mult)

        def emit_veh():
            vector.tensor_tensor(vehA, vseg[:, 0:TL], Eh[:, 0:TL], OP.mult)
            vector.tensor_tensor(vehB, vseg[:, TL:TOK], Eh[:, TL:TOK], OP.mult)

        def emit_bulk():
            vector.reciprocal(gb[:, 0:TL], Nq[:, 0:TL])
            vector.reciprocal(gb[:, TL:TOK], Nq[:, TL:TOK])
            vector.tensor_scalar(eun[:, 0:TL], Eh[:, 0:TL], eu_t[:, 0:1], None, OP.mult)
            vector.tensor_scalar(eun[:, TL:TOK], Eh[:, TL:TOK], eu_t[:, 0:1], None, OP.mult)
            for b in range(B):
                vector.tensor_tensor(bhalf(Cq, b, 1, TL - 1), bhalf(Bq, b, 0, TL - 1),
                                     bhalf(gb, b, 1, TL - 1), OP.mult)
            for b in range(B):
                veh_b = vehA if b == 0 else vehB
                vector.tensor_tensor(bhalf(Dq, b, 0, TL), veh_b,
                                     bhalf(gb, b, 0, TL), OP.mult)
            for b in range(B):
                vector.tensor_tensor(Cq[:, TL * b:TL * b + 1], stBp[:, b:b + 1],
                                     gb[:, TL * b:TL * b + 1], OP.mult)
            vector.drain()
            for b in range(B):
                vector.tensor_tensor_scan(bhalf(Gq, b, 0, TL), bhalf(Cq, b, 0, TL),
                                          bhalf(Dq, b, 0, TL), stG[:, b:b + 1],
                                          OP.mult, OP.add)
            vector.drain()
            for b in range(B):
                vector.tensor_tensor(bhalf(gb, b, 1, TL - 1), bhalf(Gq, b, 0, TL - 1),
                                     bhalf(Bq, b, 0, TL - 1), OP.mult)
            for b in range(B):
                vector.tensor_tensor(gb[:, TL * b:TL * b + 1], stG[:, b:b + 1],
                                     stBp[:, b:b + 1], OP.mult)
            for b in range(B):
                veh_b = vehA if b == 0 else vehB
                vector.tensor_scalar(bhalf(Dq, b, 0, TL), veh_b,
                                     eu_t[:, 0:1], None, OP.mult)
            vector.drain()
            vector.tensor_tensor(numA, gb[:, 0:TL], Dq[:, 0:TL], OP.add)
            vector.tensor_tensor(numB, gb[:, TL:TOK], Dq[:, TL:TOK], OP.add)
            for b in range(B):
                dstd = denA if b == 0 else denB
                vector.tensor_tensor(dstd[:, 1:TL], bhalf(Bq, b, 0, TL - 1),
                                     bhalf(eun, b, 1, TL - 1), OP.add)
            for b in range(B):
                dstd = denA if b == 0 else denB
                vector.tensor_tensor(dstd[:, 0:1], stBp[:, b:b + 1],
                                     eun[:, TL * b:TL * b + 1], OP.add)
            vector.drain()
            vector.reciprocal(denA, denA)
            vector.reciprocal(denB, denB)
            vector.drain()
            vector.tensor_tensor(yA, numA, denA, OP.mult)
            vector.tensor_tensor(yB, numB, denB, OP.mult)
            vector.drain()
            vector.tensor_copy(stG[:, :], G2[:, TL - 1, :])
            vector.drain()

        def emit_zupd():
            vector.tensor_reduce(zsc[:, 0:1], SCln[:, 0:NCHUNK], AX.X, OP.add)
            vector.tensor_reduce(zsc[:, 1:2], SCln[:, NCHUNK:2 * NCHUNK], AX.X, OP.add)
            vector.drain()
            vector.tensor_tensor(zt[:, :], zt[:, :], lnN[:, :], OP.add)
            vector.drain()
            vector.tensor_tensor(zt[:, :], zt[:, :], zsc[:, :], OP.subtract)
            vector.drain()
            vector.tensor_scalar(mzt[:, :], zt[:, :], -1.0, None, OP.mult)
            vector.tensor_tensor(mzw[:, 0:1], wkv_w[:, 0:1], zt[:, 0:1], OP.subtract)
            vector.tensor_tensor(mzw[:, 1:2], wkv_w[:, 0:1], zt[:, 1:2], OP.subtract)
            vector.drain()

        def emit_seg_prep():
            vector.memset(SCq[:, :], 1.0)
            vector.memset(sc2[:, :], 1.0)
            vector.tensor_copy(stBp[:, :], stB[:, :])
            vector.drain()

        vector.wait_ge(s_wa.h, 1)

        with vector.Fori(1, NIT + 1) as i:
            ic = (i - 1) % NCHUNK
            vector.tensor_copy(El3[:, :, :], E4[:, ic, :, :])
            vector.tensor_copy(Ehl3[:, :, :], Eh4[:, ic, :, :])
            vector.tensor_scalar(El3[:, :, 0], El3[:, :, 0], sc2[:, 0:1], None, OP.mult)
            vector.tensor_scalar(El3[:, :, 1], El3[:, :, 1], sc2[:, 1:2], None, OP.mult)
            vector.tensor_scalar(Ehl3[:, :, 0], Ehl3[:, :, 0], sc2[:, 0:1], None, OP.mult)
            vector.tensor_scalar(Ehl3[:, :, 1], Ehl3[:, :, 1], sc2[:, 1:2], None, OP.mult)
            vector.tensor_tensor(mt(0), stN[:, :], El3[:, 0, :], OP.max)
            vector.tensor_tensor(Nl3[:, 0, :], stB[:, :], Ehl3[:, 0, :], OP.add)
            vector.reciprocal(imt[:, :], mt(0))
            vector.tensor_tensor(nsq[:, :], Nl3[:, 0, :], Nl3[:, 0, :], OP.mult)
            vector.tensor_tensor(mt(1), Nl3[:, 0, :], El3[:, 1, :], OP.max)
            vector.tensor_tensor(Bl3[:, 0, :], nsq[:, :], imt[:, :], OP.mult)
            dmyop()
            for j in range(1, RB - 1):
                vector.tensor_tensor(Nl3[:, j, :], Bl3[:, j - 1, :], Ehl3[:, j, :], OP.add)
                vector.reciprocal(imt[:, :], mt(j))
                vector.tensor_tensor(nsq[:, :], Nl3[:, j, :], Nl3[:, j, :], OP.mult)
                vector.tensor_tensor(mt(j + 1), Nl3[:, j, :], El3[:, j + 1, :], OP.max)
                vector.tensor_tensor(Bl3[:, j, :], nsq[:, :], imt[:, :], OP.mult)
                dmyop()
            J = RB - 1
            vector.tensor_tensor(Nl3[:, J, :], Bl3[:, J - 1, :], Ehl3[:, J, :], OP.add)
            vector.reciprocal(imt[:, :], mt(J))
            vector.tensor_copy(Eh4[:, ic, :, :], Ehl3[:, :, :])
            vector.tensor_tensor(nsq[:, :], Nl3[:, J, :], Nl3[:, J, :], OP.mult)
            vector.reciprocal(sct[:, :], Nl3[:, J, :])
            vector.tensor_tensor(Bl3[:, J, :], nsq[:, :], imt[:, :], OP.mult)
            vector.tensor_tensor(sc2[:, :], sc2[:, :], sct[:, :], OP.mult)
            vector.tensor_tensor(stN[:, :], Nl3[:, J, :], sct[:, :], OP.mult)
            vector.tensor_copy(SCR[:, ic, :], sct[:, :])
            vector.tensor_tensor(Bl3[:, J, :], Bl3[:, J, :], sct[:, :], OP.mult)
            vector.tensor_copy(N4[:, ic, :, :], Nl3[:, :, :])
            vector.tensor_copy(stB[:, :], Bl3[:, J, :])
            vector.tensor_copy(B4[:, ic, :, :], Bl3[:, :, :])
            mr = vector.to_reg(i % NCHUNK)
            with vector.If_cmp(mr, 0, "IS_EQ"):
                vector.drain()
                emit_veh()
                vector.drain()
                s_wd.inc(dmyop(), 1)
                rw = vector.to_reg((i >> 2) - 32)
                vector.wait_ge(s_wy.h, rw)
                emit_bulk()
                s_wd.inc(dmyop(), 1)
                r1 = vector.to_reg(i >> 6)
                vector.wait_ge(s_wa.h, r1)
                emit_zupd()
                s_wd.inc(dmyop(), 1)
                r2 = vector.to_reg((i >> 6) + 1)
                vector.wait_ge(s_wa.h, r2)
                emit_seg_prep()
        vector.drain()

    ctx.barrier()


def emit_s5(ctx, stage):
    nc, blk = ctx.nc, ctx.block
    X = XTiles(ctx)
    AB = ctx.sbuf("ABUF", (128, 16384))
    Y = [AB[:, 2048 * c:2048 * (c + 1)] for c in range(CH)]
    SCR0 = ctx.sbuf("SCR0", (128, TOK))
    SCR1 = ctx.sbuf("SCR1", (128, TOK))
    SCR2 = ctx.sbuf("SCR2", (128, TOK))
    XQ = [ctx.sbuf(f"XQ{c}", (128, TOK), F16) for c in range(CH)]
    XQf = [XQ[c].bitcast(F32) for c in range(CH)]
    wt3 = ctx.sbuf("wt3", (128, CH * 128))
    psN = [ctx.psum(f"ps{i}", (128, TS)) for i in range(4)]
    ones = ctx.sbuf("ones_col", (128, 1))
    s_dve = ctx.sem("dve")
    s_mm = ctx.sem("mm")
    s_act = ctx.sem("act")
    s_io = ctx.sem("io")
    s_st = ctx.sem("st")
    s_cc = ctx.sem("cc")

    ba_t = ctx.sbuf("s5_ba", (128, CH))
    bx_t = ctx.sbuf("s5_bx", (128, CH))
    aL = ctx.sbuf("s5_aL", (128, CH))
    zer = ctx.sbuf("s5_zer", (128, TL))
    pk = ctx.sbuf("s5_pk", (128, 4 * CH))
    pkg = ctx.sbuf("s5_pkg", (128, 32 * NCORE))
    Hall = ctx.sbuf("s5_Hall", (128, 16 * (NCORE - 1)))
    t16 = ctx.sbuf("s5_t16", (128, 16))
    ct = ctx.sbuf("s5_c", (128, 16))

    sr = SCR0
    si = SCR1
    tmpe = SCR2
    at_b = [XQf[0][:, 0:TL], XQf[1][:, 0:TL]]
    bt_b = [XQf[2][:, 0:TL], XQf[3][:, 0:TL]]
    yl_b = [XQf[4][:, 0:TL], XQf[5][:, 0:TL]]
    P_b = [XQf[6][:, 0:TL], XQf[7][:, 0:TL]]
    LN8 = math.log(8.0)

    y_sp = ctx.dramt("s5_y_sp", (C, TOK))
    p_sp = ctx.dramt("s5_p_sp", (C, TOK))
    cc_pk_in = ctx.dramt("cc_pk_in", (128 * 4 * CH,))
    cc_pk_out = ctx.dramt("cc_pk_out", (NCORE, 128 * 4 * CH))

    @blk.sync
    def _(sync):
        for nm, t in (("s5_Wba", ba_t), ("s5_Wbx", bx_t)):
            for c in range(CH):
                s_io.inc(sync.dma_start(out=t[:, c:c + 1],
                                        in_=ctx.dram[nm][128 * c:128 * (c + 1)]), 16)
        for c in range(CH):
            s_io.inc(sync.dma_start(out=aL[:, c:c + 1],
                                    in_=ctx.dram["s5_Lam"][128 * c:128 * (c + 1)]), 16)

    @blk.scalar
    def _(scalar):
        scalar.wait_ge(s_io.h, s_io.val)
        scalar.activation(aL[:, :], aL[:, :], AF.Sigmoid)
        scalar.drain()
        s_act.inc(scalar.copy(aL[:, 0:1], aL[:, 0:1]), 1)

    @blk.vector
    def _(vector):
        vector.memset(zer[:, :], 0.0)
        vector.drain()
        s_dve.inc(vector.tensor_copy(zer[:, 0:1], zer[:, 1:2]), 1)

    emit_ln(ctx, X, Y)

    for oo in range(CH):
        @blk.sync
        def _(sync, oo=oo):
            sync.wait_ge(s_mm.h, s_mm.val)
            for kk in range(CH):
                s_io.inc(sync.dma_start(
                    out=wt3[:, 128 * kk:128 * (kk + 1)],
                    in_=ctx.dram["s5_Wa"][128 * kk:128 * (kk + 1), 128 * oo:128 * (oo + 1)]), 16)

        @blk.tensor
        def _(tensor):
            tensor.wait_ge(s_io.h, s_io.val)
            tensor.wait_ge(s_act.h, s_act.val)
            tensor.wait_ge(s_dve.h, s_dve.val)
            for nt in range(NT):
                for kk in range(CH):
                    i = tensor.matmul(psN[nt][:, :], wt3[:, 128 * kk:128 * (kk + 1)],
                                      Y[kk][:, TS * nt:TS * (nt + 1)],
                                      start=(kk == 0), stop=(kk == CH - 1))
            s_mm.inc(i, 1)

        @blk.scalar
        def _(scalar, oo=oo):
            scalar.wait_ge(s_mm.h, s_mm.val)
            for nt in range(NT):
                scalar.activation(sr[:, TS * nt:TS * (nt + 1)], psN[nt][:, :],
                                  AF.Sigmoid, bias=ba_t[:, oo:oo + 1])
            scalar.drain()
            s_act.inc(scalar.copy(sr[:, 0:1], sr[:, 0:1]), 1)

        @blk.sync
        def _(sync, oo=oo):
            sync.wait_ge(s_mm.h, s_mm.val)
            for kk in range(CH):
                s_io.inc(sync.dma_start(
                    out=wt3[:, 128 * kk:128 * (kk + 1)],
                    in_=ctx.dram["s5_Wx"][128 * kk:128 * (kk + 1), 128 * oo:128 * (oo + 1)]), 16)

        @blk.tensor
        def _(tensor):
            tensor.wait_ge(s_io.h, s_io.val)
            tensor.wait_ge(s_act.h, s_act.val)
            for nt in range(NT):
                for kk in range(CH):
                    i = tensor.matmul(psN[nt][:, :], wt3[:, 128 * kk:128 * (kk + 1)],
                                      Y[kk][:, TS * nt:TS * (nt + 1)],
                                      start=(kk == 0), stop=(kk == CH - 1))
            s_mm.inc(i, 1)

        @blk.scalar
        def _(scalar, oo=oo):
            scalar.wait_ge(s_mm.h, s_mm.val)
            for nt in range(NT):
                scalar.activation(si[:, TS * nt:TS * (nt + 1)], psN[nt][:, :],
                                  AF.Sigmoid, bias=bx_t[:, oo:oo + 1])
            scalar.activation(tmpe[:, :], sr[:, :], AF.Exp, scale=-LN8)
            scalar.drain()
            s_act.inc(scalar.copy(tmpe[:, 0:1], tmpe[:, 0:1]), 1)

        @blk.vector
        def _(vector, oo=oo):
            vector.wait_ge(s_act.h, s_act.val)
            vector.tensor_scalar(at_b[0], tmpe[:, 0:TL], aL[:, oo:oo + 1], None, OP.mult)
            vector.tensor_scalar(at_b[1], tmpe[:, TL:TOK], aL[:, oo:oo + 1], None, OP.mult)
            vector.tensor_tensor(bt_b[0], at_b[0], at_b[0], OP.mult)
            vector.tensor_tensor(bt_b[1], at_b[1], at_b[1], OP.mult)
            vector.tensor_scalar(bt_b[0], bt_b[0], -1.0, 1.0, OP.mult, OP.add)
            vector.tensor_scalar(bt_b[1], bt_b[1], -1.0, 1.0, OP.mult, OP.add)
            vector.drain()
            s_dve.inc(vector.tensor_copy(t16[:, 0:1], t16[:, 1:2]), 1)

        @blk.scalar
        def _(scalar):
            scalar.wait_ge(s_dve.h, s_dve.val)
            scalar.activation(tmpe[:, 0:TL], bt_b[0], AF.Sqrt)
            scalar.activation(tmpe[:, TL:TOK], bt_b[1], AF.Sqrt)
            scalar.drain()
            s_act.inc(scalar.copy(tmpe[:, 0:1], tmpe[:, 0:1]), 1)

        @blk.vector
        def _(vector, oo=oo):
            vector.wait_ge(s_act.h, s_act.val)
            vector.wait_ge(s_st.h, s_st.val)
            vector.tensor_tensor(bt_b[0], tmpe[:, 0:TL], si[:, 0:TL], OP.mult)
            vector.tensor_tensor(bt_b[1], tmpe[:, TL:TOK], si[:, TL:TOK], OP.mult)
            vector.tensor_tensor(bt_b[0], bt_b[0], Y[oo][:, 0:TL], OP.mult)
            vector.tensor_tensor(bt_b[1], bt_b[1], Y[oo][:, TL:TOK], OP.mult)
            vector.drain()
            vector.tensor_tensor_scan(yl_b[0], at_b[0], bt_b[0], zer[:, 0:1],
                                      OP.mult, OP.add)
            vector.tensor_tensor_scan(yl_b[1], at_b[1], bt_b[1], zer[:, 0:1],
                                      OP.mult, OP.add)
            vector.tensor_tensor_scan(P_b[0], at_b[0], zer[:, :], ones[:, 0:1],
                                      OP.mult, OP.add)
            vector.tensor_tensor_scan(P_b[1], at_b[1], zer[:, :], ones[:, 0:1],
                                      OP.mult, OP.add)
            vector.drain()
            vector.tensor_copy(pk[:, 4 * oo + 0:4 * oo + 1], P_b[0][:, TL - 1:TL])
            vector.tensor_copy(pk[:, 4 * oo + 1:4 * oo + 2], P_b[1][:, TL - 1:TL])
            vector.tensor_copy(pk[:, 4 * oo + 2:4 * oo + 3], yl_b[0][:, TL - 1:TL])
            vector.tensor_copy(pk[:, 4 * oo + 3:4 * oo + 4], yl_b[1][:, TL - 1:TL])
            vector.drain()
            s_dve.inc(vector.tensor_copy(t16[:, 0:1], t16[:, 1:2]), 1)

        @blk.gpsimd
        def _(gp, oo=oo):
            gp.wait_ge(s_dve.h, s_dve.val)
            s_st.inc(gp.dma_start(out=y_sp[128 * oo:128 * (oo + 1), 0:TL], in_=yl_b[0]), 16)
            s_st.inc(gp.dma_start(out=y_sp[128 * oo:128 * (oo + 1), TL:TOK], in_=yl_b[1]), 16)
            s_st.inc(gp.dma_start(out=p_sp[128 * oo:128 * (oo + 1), 0:TL], in_=P_b[0]), 16)
            s_st.inc(gp.dma_start(out=p_sp[128 * oo:128 * (oo + 1), TL:TOK], in_=P_b[1]), 16)

    @blk.gpsimd
    def _(gp):
        gp.wait_ge(s_st.h, s_st.val)
        s_st.inc(gp.dma_start(out=cc_pk_in[:], in_=pk[:, :]), 16)
        gp.wait_ge(s_st.h, s_st.val)
        s_cc.inc(gp.collective_compute(
            "AllGather", OP.bypass, replica_groups=[list(range(NCORE))],
            ins=[cc_pk_in[:]], outs=[cc_pk_out[:, :]]), 1)
        gp.wait_ge(s_cc.h, s_cc.val)
        for s in range(NCORE):
            s_io.inc(gp.dma_start(out=pkg[:, 32 * s:32 * (s + 1)], in_=cc_pk_out[s, :]), 16)
        gp.wait_ge(s_io.h, s_io.val)

    pkv = pkg.rearrange("p (s o f) -> p s o f", s=NCORE, f=4)
    Hv = Hall.rearrange("p (s o f) -> p s o f", s=NCORE - 1, f=2)
    t16v = t16.rearrange("p (o f) -> p o f", f=2)

    @blk.vector
    def _(vector):
        vector.wait_ge(s_io.h, s_io.val)
        vector.tensor_copy(Hv[:, 0, :, :], pkv[:, 0, :, 2:4])
        vector.drain()
        for s in range(1, NCORE - 1):
            vector.tensor_tensor(t16v[:, :, :], pkv[:, s, :, 0:2], Hv[:, s - 1, :, :], OP.mult)
            vector.drain()
            vector.tensor_tensor(Hv[:, s, :, :], t16v[:, :, :], pkv[:, s, :, 2:4], OP.add)
            vector.drain()
        for arm in vector.switch_core_id(n=128):
            r = arm.logical % NCORE
            if r == 0:
                vector.memset(ct[:, :], 0.0)
            else:
                vector.tensor_copy(ct[:, :], Hall[:, 16 * (r - 1):16 * r])
        vector.drain()
        s_dve.inc(vector.tensor_copy(t16[:, 0:1], t16[:, 1:2]), 1)

    ctx.barrier()

    for oo in range(CH):
        @blk.gpsimd
        def _(gp, oo=oo):
            gp.wait_ge(s_dve.h, s_dve.val)
            s_io.inc(gp.dma_start(out=SCR0[:, :], in_=y_sp[128 * oo:128 * (oo + 1), :]), 16)
            s_io.inc(gp.dma_start(out=SCR1[:, :], in_=p_sp[128 * oo:128 * (oo + 1), :]), 16)

        @blk.vector
        def _(vector, oo=oo):
            vector.wait_ge(s_io.h, s_io.val)
            for b in range(B):
                vector.tensor_scalar(SCR1[:, TL * b:TL * (b + 1)], SCR1[:, TL * b:TL * (b + 1)],
                                     ct[:, 2 * oo + b:2 * oo + b + 1], None, OP.mult)
            vector.drain()
            for b in range(B):
                vector.tensor_tensor(SCR1[:, TL * b:TL * (b + 1)], SCR1[:, TL * b:TL * (b + 1)],
                                     SCR0[:, TL * b:TL * (b + 1)], OP.add)
            vector.drain()
            for b in range(B):
                i = vector.tensor_tensor(X[oo][:, TL * b:TL * (b + 1)],
                                         X[oo][:, TL * b:TL * (b + 1)],
                                         SCR1[:, TL * b:TL * (b + 1)], OP.add)
            vector.drain()
            s_dve.inc(vector.tensor_copy(t16[:, 0:1], t16[:, 1:2]), 1)

    ctx.barrier()


def emit_s4(ctx, stage):
    nc, blk = ctx.nc, ctx.block
    X = XTiles(ctx)
    AB = ctx.sbuf("ABUF", (128, 16384))
    Y = [AB[:, 2048 * c:2048 * (c + 1)] for c in range(CH)]
    SCR0 = ctx.sbuf("SCR0", (128, TOK))
    SCR1 = ctx.sbuf("SCR1", (128, TOK))
    SCR2 = ctx.sbuf("SCR2", (128, TOK))
    XQ = [ctx.sbuf(f"XQ{c}", (128, TOK), F16) for c in range(CH)]
    XQf = [XQ[c].bitcast(F32) for c in range(CH)]
    wt3 = ctx.sbuf("wt3", (128, CH * 128))
    psN = [ctx.psum(f"ps{i}", (128, TS)) for i in range(4)]
    s_dve = ctx.sem("dve")
    s_mm = ctx.sem("mm")
    s_act = ctx.sem("act")
    s_io = ctx.sem("io")
    s_st = ctx.sem("st")

    gw_t = ctx.sbuf("s4_gw", (128, 64))
    gb_t = ctx.sbuf("s4_gb", (128, 8))
    sc8 = ctx.sbuf("s4_sc8", (128, 8))
    g_sfc = ctx.sbuf("s4_sfc", (128, 8))
    g_gs = ctx.sbuf("s4_gs", (128, 8))
    g_eq = ctx.sbuf("s4_eq", (128, 8))
    g_cum = ctx.sbuf("s4_cum", (128, 8))
    g_fm = ctx.sbuf("s4_fm", (128, 8))
    g_t = ctx.sbuf("s4_t", (128, 8))
    g_m = ctx.sbuf("s4_m", (128, 8))
    g_e = ctx.sbuf("s4_e", (128, 16))
    g_msk = ctx.sbuf("s4_msk", (128, 8))
    g_tmp8 = ctx.sbuf("s4_tmp8", (128, 16))
    g_w = ctx.sbuf("s4_w", (128, 8))
    s4o = ctx.sbuf("s4_ones", (128, 8))
    s4z = ctx.sbuf("s4_zc", (128, 1))
    fnc = ctx.sbuf("s4_fnc", (128, 2))
    BIG = 1.0e30

    om_spT = ctx.dramt(f"{stage}_omT", (EMOE, TOK))

    @blk.sync
    def _(sync):
        for kk in range(CH):
            s_io.inc(sync.dma_start(out=gw_t[:, 8 * kk:8 * (kk + 1)],
                                    in_=ctx.dram["s4_gatew"][128 * kk:128 * (kk + 1), :]), 16)
        s_io.inc(sync.dma_start(
            out=gb_t[:, :],
            in_=ctx.dram["s4_gateb"][:].unsqueeze(0).broadcast_to([128, EMOE])), 16)

    @blk.vector
    def _(vector):
        vector.memset(s4o[:, :], 1.0)
        vector.memset(s4z[:, :], 0.0)
        vector.memset(fnc[:, :], 1.0)
        vector.drain()
        s_dve.inc(vector.tensor_copy(fnc[:, 0:1], fnc[:, 1:2]), 1)

    emit_ln(ctx, X, Y)

    gsv = g_sfc.rearrange("p (g two) -> p g two", two=2)
    mv = g_msk.rearrange("p (g two) -> p g two", two=2)

    for t in range(TOK // 128):
        kc, tc = t // 8, t % 8
        @blk.tensor
        def _(tensor, t=t):
            tensor.wait_ge(s_io.h, s_io.val)
            tensor.wait_ge(s_dve.h, s_dve.val)
            for kk in range(CH):
                i = tensor.matmul(psN[0][:, 0:EMOE], Y[kk][:, 128 * t:128 * (t + 1)],
                                  gw_t[:, 8 * kk:8 * (kk + 1)],
                                  start=(kk == 0), stop=(kk == CH - 1))
            s_mm.inc(i, 1)

        @blk.scalar
        def _(scalar):
            scalar.wait_ge(s_mm.h, s_mm.val)
            scalar.activation(sc8[:, :], psN[0][:, 0:EMOE], AF.Sigmoid)
            scalar.drain()
            s_act.inc(scalar.copy(sc8[:, 0:1], sc8[:, 0:1]), 1)

        @blk.vector
        def _(vector, t=t):
            vector.wait_ge(s_act.h, s_act.val)
            vector.wait_ge(s_st.h, s_st.val)
            D = vector.drain
            vector.tensor_tensor(g_sfc[:, :], sc8[:, :], gb_t[:, :], OP.add)
            D()
            vector.tensor_tensor(g_gs[:, 0:4], gsv[:, :, 0], gsv[:, :, 1], OP.add)
            D()
            vector.tensor_reduce(g_m[:, 0:1], g_gs[:, 0:4], AX.X, OP.max)
            D()
            vector.tensor_scalar(g_eq[:, 0:4], g_gs[:, 0:4], g_m[:, 0:1], None, OP.is_equal)
            D()
            vector.tensor_tensor_scan(g_cum[:, 0:4], s4o[:, 0:4], g_eq[:, 0:4],
                                      s4z[:, 0:1], OP.mult, OP.add)
            D()
            vector.tensor_scalar(g_t[:, 0:4], g_cum[:, 0:4], 1.0, None, OP.is_equal)
            D()
            vector.tensor_tensor(g_fm[:, 0:4], g_eq[:, 0:4], g_t[:, 0:4], OP.mult)
            D()
            vector.tensor_scalar(g_t[:, 0:4], g_fm[:, 0:4], BIG, None, OP.mult)
            D()
            vector.tensor_tensor(g_gs[:, 4:8], g_gs[:, 0:4], g_t[:, 0:4], OP.subtract)
            D()
            vector.tensor_reduce(g_m[:, 1:2], g_gs[:, 4:8], AX.X, OP.max)
            D()
            vector.tensor_scalar(g_eq[:, 4:8], g_gs[:, 4:8], g_m[:, 1:2], None, OP.is_equal)
            D()
            vector.tensor_tensor_scan(g_cum[:, 4:8], s4o[:, 0:4], g_eq[:, 4:8],
                                      s4z[:, 0:1], OP.mult, OP.add)
            D()
            vector.tensor_scalar(g_t[:, 4:8], g_cum[:, 4:8], 1.0, None, OP.is_equal)
            D()
            vector.tensor_tensor(g_fm[:, 4:8], g_eq[:, 4:8], g_t[:, 4:8], OP.mult)
            D()
            vector.tensor_tensor(g_t[:, 0:4], g_fm[:, 0:4], g_fm[:, 4:8], OP.add)
            D()
            vector.tensor_copy(mv[:, :, 0], g_t[:, 0:4])
            vector.tensor_copy(mv[:, :, 1], g_t[:, 0:4])
            D()
            vector.tensor_tensor(g_tmp8[:, 0:8], g_sfc[:, :], g_msk[:, :], OP.mult)
            D()
            vector.tensor_reduce(g_m[:, 2:3], g_tmp8[:, 0:8], AX.X, OP.max)
            D()
            vector.tensor_scalar(g_eq[:, 0:8], g_tmp8[:, 0:8], g_m[:, 2:3], None, OP.is_equal)
            D()
            vector.tensor_tensor_scan(g_cum[:, 0:8], s4o[:, 0:8], g_eq[:, 0:8],
                                      s4z[:, 0:1], OP.mult, OP.add)
            D()
            vector.tensor_scalar(g_t[:, 0:8], g_cum[:, 0:8], 1.0, None, OP.is_equal)
            D()
            vector.tensor_tensor(g_e[:, 0:8], g_eq[:, 0:8], g_t[:, 0:8], OP.mult)
            D()
            vector.tensor_tensor(g_t[:, 0:8], g_e[:, 0:8], g_sfc[:, :], OP.mult)
            D()
            vector.tensor_reduce(g_m[:, 4:5], g_t[:, 0:8], AX.X, OP.add)
            vector.tensor_scalar(g_t[:, 0:8], g_e[:, 0:8], BIG, None, OP.mult)
            D()
            vector.tensor_tensor(g_tmp8[:, 8:16], g_tmp8[:, 0:8], g_t[:, 0:8], OP.subtract)
            D()
            vector.tensor_reduce(g_m[:, 3:4], g_tmp8[:, 8:16], AX.X, OP.max)
            D()
            vector.tensor_scalar(g_eq[:, 0:8], g_tmp8[:, 8:16], g_m[:, 3:4], None, OP.is_equal)
            D()
            vector.tensor_tensor_scan(g_cum[:, 0:8], s4o[:, 0:8], g_eq[:, 0:8],
                                      s4z[:, 0:1], OP.mult, OP.add)
            D()
            vector.tensor_scalar(g_t[:, 0:8], g_cum[:, 0:8], 1.0, None, OP.is_equal)
            D()
            vector.tensor_tensor(g_e[:, 8:16], g_eq[:, 0:8], g_t[:, 0:8], OP.mult)
            D()
            vector.tensor_tensor(g_t[:, 0:8], g_e[:, 8:16], g_sfc[:, :], OP.mult)
            D()
            vector.tensor_reduce(g_m[:, 5:6], g_t[:, 0:8], AX.X, OP.add)
            D()
            vector.tensor_tensor(g_m[:, 6:7], g_m[:, 4:5], g_m[:, 5:6], OP.add)
            D()
            vector.tensor_scalar(g_m[:, 6:7], g_m[:, 6:7], 1.0, 1.0e-20, OP.mult, OP.add)
            D()
            vector.reciprocal(g_m[:, 7:8], g_m[:, 6:7])
            D()
            vector.tensor_tensor(g_m[:, 4:5], g_m[:, 4:5], g_m[:, 7:8], OP.mult)
            vector.tensor_tensor(g_m[:, 5:6], g_m[:, 5:6], g_m[:, 7:8], OP.mult)
            D()
            vector.tensor_scalar(g_m[:, 4:5], g_m[:, 4:5], 2.5, None, OP.mult)
            vector.tensor_scalar(g_m[:, 5:6], g_m[:, 5:6], 2.5, None, OP.mult)
            D()
            vector.tensor_scalar(g_w[:, :], g_e[:, 0:8], g_m[:, 4:5], None, OP.mult)
            vector.tensor_scalar(g_t[:, 0:8], g_e[:, 8:16], g_m[:, 5:6], None, OP.mult)
            D()
            vector.tensor_tensor(g_w[:, :], g_w[:, :], g_t[:, 0:8], OP.add)
            D()
            s_dve.inc(vector.tensor_copy(fnc[:, 0:1], fnc[:, 1:2]), 1)

        @blk.gpsimd
        def _(gp, t=t):
            gp.wait_ge(s_dve.h, s_dve.val)
            for e in range(EMOE):
                s_st.inc(gp.dma_start(out=om_spT[e, 128 * t:128 * (t + 1)],
                                      in_=g_w[:, e:e + 1]), 16)

    ctx.barrier()
    ctx.tap(f"{stage}_om", om_spT[:, :])

    h_t = [XQf[ic][:, 0:TS] for ic in range(4)]
    yacc = [XQf[4 + i // 2][:, (i % 2) * TS:(i % 2 + 1) * TS] for i in range(8)]
    wbc = SCR2[:, 0:TS]
    gbuf = SCR0
    ubuf = SCR1

    wt3b = ctx.sbuf("wt3k", (128, CH * 128))
    mm_i = [0]

    def mm_block(wsrc, row0, col0, nrow, psdst, moving, movsl, tag, dve_lag=0):
        slab = wt3 if mm_i[0] % 2 == 0 else wt3b
        mm_i[0] += 1

        @blk.sync
        def _(sync):
            sync.wait_ge(s_mm.h, max(0, s_mm.val - 1))
            for kk in range(nrow):
                s_io.inc(sync.dma_start(
                    out=slab[:, 128 * kk:128 * (kk + 1)],
                    in_=wsrc[row0 + 128 * kk:row0 + 128 * (kk + 1), col0:col0 + 128]), 16)

        @blk.tensor
        def _(tensor):
            tensor.wait_ge(s_io.h, s_io.val)
            tensor.wait_ge(s_dve.h, max(0, s_dve.val - dve_lag))
            tensor.wait_ge(s_act.h, max(0, s_act.val - 1))
            for kk in range(nrow):
                i = tensor.matmul(psdst[:, :], slab[:, 128 * kk:128 * (kk + 1)],
                                  moving[kk][:, movsl], start=(kk == 0), stop=(kk == nrow - 1))
            s_mm.inc(i, 1)

    for tt in range(NT):
        tsl = slice(TS * tt, TS * (tt + 1))

        for e in range(EMOE):
            @blk.gpsimd
            def _(gp, e=e, tt=tt):
                gp.wait_ge(s_dve.h, s_dve.val)
                s_io.inc(gp.dma_start(
                    out=wbc,
                    in_=om_spT[e, TS * tt:TS * (tt + 1)].unsqueeze(0).broadcast_to([128, TS])), 16)

            for ic in range(4):
                mm_block(ctx.dram["s4_wg"], 1024 * e, 128 * ic, CH, psN[ic % 2], Y, tsl, "g", dve_lag=(0 if ic == 0 else 1))

                @blk.scalar
                def _(scalar, ic=ic):
                    scalar.wait_ge(s_mm.h, s_mm.val)
                    scalar.activation(gbuf[:, TS * ic:TS * (ic + 1)], psN[ic % 2][:, :], AF.Silu)
                    scalar.drain()
                    s_act.inc(scalar.copy(gbuf[:, 0:1], gbuf[:, 0:1]), 1)
            for ic in range(4):
                mm_block(ctx.dram["s4_wu"], 1024 * e, 128 * ic, CH, psN[ic % 2], Y, tsl, "u", dve_lag=1)

                @blk.scalar
                def _(scalar, ic=ic):
                    scalar.wait_ge(s_mm.h, s_mm.val)
                    scalar.activation(ubuf[:, TS * ic:TS * (ic + 1)], psN[ic % 2][:, :], AF.Copy)
                    scalar.drain()
                    s_act.inc(scalar.copy(ubuf[:, 0:1], ubuf[:, 0:1]), 1)

            @blk.vector
            def _(vector):
                vector.wait_ge(s_act.h, s_act.val)
                vector.wait_ge(s_io.h, s_io.val)
                for ic in range(4):
                    vector.tensor_tensor(h_t[ic], gbuf[:, TS * ic:TS * (ic + 1)],
                                         ubuf[:, TS * ic:TS * (ic + 1)], OP.mult)
                for ic in range(4):
                    vector.tensor_tensor(h_t[ic], h_t[ic], wbc, OP.mult)
                vector.drain()
                s_dve.inc(vector.tensor_copy(fnc[:, 0:1], fnc[:, 1:2]), 1)

            for oc in range(CH):
                mm_block(ctx.dram["s4_wd"], 512 * e, 128 * oc, 4, psN[2 + oc % 2],
                         h_t, slice(0, TS), "d", dve_lag=(0 if oc == 0 else 1))

                @blk.vector
                def _(vector, e=e, oc=oc):
                    vector.wait_ge(s_mm.h, s_mm.val)
                    if e == 0:
                        vector.tensor_copy(yacc[oc], psN[2 + oc % 2][:, :])
                    else:
                        vector.tensor_tensor(yacc[oc], yacc[oc], psN[2 + oc % 2][:, :], OP.add)
                    vector.drain()
                    s_dve.inc(vector.tensor_copy(fnc[:, 0:1], fnc[:, 1:2]), 1)

        for half in range(2):
            for ic4 in range(4):
                ic = 4 * half + ic4
                mm_block(ctx.dram["s4_swg"], 0, 128 * ic, CH, psN[ic4 % 2], Y, tsl, "sg", dve_lag=(0 if ic4 == 0 else 1))

                @blk.scalar
                def _(scalar, ic4=ic4):
                    scalar.wait_ge(s_mm.h, s_mm.val)
                    scalar.activation(gbuf[:, TS * ic4:TS * (ic4 + 1)], psN[ic4 % 2][:, :], AF.Silu)
                    scalar.drain()
                    s_act.inc(scalar.copy(gbuf[:, 0:1], gbuf[:, 0:1]), 1)
            for ic4 in range(4):
                ic = 4 * half + ic4
                mm_block(ctx.dram["s4_swu"], 0, 128 * ic, CH, psN[ic4 % 2], Y, tsl, "su", dve_lag=1)

                @blk.scalar
                def _(scalar, ic4=ic4):
                    scalar.wait_ge(s_mm.h, s_mm.val)
                    scalar.activation(ubuf[:, TS * ic4:TS * (ic4 + 1)], psN[ic4 % 2][:, :], AF.Copy)
                    scalar.drain()
                    s_act.inc(scalar.copy(ubuf[:, 0:1], ubuf[:, 0:1]), 1)

            @blk.vector
            def _(vector):
                vector.wait_ge(s_act.h, s_act.val)
                for ic4 in range(4):
                    vector.tensor_tensor(h_t[ic4], gbuf[:, TS * ic4:TS * (ic4 + 1)],
                                         ubuf[:, TS * ic4:TS * (ic4 + 1)], OP.mult)
                vector.drain()
                s_dve.inc(vector.tensor_copy(fnc[:, 0:1], fnc[:, 1:2]), 1)

            for oc in range(CH):
                mm_block(ctx.dram["s4_swd"], 512 * half, 128 * oc, 4, psN[2 + oc % 2],
                         h_t, slice(0, TS), "sd", dve_lag=(0 if oc == 0 else 1))

                @blk.vector
                def _(vector, oc=oc):
                    vector.wait_ge(s_mm.h, s_mm.val)
                    vector.tensor_tensor(yacc[oc], yacc[oc], psN[2 + oc % 2][:, :], OP.add)
                    vector.drain()
                    s_dve.inc(vector.tensor_copy(fnc[:, 0:1], fnc[:, 1:2]), 1)

        @blk.vector
        def _(vector, tt=tt, tsl=tsl):
            vector.wait_ge(s_dve.h, s_dve.val)
            for oc in range(CH):
                vector.tensor_scalar(gbuf[:, TS * (oc % 4):TS * (oc % 4 + 1)],
                                     yacc[oc], 0.5, None, OP.mult)
                if oc % 4 == 3 or oc == CH - 1:
                    vector.drain()
                    for oc2 in range(oc - oc % 4, oc + 1):
                        vector.tensor_tensor(X[oc2][:, tsl], X[oc2][:, tsl],
                                             gbuf[:, TS * (oc2 % 4):TS * (oc2 % 4 + 1)], OP.add)
                    vector.drain()
            s_dve.inc(vector.tensor_copy(fnc[:, 0:1], fnc[:, 1:2]), 1)

    ctx.barrier()


def emit_stub(ctx, stage):
    ctx.barrier()


STAGE_EMITTERS = {
    "s1": emit_ffn0,
    "s2": emit_s2,
    "s3": emit_ffn0,
    "s4": emit_s4,
    "s5": emit_s5,
    "s6": emit_s4,
}


def build_program(stages):
    nc = bass.Bass("TRN2", target_bir_lowering=False)

    io = {}
    io["x_in"] = nc.dram_tensor("x_in", [C, TOK], F32, kind="ExternalInput").ap()
    io["x_out"] = nc.dram_tensor("x_out", [C, TOK], F32, kind="ExternalOutput").ap()
    io["ident_in"] = nc.dram_tensor("ident_in", [128, 128], F32, kind="ExternalInput").ap()

    need_ffn0 = ("s1" in stages) or ("s3" in stages)
    if "s2" in stages:
        for nm, shp in (("Wk", (C, C)), ("Wv", (C, C)), ("Wr", (C, C)),
                        ("Wck", (C, CKDIM)), ("Wcv", (CKDIM, C)), ("Wcr", (C, C)),
                        ("ln1_g", (C,)), ("ln1_b", (C,)), ("ln2_g", (C,)), ("ln2_b", (C,)),
                        ("wkv_w", (128,)), ("wkv_u", (128,)), ("ts_pack", (TOK,))):
            io[nm] = nc.dram_tensor(nm, list(shp), F32, kind="ExternalInput").ap()
    if need_ffn0:
        for nm, shp in (("wg_a", (IFFN, C)), ("wu_a", (IFFN, C)), ("wd_a", (C, IFFN)),
                        ("wg_b", (C, IFFN)), ("wu_b", (C, IFFN)), ("wd_b", (IFFN, C)),
                        ("clip_x", (1,)), ("clip_down", (1,))):
            io[nm] = nc.dram_tensor(nm, list(shp), F32, kind="ExternalInput").ap()
    if ("s4" in stages) or ("s6" in stages):
        for nm, shp in (("s4_gatew", (C, EMOE)), ("s4_gateb", (EMOE,)),
                        ("s4_wg", (EMOE * C, IEXP)), ("s4_wu", (EMOE * C, IEXP)),
                        ("s4_wd", (EMOE * IEXP, C)), ("s4_swg", (C, C)),
                        ("s4_swu", (C, C)), ("s4_swd", (C, C))):
            io[nm] = nc.dram_tensor(nm, list(shp), F32, kind="ExternalInput").ap()
    if "s5" in stages:
        for nm, shp in (("s5_Wa", (C, C)), ("s5_Wx", (C, C)), ("s5_Wba", (C,)),
                        ("s5_Wbx", (C,)), ("s5_Lam", (C,))):
            io[nm] = nc.dram_tensor(nm, list(shp), F32, kind="ExternalInput").ap()

    with nc.Block() as block:
        ctx = Ctx(nc, block)
        ctx.dram.update(io)
        ctx.ident = ctx.sbuf("ident", (128, 128))
        s_io = ctx.sem("io")

        X = XTiles(ctx)
        ctx.sbuf("ABUF", (128, 16384))

        @block.sync
        def _(sync):
            s_io.inc(sync.dma_start(out=ctx.ident[:, :], in_=io["ident_in"][:, :]), 16)
            for c in range(CH):
                s_io.inc(sync.dma_start(out=X[c][:, :], in_=io["x_in"][128 * c:128 * (c + 1), :]), 16)

        @block.vector
        def _(vector):
            vector.wait_ge(s_io.h, s_io.val)
            vector.memset(ctx.sbuf("ones_col", (128, 1))[:, :], 1.0)

        if need_ffn0:
            clip_x = ctx.sbuf("clip_x_t", (1, 2))
            clip_d = ctx.sbuf("clip_d_t", (1, 2))

            @block.sync
            def _(sync):
                s_io.inc(sync.dma_start(out=clip_x[0:1, 0:1], in_=io["clip_x"][0:1]), 16)
                s_io.inc(sync.dma_start(out=clip_d[0:1, 0:1], in_=io["clip_down"][0:1]), 16)

        ctx.barrier()

        if need_ffn0:
            emit_ffn0_weight_quant(ctx)

        for st in stages:
            STAGE_EMITTERS[st](ctx, st)

        s_st = ctx.sem("st")

        @block.gpsimd
        def _(gp):
            for c in range(CH):
                s_st.inc(gp.dma_start(out=io["x_out"][128 * c:128 * (c + 1), :], in_=X[c][:, :]), 16)
            gp.wait_ge(s_st.h, s_st.val)

    return nc


def _core_inmap(inputs, core, x_full):
    sl = x_full[:, TL * core:TL * (core + 1), :]
    xfm = np.ascontiguousarray(sl.transpose(2, 0, 1).reshape(C, TOK))
    m = {
        "x_in": xfm,
        "ident_in": np.eye(128, dtype=np.float32),
        "wg_a": np.ascontiguousarray(inputs["L0_ffn_w_gate"]),
        "wu_a": np.ascontiguousarray(inputs["L0_ffn_w_up"]),
        "wd_a": np.ascontiguousarray(inputs["L0_ffn_w_down"]),
        "wg_b": np.ascontiguousarray(inputs["L0_ffn_w_gate"].T),
        "wu_b": np.ascontiguousarray(inputs["L0_ffn_w_up"].T),
        "wd_b": np.ascontiguousarray(inputs["L0_ffn_w_down"].T),
        "clip_x": np.ascontiguousarray(inputs["L0_ffn_clip_x"]),
        "clip_down": np.ascontiguousarray(inputs["L0_ffn_clip_down"]),
    }
    ts = inputs["L0_attn_time_shift"].reshape(C)
    ts_pack = np.zeros((128, 16), dtype=np.float32)
    for c in range(CH):
        ts_pack[:, 2 * c] = ts[128 * c:128 * (c + 1)]
        ts_pack[:, 2 * c + 1] = ts[128 * c:128 * (c + 1)]
    m.update({
        "Wk": np.ascontiguousarray(inputs["L0_attn_Wk"]),
        "Wv": np.ascontiguousarray(inputs["L0_attn_Wv"]),
        "Wr": np.ascontiguousarray(inputs["L0_attn_Wr"]),
        "Wck": np.ascontiguousarray(inputs["L0_attn_Wck"]),
        "Wcv": np.ascontiguousarray(inputs["L0_attn_Wcv"]),
        "Wcr": np.ascontiguousarray(inputs["L0_attn_Wcr"]),
        "ln1_g": np.ascontiguousarray(inputs["L0_attn_ln1_g"]),
        "ln1_b": np.ascontiguousarray(inputs["L0_attn_ln1_b"]),
        "ln2_g": np.ascontiguousarray(inputs["L0_attn_ln2_g"]),
        "ln2_b": np.ascontiguousarray(inputs["L0_attn_ln2_b"]),
        "wkv_w": np.ascontiguousarray(inputs["L0_attn_time_decay"][128 * core:128 * (core + 1)]),
        "wkv_u": np.ascontiguousarray(inputs["L0_attn_time_first"][128 * core:128 * (core + 1)]),
        "ts_pack": np.ascontiguousarray(ts_pack.reshape(TOK)),
    })
    m.update({
        "s4_gatew": np.ascontiguousarray(inputs["L1_ffn_gate_w"]),
        "s4_gateb": np.ascontiguousarray(inputs["L1_ffn_gate_bias"]),
        "s4_wg": np.ascontiguousarray(inputs["L1_ffn_w_gate"].reshape(EMOE * C, IEXP)),
        "s4_wu": np.ascontiguousarray(inputs["L1_ffn_w_up"].reshape(EMOE * C, IEXP)),
        "s4_wd": np.ascontiguousarray(inputs["L1_ffn_w_down"].reshape(EMOE * IEXP, C)),
        "s4_swg": np.ascontiguousarray(inputs["L1_ffn_sw_gate"]),
        "s4_swu": np.ascontiguousarray(inputs["L1_ffn_sw_up"]),
        "s4_swd": np.ascontiguousarray(inputs["L1_ffn_sw_down"]),
    })
    m.update({
        "s5_Wa": np.ascontiguousarray(inputs["L1_attn_Wa"]),
        "s5_Wx": np.ascontiguousarray(inputs["L1_attn_Wx"]),
        "s5_Wba": np.ascontiguousarray(inputs["L1_attn_ba"]),
        "s5_Wbx": np.ascontiguousarray(inputs["L1_attn_bx"]),
        "s5_Lam": np.ascontiguousarray(inputs["L1_attn_Lam"]),
    })
    return m


def kernel(**inputs):
    inputs = {k: np.asarray(v, dtype=np.float32) for k, v in inputs.items()}
    x = inputs["L0_attn_x"]

    for stages in SPLIT:
        nc = build_program(stages)
        in_maps = [_core_inmap(inputs, i, x) for i in range(NCORE)]
        res = run_bass_kernel_spmd(nc, in_maps, core_ids=list(range(NCORE)))
        parts = []
        for i in range(NCORE):
            xo = np.asarray(res.results[i]["x_out"])
            parts.append(xo.reshape(C, B, TL).transpose(1, 2, 0))
        x = np.concatenate(parts, axis=1)

    return np.ascontiguousarray(x)
